# version 04
# baseline (speedup 1.0000x reference)
import math
import numpy as np

import concourse.bass as bass
import concourse.mybir as mybir
from concourse.bass_utils import run_bass_kernel_spmd

F32 = mybir.dt.float32
F32R = mybir.dt.float32r
BF16 = mybir.dt.bfloat16
I32 = mybir.dt.int32
AF = mybir.ActivationFunctionType
ALU = mybir.AluOpType

NCORE = 8
B, L, C = 4, 4096, 1024
TPC = B * L // NCORE
PPC = L // NCORE
NCH = C // 128
NTT = TPC // 512
H, H_KV, D = 16, 4, 64
K_SEL = 256
H_I, D_I = 4, 64
ROPE_BASE = 10000.0
SM_SCALE = D ** -0.5
I_L1 = 3072
LN_EPS = 1e-5
NEG_BIG = -30000.0
TWO_PI = 2.0 * math.pi

ENGINES = ("sync", "tensor", "vector", "scalar", "gpsimd")

_prog_cache = {}


class Seq:

    def __init__(self, nc):
        self.nc = nc
        keys = list(ENGINES) + [e + "_dma" for e in ENGINES]
        self.sems = {e: nc.alloc_semaphore(f"sq_{e}") for e in keys}
        self.cnt = {e: 0 for e in keys}

    def step(self, blk, ename):
        waits = [(self.sems[e], self.cnt[e]) for e in self.cnt if self.cnt[e] > 0]

        def deco(f):
            seq = self

            def body(eng):
                for s, val in waits:
                    eng.wait_ge(s, val)
                st = _Step(seq, ename, eng)
                last = f(eng, st)
                if last is not None:
                    last.then_inc(seq.sems[ename], 1)
                    seq.cnt[ename] += 1
                else:
                    assert st.marked, "step emitted nothing trackable"

            getattr(blk, ename)(body)

        return deco


class _Step:
    def __init__(self, seq, ename, eng):
        self.seq = seq
        self.ename = ename
        self.eng = eng
        self.marked = False

    def dma(self, out, in_, **kw):
        key = self.ename + "_dma"
        inst = self.eng.dma_start(out=out, in_=in_, **kw)
        inst.then_inc(self.seq.sems[key], 16)
        self.seq.cnt[key] += 16
        self.marked = True
        return inst

    def chain(self, inst):
        inst.then_inc(self.seq.sems[self.ename], 1)
        self.seq.cnt[self.ename] += 1
        self.eng.wait_ge(self.seq.sems[self.ename], self.seq.cnt[self.ename])
        self.marked = True
        return inst


def make_rot_T(nc, g, rt, st):
    st.chain(g.memset(rt, 0.0))
    last = None
    for blk_ in (0, 64):
        g.affine_select(out=rt[:, blk_:blk_ + 32], in_=rt[:, blk_:blk_ + 32],
                        compare_op=ALU.not_equal, fill=-1.0,
                        base=-(blk_ + 32), pattern=[[-1, 32]], channel_multiplier=1)
        last = g.affine_select(out=rt[:, blk_ + 32:blk_ + 64], in_=rt[:, blk_ + 32:blk_ + 64],
                               compare_op=ALU.not_equal, fill=1.0,
                               base=-blk_, pattern=[[-1, 32]], channel_multiplier=1)
    return last


def make_ident(nc, g, ident, st):
    st.chain(g.memset(ident, 0.0))
    return g.affine_select(out=ident, in_=ident, compare_op=ALU.not_equal, fill=1.0,
                           base=0, pattern=[[-1, ident.shape[1]]], channel_multiplier=1)



def build_A():
    nc = bass.Bass("TRN2", target_bir_lowering=False)
    xT = nc.dram_tensor("xT", [C, TPC], F32, kind="ExternalInput").ap()
    win = {}
    for name, cols in [("wq", 1024), ("wk", 256), ("wvv", 256), ("wvg", 256),
                       ("wqi", 256), ("wki", 64), ("wwi", 4), ("wog", 1024)]:
        win[name] = (nc.dram_tensor(name, [C, cols], F32, kind="ExternalInput").ap(), cols)
    posoff = nc.dram_tensor("posoff", [128, 1], F32, kind="ExternalInput").ap()
    bvg = nc.dram_tensor("bvg", [1, 256], F32, kind="ExternalInput").ap()
    bog = nc.dram_tensor("bog", [1, 1024], F32, kind="ExternalInput").ap()

    kv_loc = nc.dram_tensor("kv_loc", [TPC, 576], BF16, kind="ExternalOutput").ap()
    qrT = nc.dram_tensor("qrT", [C, TPC], BF16, kind="ExternalOutput").ap()
    qiT = nc.dram_tensor("qiT", [256, TPC], BF16, kind="ExternalOutput").ap()
    g1T = nc.dram_tensor("g1T", [C, TPC], BF16, kind="ExternalOutput").ap()
    xnT = nc.dram_tensor("xnT", [C, TPC], BF16, kind="ExternalOutput").ap()
    wI_tok = nc.dram_tensor("wI_tok", [TPC, 4], F32, kind="ExternalOutput").ap()

    woff = {}
    off = 0
    for name in ("wq", "wk", "wvv", "wvg", "wqi", "wki", "wwi", "wog"):
        woff[name] = off
        off += win[name][1]
    WCOLS = off

    from contextlib import ExitStack
    with ExitStack() as _ctx:
        wsb = _ctx.enter_context(nc.sbuf_tensor("wsb", [128, NCH * WCOLS], BF16))
        xtt = _ctx.enter_context(nc.sbuf_tensor("xtt", [128, NCH * 512], F32))
        xntt = _ctx.enter_context(nc.sbuf_tensor("xntt", [128, NCH * 512], BF16))
        x2 = _ctx.enter_context(nc.sbuf_tensor("x2", [128, 512], F32))
        onesf = _ctx.enter_context(nc.sbuf_tensor("onesf", [128, 1], F32))
        identb = _ctx.enter_context(nc.sbuf_tensor("identb", [128, 128], BF16))
        srow = _ctx.enter_context(nc.sbuf_tensor("srow", [1, 3 * 512], F32))
        mb = _ctx.enter_context(nc.sbuf_tensor("mb", [128, 512], F32))
        rb = _ctx.enter_context(nc.sbuf_tensor("rb", [128, 512], F32))
        rotT = _ctx.enter_context(nc.sbuf_tensor("rotT", [128, 128], BF16))
        costab = _ctx.enter_context(nc.sbuf_tensor("costab", [128, 512], BF16))
        sintab = _ctx.enter_context(nc.sbuf_tensor("sintab", [128, 512], BF16))
        tsc = _ctx.enter_context(nc.sbuf_tensor("tsc", [128, 512], F32))
        tsc2 = _ctx.enter_context(nc.sbuf_tensor("tsc2", [128, 512], F32))
        tin = _ctx.enter_context(nc.sbuf_tensor("tin", [128, 512], I32))
        ifr = _ctx.enter_context(nc.sbuf_tensor("ifr", [128, 1], F32))
        pidf = _ctx.enter_context(nc.sbuf_tensor("pidf", [128, 1], F32))
        epsc = _ctx.enter_context(nc.sbuf_tensor("epsc", [1, 1], F32))
        onesr = _ctx.enter_context(nc.sbuf_tensor("onesr", [1, 128], F32))
        qsb = _ctx.enter_context(nc.sbuf_tensor("qsb", [128, 512], BF16))
        rosb = _ctx.enter_context(nc.sbuf_tensor("rosb", [128, 512], BF16))
        t1 = _ctx.enter_context(nc.sbuf_tensor("t1", [128, 512], F32))
        gsb = _ctx.enter_context(nc.sbuf_tensor("gsb", [128, 512], BF16))
        biasb = _ctx.enter_context(nc.sbuf_tensor("biasb", [128, 10], F32))
        wisb = _ctx.enter_context(nc.sbuf_tensor("wisb", [128, 4], F32))
        ktt = _ctx.enter_context(nc.sbuf_tensor("ktt", [128, 5 * 512], BF16))
        kvsb = _ctx.enter_context(nc.sbuf_tensor("kvsb", [128, 4 * 576], BF16))
        ps0 = _ctx.enter_context(nc.psum_tensor("ps0", [128, 512], F32))
        ps1 = _ctx.enter_context(nc.psum_tensor("ps1", [128, 512], F32))
        ps2 = _ctx.enter_context(nc.psum_tensor("ps2", [128, 512], F32))
        prow = _ctx.enter_context(nc.psum_tensor("prow", [1, 1024], F32))
        pqt = _ctx.enter_context(nc.psum_tensor("pqt", [128, 4], F32))
        ptr = _ctx.enter_context(nc.psum_tensor("ptr", [128, 640], BF16))
        blk = _ctx.enter_context(nc.Block())
        sq = Seq(nc)
        wv_ = wsb.ap().rearrange("p (k c) -> p k c", k=NCH)
        xv_ = xtt.ap().rearrange("p (c t) -> p c t", c=NCH)
        xnv = xntt.ap().rearrange("p (c t) -> p c t", c=NCH)
        ktv = ktt.ap().rearrange("p (c t) -> p c t", c=5)

        def W(name, k, c0, c1):
            return wv_[:, k, woff[name] + c0:woff[name] + c1]

        @sq.step(blk, "gpsimd")
        def _(eng, st):
            for name in woff:
                wap, cols = win[name]
                for k in range(NCH):
                    st.dma(wv_[:, k, woff[name]:woff[name] + cols],
                           wap[128 * k:128 * (k + 1), :])
            st.dma(pidf[:, :], posoff[:, :])
            st.dma(biasb[:, 0:1], bvg[0:1, 0:128].rearrange("o p -> p o"))
            st.dma(biasb[:, 1:2], bvg[0:1, 128:256].rearrange("o p -> p o"))
            for cc in range(8):
                st.dma(biasb[:, 2 + cc:3 + cc],
                       bog[0:1, 128 * cc:128 * (cc + 1)].rearrange("o p -> p o"))

        @sq.step(blk, "gpsimd")
        def _(eng, st):
            make_rot_T(nc, eng, rotT[:, :], st)
            make_ident(nc, eng, identb[:, :], st)
            eng.iota(tin[:, 0:1], pattern=[[0, 1]], base=0, channel_multiplier=1)
            return eng.iota(tin[:, 1:2], pattern=[[0, 1]], base=0, channel_multiplier=1)

        @sq.step(blk, "vector")
        def _(eng, st):
            eng.memset(onesf[:, :], 1.0)
            eng.memset(epsc[:, :], LN_EPS)
            eng.memset(onesr[:, :], 1.0)
            st.chain(eng.tensor_copy(out=t1[:, 0:1], in_=tin[:, 0:1]))
            return eng.tensor_copy(out=ifr[:, :], in_=t1[:, 0:1])

        for thr in (32.0, 64.0, 96.0):
            @sq.step(blk, "vector")
            def _(eng, st, thr=thr):
                st.chain(eng.tensor_scalar(out=tsc[:, 0:1], in0=t1[:, 0:1], scalar1=thr,
                                           scalar2=-32.0, op0=ALU.is_ge, op1=ALU.mult))
                return eng.tensor_tensor(out=ifr[:, :], in0=ifr[:, :],
                                         in1=tsc[:, 0:1], op=ALU.add)

        @sq.step(blk, "scalar")
        def _(eng, st):
            return eng.activation(out=ifr[:, :], in_=ifr[:, :], func=AF.Exp,
                                  scale=-math.log(ROPE_BASE) / 32.0)

        @sq.step(blk, "gpsimd")
        def _(eng, st):
            return eng.iota(tin[:, :], pattern=[[1, 512]], base=0, channel_multiplier=0)

        @sq.step(blk, "vector")
        def _(eng, st):
            return eng.tensor_copy(out=tsc[:, :], in_=tin[:, :])

        @sq.step(blk, "vector")
        def _(eng, st):
            return eng.tensor_scalar(out=tsc[:, :], in0=tsc[:, :],
                                     scalar1=pidf[:, 0:1], scalar2=None, op0=ALU.add)

        @sq.step(blk, "vector")
        def _(eng, st):
            st.chain(eng.tensor_scalar(out=tsc[:, :], in0=tsc[:, :], scalar1=ifr[:, 0:1],
                                       scalar2=None, op0=ALU.mult))
            return eng.tensor_scalar(out=tsc2[:, :], in0=tsc[:, :],
                                     scalar1=1.0 / TWO_PI, scalar2=None, op0=ALU.mult)

        @sq.step(blk, "vector")
        def _(eng, st):
            return eng.tensor_copy(out=tin[:, :], in_=tsc2[:, :])

        @sq.step(blk, "vector")
        def _(eng, st):
            return eng.tensor_copy(out=tsc[:, :], in_=tin[:, :])

        @sq.step(blk, "vector")
        def _(eng, st):
            st.chain(eng.tensor_tensor(out=tsc2[:, :], in0=tsc2[:, :], in1=tsc[:, :],
                                       op=ALU.subtract))
            st.chain(eng.tensor_scalar(out=t1[:, :], in0=tsc2[:, :], scalar1=0.5,
                                       scalar2=None, op0=ALU.is_gt))
            st.chain(eng.tensor_tensor(out=tsc2[:, :], in0=tsc2[:, :], in1=t1[:, :],
                                       op=ALU.subtract))
            st.chain(eng.tensor_scalar(out=t1[:, :], in0=tsc2[:, :], scalar1=-0.5,
                                       scalar2=None, op0=ALU.is_lt))
            st.chain(eng.tensor_tensor(out=tsc2[:, :], in0=tsc2[:, :], in1=t1[:, :],
                                       op=ALU.add))
            return eng.tensor_scalar(out=tsc[:, :], in0=tsc2[:, :],
                                     scalar1=TWO_PI, scalar2=None, op0=ALU.mult)

        @sq.step(blk, "scalar")
        def _(eng, st):
            return eng.activation(out=sintab[:, :], in_=tsc[:, :], func=AF.Sin)

        @sq.step(blk, "vector")
        def _(eng, st):
            st.chain(eng.tensor_scalar(out=tsc2[:, :], in0=tsc[:, :], scalar1=-1.0,
                                       scalar2=math.pi / 2.0, op0=ALU.mult, op1=ALU.add))
            st.chain(eng.tensor_scalar(out=tsc[:, :], in0=tsc2[:, :], scalar1=math.pi,
                                       scalar2=-TWO_PI, op0=ALU.is_gt, op1=ALU.mult))
            return eng.tensor_tensor(out=tsc2[:, :], in0=tsc2[:, :], in1=tsc[:, :],
                                     op=ALU.add)

        @sq.step(blk, "scalar")
        def _(eng, st):
            return eng.activation(out=costab[:, :], in_=tsc2[:, :], func=AF.Sin)

        for tt in range(NTT):
            @sq.step(blk, "sync")
            def _(eng, st, tt=tt):
                for c in range(NCH):
                    st.dma(xv_[:, c], xT[128 * c:128 * (c + 1), 512 * tt:512 * (tt + 1)])

            for c in range(NCH):
                @sq.step(blk, "vector")
                def _(eng, st, c=c):
                    return eng.tensor_tensor(out=x2[:, :], in0=xv_[:, c],
                                             in1=xv_[:, c], op=ALU.mult)

                @sq.step(blk, "tensor")
                def _(eng, st, c=c):
                    eng.matmul(prow[:, 0:512], onesf[:, :],
                               xv_[:, c],
                               start=(c == 0), stop=(c == NCH - 1))
                    return eng.matmul(prow[:, 512:1024], onesf[:, :],
                                      x2[:, :],
                                      start=(c == 0), stop=(c == NCH - 1))

            @sq.step(blk, "vector")
            def _(eng, st):
                eng.tensor_scalar(out=srow[:, 0:512], in0=prow[:, 0:512],
                                  scalar1=1.0 / C, scalar2=None, op0=ALU.mult)
                st.chain(eng.tensor_scalar(out=srow[:, 512:1024], in0=prow[:, 512:1024],
                                           scalar1=1.0 / C, scalar2=None, op0=ALU.mult))
                st.chain(eng.tensor_tensor(out=srow[:, 1024:1536], in0=srow[:, 0:512],
                                           in1=srow[:, 0:512], op=ALU.mult))
                return eng.tensor_tensor(out=srow[:, 512:1024], in0=srow[:, 512:1024],
                                         in1=srow[:, 1024:1536], op=ALU.subtract)

            @sq.step(blk, "scalar")
            def _(eng, st):
                return eng.activation(out=srow[:, 1024:1536], in_=srow[:, 512:1024],
                                      func=AF.Sqrt, bias=epsc[0:1, 0:1])

            @sq.step(blk, "vector")
            def _(eng, st):
                return eng.reciprocal(out=srow[:, 512:1024], in_=srow[:, 1024:1536])

            @sq.step(blk, "tensor")
            def _(eng, st):
                eng.matmul(ps0[:, :], onesr[:, :],
                           srow[:, 0:512], start=True, stop=True)
                return eng.matmul(ps1[:, :], onesr[:, :],
                                  srow[:, 512:1024], start=True, stop=True)

            @sq.step(blk, "vector")
            def _(eng, st):
                eng.tensor_copy(out=mb[:, :], in_=ps0[:, :])
                return eng.tensor_copy(out=rb[:, :], in_=ps1[:, :])

            for c in range(NCH):
                @sq.step(blk, "vector")
                def _(eng, st, c=c):
                    st.chain(eng.tensor_tensor(out=x2[:, :], in0=xv_[:, c], in1=mb[:, :],
                                               op=ALU.subtract))
                    return eng.tensor_tensor(out=xnv[:, c], in0=x2[:, :], in1=rb[:, :],
                                             op=ALU.mult)

            @sq.step(blk, "sync")
            def _(eng, st, tt=tt):
                for c in range(NCH):
                    st.dma(xnT[128 * c:128 * (c + 1), 512 * tt:512 * (tt + 1)],
                           xnv[:, c])

            def proj_step(name, oc, psb, c0=None, c1=None):
                c0 = 128 * oc if c0 is None else c0
                c1 = 128 * (oc + 1) if c1 is None else c1

                @sq.step(blk, "tensor")
                def _(eng, st):
                    last = None
                    for k in range(NCH):
                        last = eng.matmul(psb[0:c1 - c0, 0:512], W(name, k, c0, c1),
                                          xnv[:, k], start=(k == 0), stop=(k == NCH - 1))
                    return last

            def rope_steps(oc, dst_ap, dma_to=None, tt=tt):
                @sq.step(blk, "scalar")
                def _(eng, st):
                    return eng.activation(out=qsb[:, :], in_=ps0[:, :], func=AF.Copy)

                @sq.step(blk, "tensor")
                def _(eng, st):
                    return eng.matmul(ps1[:, :], rotT[:, :], qsb[:, :],
                                      start=True, stop=True)

                @sq.step(blk, "scalar")
                def _(eng, st):
                    return eng.activation(out=rosb[:, :], in_=ps1[:, :], func=AF.Copy)

                @sq.step(blk, "vector")
                def _(eng, st):
                    eng.tensor_tensor(out=t1[:, :], in0=qsb[:, :], in1=costab[:, :],
                                      op=ALU.mult)
                    st.chain(eng.tensor_tensor(out=x2[:, :], in0=rosb[:, :], in1=sintab[:, :],
                                               op=ALU.mult))
                    return eng.tensor_tensor(out=dst_ap, in0=t1[:, :], in1=x2[:, :],
                                             op=ALU.add)

                if dma_to is not None:
                    @sq.step(blk, "sync")
                    def _(eng, st):
                        st.dma(dma_to, dst_ap)

            for oc in range(NCH):
                proj_step("wq", oc, ps0)
                rope_steps(oc, qsb[:, :],
                           dma_to=qrT[128 * oc:128 * (oc + 1), 512 * tt:512 * (tt + 1)])

            for oc in range(2):
                proj_step("wk", oc, ps0)
                rope_steps(oc, ktv[:, oc])

            for oc in range(2):
                proj_step("wvv", oc, ps0)
                proj_step("wvg", oc, ps1)

                @sq.step(blk, "scalar")
                def _(eng, st, oc=oc):
                    return eng.activation(out=gsb[:, :], in_=ps1[:, :],
                                          func=AF.Sigmoid, bias=biasb[:, oc:oc + 1])

                @sq.step(blk, "vector")
                def _(eng, st, oc=oc):
                    return eng.tensor_tensor(out=ktv[:, 2 + oc], in0=ps0[:, :],
                                             in1=gsb[:, :], op=ALU.mult)

            proj_step("wki", 0, ps0, 0, 64)

            @sq.step(blk, "vector")
            def _(eng, st):
                st.chain(eng.memset(ktv[:, 4], 0.0))
                return eng.tensor_copy(out=ktv[:, 4][0:64, :], in_=ps0[0:64, 0:512])

            for oc in range(2):
                proj_step("wqi", oc, ps2)

                @sq.step(blk, "scalar")
                def _(eng, st):
                    return eng.activation(out=gsb[:, :], in_=ps2[:, :], func=AF.Copy)

                @sq.step(blk, "sync")
                def _(eng, st, oc=oc, tt=tt):
                    st.dma(qiT[128 * oc:128 * (oc + 1), 512 * tt:512 * (tt + 1)],
                           gsb[:, :])

            for oc in range(NCH):
                proj_step("wog", oc, ps2)

                @sq.step(blk, "scalar")
                def _(eng, st, oc=oc):
                    return eng.activation(out=gsb[:, :], in_=ps2[:, :],
                                          func=AF.Sigmoid, bias=biasb[:, 2 + oc:3 + oc])

                @sq.step(blk, "sync")
                def _(eng, st, oc=oc, tt=tt):
                    st.dma(g1T[128 * oc:128 * (oc + 1), 512 * tt:512 * (tt + 1)],
                           gsb[:, :])

            for qs in range(4):
                @sq.step(blk, "tensor")
                def _(eng, st, qs=qs):
                    last = None
                    for k in range(NCH):
                        last = eng.matmul(pqt[:, :], xnv[:, k][:, 128 * qs:128 * (qs + 1)],
                                          W("wwi", k, 0, 4), start=(k == 0), stop=(k == NCH - 1))
                    return last

                @sq.step(blk, "scalar")
                def _(eng, st):
                    return eng.activation(out=wisb[:, :], in_=pqt[:, :], func=AF.Sigmoid)

                @sq.step(blk, "sync")
                def _(eng, st, qs=qs, tt=tt):
                    st.dma(wI_tok[512 * tt + 128 * qs:512 * tt + 128 * (qs + 1), :],
                           wisb[:, :])

            for ts in range(4):
                @sq.step(blk, "tensor")
                def _(eng, st, ts=ts):
                    last = None
                    for fc in range(5):
                        last = eng.transpose(ptr[:, 128 * fc:128 * (fc + 1)],
                                             ktv[:, fc][:, 128 * ts:128 * (ts + 1)],
                                             identb[:, :])
                    return last

                @sq.step(blk, "vector")
                def _(eng, st, ts=ts):
                    for fc in range(4):
                        eng.tensor_copy(out=kvsb[:, 576 * ts + 128 * fc:576 * ts + 128 * (fc + 1)],
                                        in_=ptr[:, 128 * fc:128 * (fc + 1)])
                    return eng.tensor_copy(out=kvsb[:, 576 * ts + 512:576 * ts + 576],
                                           in_=ptr[:, 512:576])

            @sq.step(blk, "gpsimd")
            def _(eng, st, tt=tt):
                for ts in range(4):
                    st.dma(kv_loc[512 * tt + 128 * ts:512 * tt + 128 * (ts + 1), :],
                           kvsb[:, 576 * ts:576 * (ts + 1)])

        @sq.step(blk, "sync")
        def _(eng, st):
            return eng.nop()

    return nc





def build_B(n_batches=4, bisect_iters=34):
    nc = bass.Bass("TRN2", target_bir_lowering=False)
    qrT = nc.dram_tensor("qrT", [C, TPC], BF16, kind="ExternalInput").ap()
    qiT = nc.dram_tensor("qiT", [256, TPC], BF16, kind="ExternalInput").ap()
    g1T = nc.dram_tensor("g1T", [C, TPC], BF16, kind="ExternalInput").ap()
    xT = nc.dram_tensor("xT", [C, TPC], F32, kind="ExternalInput").ap()
    wI_tok = nc.dram_tensor("wI_tok", [TPC, 4], F32, kind="ExternalInput").ap()
    kv_all = nc.dram_tensor("kv_all", [B * L, 576], BF16, kind="ExternalInput").ap()
    bi_in = nc.dram_tensor("bi_in", [1, 4], F32, kind="ExternalInput").ap()
    wo = nc.dram_tensor("wo", [1024, 1024], F32, kind="ExternalInput").ap()
    xAT = nc.dram_tensor("xAT", [C, TPC], F32, kind="ExternalOutput").ap()

    from contextlib import ExitStack
    with ExitStack() as _ctx:
        wosb = _ctx.enter_context(nc.sbuf_tensor("wosb", [128, NCH * 1024], BF16))
        identb = _ctx.enter_context(nc.sbuf_tensor("identb", [128, 128], BF16))
        onesr = _ctx.enter_context(nc.sbuf_tensor("onesr", [1, 128], F32))
        bib = _ctx.enter_context(nc.sbuf_tensor("bib", [128, 4], F32))
        kt64 = _ctx.enter_context(nc.sbuf_tensor("kt64", [64, 4 * L], BF16))
        ki64 = _ctx.enter_context(nc.sbuf_tensor("ki64", [64, L], BF16))
        vtok = _ctx.enter_context(nc.sbuf_tensor("vtok", [128, 32 * 260], BF16))
        qr64 = _ctx.enter_context(nc.sbuf_tensor("qr64", [64, H * 512], BF16))
        qi64 = _ctx.enter_context(nc.sbuf_tensor("qi64", [64, 4 * 512], BF16))
        wib = _ctx.enter_context(nc.sbuf_tensor("wib", [128, 4 * 4], F32))
        ktok = _ctx.enter_context(nc.sbuf_tensor("ktok", [128, 384], BF16))
        mtsb = _ctx.enter_context(nc.sbuf_tensor("mtsb", [128, 32 * 512], BF16))
        isb = _ctx.enter_context(nc.sbuf_tensor("isb", [128, L], F32))
        scr = _ctx.enter_context(nc.sbuf_tensor("scr", [128, L], BF16))
        bst = _ctx.enter_context(nc.sbuf_tensor("bst", [128, 8], F32))
        sx0 = _ctx.enter_context(nc.sbuf_tensor("sx0", [128, 512], F32))
        sx1 = _ctx.enter_context(nc.sbuf_tensor("sx1", [128, 512], F32))
        sx2 = _ctx.enter_context(nc.sbuf_tensor("sx2", [128, 512], F32))
        sx3 = _ctx.enter_context(nc.sbuf_tensor("sx3", [128, 512], F32))
        pt0 = _ctx.enter_context(nc.sbuf_tensor("pt0", [128, 512], BF16))
        pt1 = _ctx.enter_context(nc.sbuf_tensor("pt1", [128, 512], BF16))
        pt2 = _ctx.enter_context(nc.sbuf_tensor("pt2", [128, 512], BF16))
        pt3 = _ctx.enter_context(nc.sbuf_tensor("pt3", [128, 512], BF16))
        drw = _ctx.enter_context(nc.sbuf_tensor("drw", [1, 512], F32))
        drb = _ctx.enter_context(nc.sbuf_tensor("drb", [64, 512], F32))
        att = _ctx.enter_context(nc.sbuf_tensor("att", [128, NCH * 512], BF16))
        g1b = _ctx.enter_context(nc.sbuf_tensor("g1b", [128, NCH * 512], BF16))
        xbu = _ctx.enter_context(nc.sbuf_tensor("xbu", [128, 512], F32))
        xao = _ctx.enter_context(nc.sbuf_tensor("xao", [128, 512], F32))
        psS0 = _ctx.enter_context(nc.psum_tensor("psS0", [128, 512], F32))
        psS1 = _ctx.enter_context(nc.psum_tensor("psS1", [128, 512], F32))
        psS2 = _ctx.enter_context(nc.psum_tensor("psS2", [128, 512], F32))
        psS3 = _ctx.enter_context(nc.psum_tensor("psS3", [128, 512], F32))
        psO = _ctx.enter_context(nc.psum_tensor("psO", [65, 512], F32))
        psB = _ctx.enter_context(nc.psum_tensor("psB", [64, 512], F32))
        ptr = _ctx.enter_context(nc.psum_tensor("ptr", [128, 512], BF16))
        psY = _ctx.enter_context(nc.psum_tensor("psY", [128, 512], F32))
        blk = _ctx.enter_context(nc.Block())

        sq = Seq(nc)
        psS = (psS0, psS1, psS2, psS3)
        sxs = (sx0, sx1, sx2, sx3)
        pts = (pt0, pt1, pt2, pt3)
        vtv = vtok.ap().rearrange("p (c t) -> p c t", c=32)
        mtv = mtsb.ap().rearrange("p (c t) -> p c t", c=32)
        attv = att.ap().rearrange("p (c t) -> p c t", c=NCH)
        g1v = g1b.ap().rearrange("p (c t) -> p c t", c=NCH)
        wov = wosb.ap().rearrange("p (c t) -> p c t", c=NCH)
        ktg = kt64.ap().rearrange("p (g t) -> p g t", g=4)
        qrh = qr64.ap().rearrange("p (h t) -> p h t", h=H)
        qih = qi64.ap().rearrange("p (h t) -> p h t", h=4)

        @sq.step(blk, "gpsimd")
        def _(eng, st):
            for k in range(NCH):
                st.dma(wov[:, k], wo[128 * k:128 * (k + 1), :])

        @sq.step(blk, "gpsimd")
        def _(eng, st):
            return make_ident(nc, eng, identb[:, :], st)

        @sq.step(blk, "vector")
        def _(eng, st):
            eng.memset(onesr[:, :], 1.0)
            return eng.memset(
                vtok.ap().rearrange("p (c g o) -> p c g o", c=32, g=4)[:, :, :, 64:65], 1.0)

        @sq.step(blk, "sync")
        def _(eng, st):
            st.dma(drw[0:1, 0:4], bi_in[0:1, :])

        @sq.step(blk, "tensor")
        def _(eng, st):
            return eng.matmul(psS0[:, 0:4], onesr[:, :],
                              drw[0:1, 0:4], start=True, stop=True)

        @sq.step(blk, "vector")
        def _(eng, st):
            return eng.tensor_copy(out=bib[:, :], in_=psS0[:, 0:4])

        for b in range(n_batches):
            @sq.step(blk, "sync")
            def _(eng, st, b=b):
                for h in range(H):
                    st.dma(qrh[0:64, h, :],
                           qrT[64 * h:64 * (h + 1), 512 * b:512 * (b + 1)])
                for h in range(4):
                    st.dma(qih[0:64, h, :],
                           qiT[64 * h:64 * (h + 1), 512 * b:512 * (b + 1)])
                for qt in range(4):
                    st.dma(wib[:, 4 * qt:4 * (qt + 1)],
                           wI_tok[512 * b + 128 * qt:512 * b + 128 * (qt + 1), :])
                for c in range(NCH):
                    st.dma(g1v[:, c], g1T[128 * c:128 * (c + 1), 512 * b:512 * (b + 1)])

            @sq.step(blk, "gpsimd")
            def _(eng, st, b=b):
                for j in range(32):
                    r0 = L * b + 128 * j
                    for g in range(4):
                        st.dma(vtv[:, j][:, 65 * g:65 * g + 64],
                               kv_all[r0:r0 + 128, 256 + 64 * g:256 + 64 * (g + 1)])

            for j in range(32):
                @sq.step(blk, "gpsimd")
                def _(eng, st, j=j, b=b):
                    r0 = L * b + 128 * j
                    st.dma(ktok[:, 0:256], kv_all[r0:r0 + 128, 0:256])
                    st.dma(ktok[:, 256:320], kv_all[r0:r0 + 128, 512:576])

                @sq.step(blk, "tensor")
                def _(eng, st, j=j):
                    last = None
                    for fc in range(3):
                        last = eng.transpose(ptr[:, 128 * fc:128 * (fc + 1)],
                                             ktok[:, 128 * fc:128 * (fc + 1)],
                                             identb[:, :])
                    return last

                @sq.step(blk, "vector")
                def _(eng, st, j=j):
                    sl = slice(128 * j, 128 * (j + 1))
                    eng.tensor_copy(out=ktg[0:64, 0, sl], in_=ptr[0:64, 0:128])
                    st.chain(eng.tensor_copy(out=ktg[0:64, 1, sl], in_=ptr[64:128, 0:128]))
                    eng.tensor_copy(out=ktg[0:64, 2, sl], in_=ptr[0:64, 128:256])
                    st.chain(eng.tensor_copy(out=ktg[0:64, 3, sl], in_=ptr[64:128, 128:256]))
                    return eng.tensor_copy(out=ki64[0:64, sl], in_=ptr[0:64, 256:384])

            for qt in range(4):
                for h in range(H_I):
                    for kc in range(NCH):
                        @sq.step(blk, "tensor")
                        def _(eng, st, h=h, kc=kc, qt=qt):
                            return eng.matmul(
                                psS[kc % 2][:, :],
                                qih[0:64, h, 128 * qt:128 * (qt + 1)],
                                ki64[0:64, 512 * kc:512 * (kc + 1)],
                                start=True, stop=True)

                        @sq.step(blk, "scalar")
                        def _(eng, st, h=h, kc=kc):
                            return eng.activation(
                                out=scr[:, 512 * kc:512 * (kc + 1)],
                                in_=psS[kc % 2][:, :], func=AF.Sigmoid,
                                bias=bib[:, h:h + 1])

                    @sq.step(blk, "vector")
                    def _(eng, st, h=h, qt=qt):
                        if h == 0:
                            return eng.tensor_scalar(
                                out=isb[:, :], in0=scr[:, :],
                                scalar1=wib[:, 4 * qt + h:4 * qt + h + 1],
                                scalar2=None, op0=ALU.mult)
                        return eng.scalar_tensor_tensor(
                            out=isb[:, :], in0=scr[:, :],
                            scalar=wib[:, 4 * qt + h:4 * qt + h + 1],
                            in1=isb[:, :], op0=ALU.mult, op1=ALU.add)

                @sq.step(blk, "vector")
                def _(eng, st, qt=qt):
                    st.chain(eng.memset(bst[:, 0:2], 0.0))
                    st.chain(eng.memset(bst[:, 1:2], 4.0))
                    for it in range(bisect_iters):
                        st.chain(eng.tensor_tensor(out=bst[:, 2:3], in0=bst[:, 0:1],
                                                   in1=bst[:, 1:2], op=ALU.add))
                        st.chain(eng.tensor_scalar(out=bst[:, 2:3], in0=bst[:, 2:3],
                                                   scalar1=0.5, scalar2=None,
                                                   op0=ALU.mult))
                        st.chain(eng.tensor_scalar(out=scr[:, :], in0=isb[:, :],
                                                   scalar1=bst[:, 2:3], scalar2=0.0,
                                                   op0=ALU.is_gt, op1=ALU.add,
                                                   accum_out=bst[:, 3:4]))
                        st.chain(eng.tensor_scalar(out=bst[:, 4:5], in0=bst[:, 3:4],
                                                   scalar1=float(K_SEL), scalar2=None,
                                                   op0=ALU.is_ge))
                        st.chain(eng.tensor_tensor(out=bst[:, 5:6], in0=bst[:, 2:3],
                                                   in1=bst[:, 0:1], op=ALU.subtract))
                        st.chain(eng.scalar_tensor_tensor(out=bst[:, 0:1], in0=bst[:, 5:6],
                                                          scalar=bst[:, 4:5], in1=bst[:, 0:1],
                                                          op0=ALU.mult, op1=ALU.add))
                        st.chain(eng.tensor_tensor(out=bst[:, 5:6], in0=bst[:, 1:2],
                                                   in1=bst[:, 2:3], op=ALU.subtract))
                        st.chain(eng.scalar_tensor_tensor(out=bst[:, 1:2], in0=bst[:, 5:6],
                                                          scalar=bst[:, 4:5], in1=bst[:, 2:3],
                                                          op0=ALU.mult, op1=ALU.add))
                    return eng.tensor_scalar(out=scr[:, :], in0=isb[:, :],
                                             scalar1=bst[:, 0:1], scalar2=NEG_BIG,
                                             op0=ALU.is_le, op1=ALU.mult)

                for jg in range(8):
                    @sq.step(blk, "tensor")
                    def _(eng, st, jg=jg):
                        last = None
                        for u in range(4):
                            kc = 4 * jg + u
                            last = eng.transpose(ptr[:, 128 * u:128 * (u + 1)],
                                                 scr[:, 128 * kc:128 * (kc + 1)],
                                                 identb[:, :])
                        return last

                    @sq.step(blk, "vector")
                    def _(eng, st, jg=jg, qt=qt):
                        last = None
                        for u in range(4):
                            kc = 4 * jg + u
                            last = eng.tensor_copy(
                                out=mtv[:, kc][:, 128 * qt:128 * (qt + 1)],
                                in_=ptr[:, 128 * u:128 * (u + 1)])
                        return last

            for h in range(H):
                g = h // 4
                for jg in range(8):
                    @sq.step(blk, "tensor")
                    def _(eng, st, h=h, g=g, jg=jg):
                        last = None
                        for u in range(4):
                            kc = 4 * jg + u
                            last = eng.matmul(psS[u][:, :],
                                              ktg[0:64, g, 128 * kc:128 * (kc + 1)],
                                              qrh[0:64, h, :],
                                              start=True, stop=True)
                        return last

                    @sq.step(blk, "vector")
                    def _(eng, st, jg=jg):
                        last = None
                        for u in range(4):
                            kc = 4 * jg + u
                            last = eng.scalar_tensor_tensor(
                                out=sxs[u][:, :], in0=psS[u][:, :],
                                scalar=SM_SCALE, in1=mtv[:, kc],
                                op0=ALU.mult, op1=ALU.add)
                        return last

                    @sq.step(blk, "scalar")
                    def _(eng, st, jg=jg):
                        last = None
                        for u in range(4):
                            last = eng.activation(out=pts[u][:, :], in_=sxs[u][:, :],
                                                  func=AF.Exp)
                        return last

                    @sq.step(blk, "tensor")
                    def _(eng, st, g=g, jg=jg):
                        last = None
                        for u in range(4):
                            kc = 4 * jg + u
                            last = eng.matmul(psO[0:65, :],
                                              vtv[:, kc][:, 65 * g:65 * (g + 1)],
                                              pts[u][:, :],
                                              start=(kc == 0), stop=(kc == 31))
                        return last

                @sq.step(blk, "vector")
                def _(eng, st):
                    return eng.reciprocal(out=drw[0:1, :], in_=psO[64:65, :])

                @sq.step(blk, "tensor")
                def _(eng, st):
                    return eng.matmul(psB[0:64, :], onesr[0:1, 0:64], drw[0:1, :],
                                      start=True, stop=True)

                @sq.step(blk, "vector")
                def _(eng, st):
                    return eng.tensor_copy(out=drb[0:64, :], in_=psB[0:64, :])

                @sq.step(blk, "vector")
                def _(eng, st, h=h):
                    dst = attv[:, h // 2][64 * (h % 2):64 * (h % 2) + 64, :]
                    return eng.tensor_tensor(out=dst, in0=psO[0:64, :],
                                             in1=drb[0:64, :], op=ALU.mult)

            for c in range(NCH):
                @sq.step(blk, "vector")
                def _(eng, st, c=c):
                    return eng.tensor_tensor(out=attv[:, c], in0=attv[:, c],
                                             in1=g1v[:, c], op=ALU.mult)

            for oc in range(NCH):
                @sq.step(blk, "tensor")
                def _(eng, st, oc=oc):
                    last = None
                    for k in range(NCH):
                        last = eng.matmul(psY[:, :],
                                          wov[:, k][:, 128 * oc:128 * (oc + 1)],
                                          attv[:, k], start=(k == 0), stop=(k == NCH - 1))
                    return last

                @sq.step(blk, "sync")
                def _(eng, st, oc=oc, b=b):
                    st.dma(xbu[:, :], xT[128 * oc:128 * (oc + 1), 512 * b:512 * (b + 1)])

                @sq.step(blk, "vector")
                def _(eng, st):
                    return eng.tensor_tensor(out=xao[:, :], in0=psY[:, :],
                                             in1=xbu[:, :], op=ALU.add)

                @sq.step(blk, "sync")
                def _(eng, st, oc=oc, b=b):
                    st.dma(xAT[128 * oc:128 * (oc + 1), 512 * b:512 * (b + 1)],
                           xao[:, :])

        @sq.step(blk, "sync")
        def _(eng, st):
            return eng.nop()

    return nc




def build_C(n_experts=16):
    nc = bass.Bass("TRN2", target_bir_lowering=False)
    xnT = nc.dram_tensor("xnT", [C, TPC], BF16, kind="ExternalInput").ap()
    xAT = nc.dram_tensor("xAT", [C, TPC], F32, kind="ExternalInput").ap()
    gwT = nc.dram_tensor("gwT", [C, 16], F32, kind="ExternalInput").ap()
    ebias = nc.dram_tensor("ebias", [1, 16], F32, kind="ExternalInput").ap()
    wgT = nc.dram_tensor("wgT", [16, C, 1024], F32, kind="ExternalInput").ap()
    wuT = nc.dram_tensor("wuT", [16, C, 1024], F32, kind="ExternalInput").ap()
    wdT = nc.dram_tensor("wdT", [16, 1024, C], F32, kind="ExternalInput").ap()
    sgT = nc.dram_tensor("sgT", [C, 1024], F32, kind="ExternalInput").ap()
    suT = nc.dram_tensor("suT", [C, 1024], F32, kind="ExternalInput").ap()
    sdT = nc.dram_tensor("sdT", [1024, C], F32, kind="ExternalInput").ap()
    x1T = nc.dram_tensor("x1T", [C, TPC], F32, kind="ExternalOutput").ap()
    xn1T = nc.dram_tensor("xn1T", [C, TPC], BF16, kind="ExternalOutput").ap()
    xn1F = nc.dram_tensor("xn1F", [C, TPC], F32, kind="ExternalOutput").ap()
    xmax = nc.dram_tensor("xmax", [1, 1], F32, kind="ExternalOutput").ap()

    from contextlib import ExitStack
    with ExitStack() as _ctx:
        xnsb = _ctx.enter_context(nc.sbuf_tensor("xnsb", [128, NCH * TPC], BF16))
        ysb = _ctx.enter_context(nc.sbuf_tensor("ysb", [128, NCH * TPC], F32))
        wge = _ctx.enter_context(nc.sbuf_tensor("wge", [128, NCH * 1024], BF16))
        wue = _ctx.enter_context(nc.sbuf_tensor("wue", [128, NCH * 1024], BF16))
        wde = _ctx.enter_context(nc.sbuf_tensor("wde", [128, NCH * 1024], BF16))
        gwsb = _ctx.enter_context(nc.sbuf_tensor("gwsb", [128, NCH * 16], BF16))
        ebb = _ctx.enter_context(nc.sbuf_tensor("ebb", [128, 16], F32))
        cmb = _ctx.enter_context(nc.sbuf_tensor("cmb", [128, 16 * 16], F32))
        ctr_ = _ctx.enter_context(nc.sbuf_tensor("ctr_", [16, TPC], F32))
        crow = _ctx.enter_context(nc.sbuf_tensor("crow", [1, TPC], F32))
        cbf = _ctx.enter_context(nc.sbuf_tensor("cbf", [128, 512], F32))
        hsl = _ctx.enter_context(nc.sbuf_tensor("hsl", [128, 512], BF16))
        hsb = _ctx.enter_context(nc.sbuf_tensor("hsb", [128, NCH * 512], BF16))
        r16 = _ctx.enter_context(nc.sbuf_tensor("r16", [128, 16], F32))
        r16b = _ctx.enter_context(nc.sbuf_tensor("r16b", [128, 16], F32))
        r16c = _ctx.enter_context(nc.sbuf_tensor("r16c", [128, 16], F32))
        r8 = _ctx.enter_context(nc.sbuf_tensor("r8", [128, 8], F32))
        rcol = _ctx.enter_context(nc.sbuf_tensor("rcol", [128, 2], F32))
        onesr = _ctx.enter_context(nc.sbuf_tensor("onesr", [1, 128], F32))
        onesf = _ctx.enter_context(nc.sbuf_tensor("onesf", [128, 1], F32))
        identf = _ctx.enter_context(nc.sbuf_tensor("identf", [128, 128], F32))
        epsc = _ctx.enter_context(nc.sbuf_tensor("epsc", [1, 1], F32))
        xbu = _ctx.enter_context(nc.sbuf_tensor("xbu", [128, 512], F32))
        x1b = _ctx.enter_context(nc.sbuf_tensor("x1b", [128, NCH * 512], F32))
        x2 = _ctx.enter_context(nc.sbuf_tensor("x2", [128, 512], F32))
        srow = _ctx.enter_context(nc.sbuf_tensor("srow", [1, 3 * 512], F32))
        mb = _ctx.enter_context(nc.sbuf_tensor("mb", [128, 512], F32))
        rb = _ctx.enter_context(nc.sbuf_tensor("rb", [128, 512], F32))
        mxc = _ctx.enter_context(nc.sbuf_tensor("mxc", [128, 2], F32))
        mrow = _ctx.enter_context(nc.sbuf_tensor("mrow", [1, 128], F32))
        psG = _ctx.enter_context(nc.psum_tensor("psG", [128, 512], F32))
        psU = _ctx.enter_context(nc.psum_tensor("psU", [128, 512], F32))
        psD = _ctx.enter_context(nc.psum_tensor("psD", [128, 512], F32))
        psR = _ctx.enter_context(nc.psum_tensor("psR", [128, 16], F32))
        psB = _ctx.enter_context(nc.psum_tensor("psB", [128, 512], F32))
        psT = _ctx.enter_context(nc.psum_tensor("psT", [128, 128], F32))
        prow = _ctx.enter_context(nc.psum_tensor("prow", [1, 1024], F32))
        blk = _ctx.enter_context(nc.Block())

        sq = Seq(nc)
        xnv = xnsb.ap().rearrange("p (c t) -> p c t", c=NCH)
        yv = ysb.ap().rearrange("p (c t) -> p c t", c=NCH)
        x1v = x1b.ap().rearrange("p (c t) -> p c t", c=NCH)
        wgv = wge.ap().rearrange("p (c t) -> p c t", c=NCH)
        wuv = wue.ap().rearrange("p (c t) -> p c t", c=NCH)
        wdv = wde.ap().rearrange("p (c t) -> p c t", c=NCH)
        gwv = gwsb.ap().rearrange("p (c t) -> p c t", c=NCH)
        hsv = hsb.ap().rearrange("p (c t) -> p c t", c=NCH)

        @sq.step(blk, "gpsimd")
        def _(eng, st):
            for c in range(NCH):
                st.dma(xnv[:, c], xnT[128 * c:128 * (c + 1), :])
                st.dma(gwv[:, c], gwT[128 * c:128 * (c + 1), :])
            st.dma(srow[0:1, 0:16], ebias[0:1, :])

        @sq.step(blk, "gpsimd")
        def _(eng, st):
            return make_ident(nc, eng, identf[:, :], st)

        @sq.step(blk, "vector")
        def _(eng, st):
            eng.memset(onesr[:, :], 1.0)
            eng.memset(onesf[:, :], 1.0)
            return eng.memset(epsc[:, :], LN_EPS)

        @sq.step(blk, "tensor")
        def _(eng, st):
            return eng.matmul(psR[:, :], onesr[:, :], srow[0:1, 0:16],
                              start=True, stop=True)

        @sq.step(blk, "vector")
        def _(eng, st):
            return eng.tensor_copy(out=ebb[:, :], in_=psR[:, :])

        for qt in range(16):
            @sq.step(blk, "tensor")
            def _(eng, st, qt=qt):
                last = None
                for k in range(NCH):
                    last = eng.matmul(psR[:, :],
                                      xnv[:, k][:, 128 * qt:128 * (qt + 1)],
                                      gwv[:, k], start=(k == 0), stop=(k == NCH - 1))
                return last

            @sq.step(blk, "scalar")
            def _(eng, st):
                return eng.activation(out=r16[:, :], in_=psR[:, :], func=AF.Sigmoid)

            @sq.step(blk, "vector")
            def _(eng, st, qt=qt):
                st.chain(eng.tensor_tensor(out=r16b[:, :], in0=r16[:, :],
                                           in1=ebb[:, :], op=ALU.add))
                st.chain(eng.max(out=r8[:, :], in_=r16b[:, :]))
                st.chain(eng.match_replace(out=r16c[:, :], in_to_replace=r8[:, :],
                                           in_values=r16b[:, :], imm_value=-1e30))
                st.chain(eng.tensor_scalar(out=r16c[:, :], in0=r16c[:, :],
                                           scalar1=-1e29, scalar2=None, op0=ALU.is_lt))
                st.chain(eng.tensor_tensor(out=r16c[:, :], in0=r16c[:, :],
                                           in1=r16[:, :], op=ALU.mult))
                st.chain(eng.reduce_sum(out=rcol[:, 0:1], in_=r16c[:, :],
                                        axis=mybir.AxisListType.X))
                st.chain(eng.reciprocal(out=rcol[:, 1:2], in_=rcol[:, 0:1]))
                return eng.tensor_scalar(out=cmb[:, 16 * qt:16 * (qt + 1)],
                                         in0=r16c[:, :], scalar1=rcol[:, 1:2],
                                         scalar2=2.826, op0=ALU.mult, op1=ALU.mult)

            @sq.step(blk, "tensor")
            def _(eng, st, qt=qt):
                return eng.transpose(psT[0:16, :].bitcast(F32),
                                     cmb[:, 16 * qt:16 * (qt + 1)], identf[:, :])

            @sq.step(blk, "vector")
            def _(eng, st, qt=qt):
                return eng.tensor_copy(out=ctr_[0:16, 128 * qt:128 * (qt + 1)],
                                       in_=psT[0:16, 0:128].bitcast(F32))

        for e in range(n_experts + 1):
            shared = (e == n_experts)

            @sq.step(blk, "gpsimd")
            def _(eng, st, e=e, shared=shared):
                for c in range(NCH):
                    if shared:
                        st.dma(wgv[:, c], sgT[128 * c:128 * (c + 1), :])
                        st.dma(wuv[:, c], suT[128 * c:128 * (c + 1), :])
                        st.dma(wdv[:, c], sdT[128 * c:128 * (c + 1), :])
                    else:
                        st.dma(wgv[:, c], wgT[e, 128 * c:128 * (c + 1), :])
                        st.dma(wuv[:, c], wuT[e, 128 * c:128 * (c + 1), :])
                        st.dma(wdv[:, c], wdT[e, 128 * c:128 * (c + 1), :])

            if not shared:
                @sq.step(blk, "sync")
                def _(eng, st, e=e):
                    st.dma(crow[0:1, :], ctr_[e:e + 1, :])

            for tt in range(NTT):
                if not shared:
                    @sq.step(blk, "tensor")
                    def _(eng, st, e=e, tt=tt):
                        return eng.matmul(psB[:, :], onesr[:, :],
                                          crow[0:1, 512 * tt:512 * (tt + 1)],
                                          start=True, stop=True)

                    @sq.step(blk, "vector")
                    def _(eng, st):
                        return eng.tensor_copy(out=cbf[:, :], in_=psB[:, :])

                for ic in range(NCH):
                    @sq.step(blk, "tensor")
                    def _(eng, st, ic=ic, tt=tt):
                        for k in range(NCH):
                            eng.matmul(psG[:, :],
                                       wgv[:, k][:, 128 * ic:128 * (ic + 1)],
                                       xnv[:, k][:, 512 * tt:512 * (tt + 1)],
                                       start=(k == 0), stop=(k == NCH - 1))
                        last = None
                        for k in range(NCH):
                            last = eng.matmul(psU[:, :],
                                              wuv[:, k][:, 128 * ic:128 * (ic + 1)],
                                              xnv[:, k][:, 512 * tt:512 * (tt + 1)],
                                              start=(k == 0), stop=(k == NCH - 1))
                        return last

                    @sq.step(blk, "scalar")
                    def _(eng, st):
                        return eng.activation(out=hsl[:, :], in_=psG[:, :], func=AF.Silu)

                    @sq.step(blk, "vector")
                    def _(eng, st, ic=ic, shared=shared):
                        if shared:
                            return eng.tensor_tensor(out=hsv[:, ic], in0=hsl[:, :],
                                                     in1=psU[:, :], op=ALU.mult)
                        st.chain(eng.tensor_tensor(out=hsv[:, ic], in0=hsl[:, :],
                                                   in1=psU[:, :], op=ALU.mult))
                        return eng.tensor_tensor(out=hsv[:, ic], in0=hsv[:, ic],
                                                 in1=cbf[:, :], op=ALU.mult)

                for dc in range(NCH):
                    @sq.step(blk, "tensor")
                    def _(eng, st, dc=dc, tt=tt):
                        last = None
                        for k in range(NCH):
                            last = eng.matmul(psD[:, :],
                                              wdv[:, k][:, 128 * dc:128 * (dc + 1)],
                                              hsv[:, k], start=(k == 0),
                                              stop=(k == NCH - 1))
                        return last

                    @sq.step(blk, "vector")
                    def _(eng, st, e=e, dc=dc, tt=tt):
                        dst = yv[:, dc][:, 512 * tt:512 * (tt + 1)]
                        if e == 0:
                            return eng.tensor_copy(out=dst, in_=psD[:, :])
                        return eng.tensor_tensor(out=dst, in0=dst, in1=psD[:, :],
                                                 op=ALU.add)

        @sq.step(blk, "vector")
        def _(eng, st):
            return eng.memset(mxc[:, 0:1], 0.0)

        for tt in range(NTT):
            @sq.step(blk, "sync")
            def _(eng, st, tt=tt):
                for c in range(NCH):
                    st.dma(x1v[:, c],
                           xAT[128 * c:128 * (c + 1), 512 * tt:512 * (tt + 1)])

            for c in range(NCH):
                @sq.step(blk, "vector")
                def _(eng, st, c=c, tt=tt):
                    return eng.tensor_tensor(out=x1v[:, c], in0=x1v[:, c],
                                             in1=yv[:, c][:, 512 * tt:512 * (tt + 1)],
                                             op=ALU.add)

            @sq.step(blk, "sync")
            def _(eng, st, tt=tt):
                for c in range(NCH):
                    st.dma(x1T[128 * c:128 * (c + 1), 512 * tt:512 * (tt + 1)],
                           x1v[:, c])

            for c in range(NCH):
                @sq.step(blk, "vector")
                def _(eng, st, c=c):
                    return eng.tensor_tensor(out=x2[:, :], in0=x1v[:, c],
                                             in1=x1v[:, c], op=ALU.mult)

                @sq.step(blk, "tensor")
                def _(eng, st, c=c):
                    eng.matmul(prow[:, 0:512], onesf[:, :], x1v[:, c],
                               start=(c == 0), stop=(c == NCH - 1))
                    return eng.matmul(prow[:, 512:1024], onesf[:, :], x2[:, :],
                                      start=(c == 0), stop=(c == NCH - 1))

            @sq.step(blk, "vector")
            def _(eng, st):
                eng.tensor_scalar(out=srow[:, 0:512], in0=prow[:, 0:512],
                                  scalar1=1.0 / C, scalar2=None, op0=ALU.mult)
                st.chain(eng.tensor_scalar(out=srow[:, 512:1024], in0=prow[:, 512:1024],
                                           scalar1=1.0 / C, scalar2=None, op0=ALU.mult))
                st.chain(eng.tensor_tensor(out=srow[:, 1024:1536], in0=srow[:, 0:512],
                                           in1=srow[:, 0:512], op=ALU.mult))
                return eng.tensor_tensor(out=srow[:, 512:1024], in0=srow[:, 512:1024],
                                         in1=srow[:, 1024:1536], op=ALU.subtract)

            @sq.step(blk, "scalar")
            def _(eng, st):
                return eng.activation(out=srow[:, 1024:1536], in_=srow[:, 512:1024],
                                      func=AF.Sqrt, bias=epsc[0:1, 0:1])

            @sq.step(blk, "vector")
            def _(eng, st):
                return eng.reciprocal(out=srow[:, 512:1024], in_=srow[:, 1024:1536])

            @sq.step(blk, "tensor")
            def _(eng, st):
                eng.matmul(psG[:, :], onesr[:, :], srow[:, 0:512],
                           start=True, stop=True)
                return eng.matmul(psU[:, :], onesr[:, :], srow[:, 512:1024],
                                  start=True, stop=True)

            @sq.step(blk, "vector")
            def _(eng, st):
                eng.tensor_copy(out=mb[:, :], in_=psG[:, :])
                return eng.tensor_copy(out=rb[:, :], in_=psU[:, :])

            for c in range(NCH):
                @sq.step(blk, "vector")
                def _(eng, st, c=c):
                    st.chain(eng.tensor_tensor(out=x2[:, :], in0=x1v[:, c],
                                               in1=mb[:, :], op=ALU.subtract))
                    st.chain(eng.tensor_tensor(out=x2[:, :], in0=x2[:, :],
                                               in1=rb[:, :], op=ALU.mult))
                    st.chain(eng.tensor_copy(out=hsl[:, :], in_=x2[:, :]))
                    st.chain(eng.reduce_max(out=mxc[:, 1:2], in_=x2[:, :],
                                            axis=mybir.AxisListType.X,
                                            apply_absolute_value=True))
                    return eng.tensor_tensor(out=mxc[:, 0:1], in0=mxc[:, 0:1],
                                             in1=mxc[:, 1:2], op=ALU.max)

                @sq.step(blk, "sync")
                def _(eng, st, c=c, tt=tt):
                    st.dma(xn1T[128 * c:128 * (c + 1), 512 * tt:512 * (tt + 1)],
                           hsl[:, :])
                    st.dma(xn1F[128 * c:128 * (c + 1), 512 * tt:512 * (tt + 1)],
                           x2[:, :])

        @sq.step(blk, "tensor")
        def _(eng, st):
            return eng.transpose(psT[0:1, 0:128].bitcast(F32), mxc[:, 0:1],
                                 identf[:, :])

        @sq.step(blk, "vector")
        def _(eng, st):
            st.chain(eng.tensor_copy(out=mrow[0:1, :], in_=psT[0:1, 0:128].bitcast(F32)))
            return eng.reduce_max(out=srow[0:1, 0:1], in_=mrow[0:1, :],
                                  axis=mybir.AxisListType.X)

        @sq.step(blk, "sync")
        def _(eng, st):
            st.dma(xmax[0:1, 0:1], srow[0:1, 0:1])

        @sq.step(blk, "sync")
        def _(eng, st):
            return eng.nop()

    return nc




def build_D():
    nc = bass.Bass("TRN2", target_bir_lowering=False)
    xn1T = nc.dram_tensor("xn1T", [C, TPC], BF16, kind="ExternalInput").ap()
    x1T = nc.dram_tensor("x1T", [C, TPC], F32, kind="ExternalInput").ap()
    wkv = nc.dram_tensor("wkv", [C, 2048], F32, kind="ExternalInput").ap()
    bkv = nc.dram_tensor("bkv", [1, 2048], F32, kind="ExternalInput").ap()
    wq1 = nc.dram_tensor("wq1", [C, 1024], F32, kind="ExternalInput").ap()
    bq1 = nc.dram_tensor("bq1", [1, 1024], F32, kind="ExternalInput").ap()
    wp1 = nc.dram_tensor("wp1", [C, 1024], F32, kind="ExternalInput").ap()
    bp1 = nc.dram_tensor("bp1", [1, 1024], F32, kind="ExternalInput").ap()
    xBT = nc.dram_tensor("xBT", [C, TPC], F32, kind="ExternalOutput").ap()

    from contextlib import ExitStack
    with ExitStack() as _ctx:
        xnsb = _ctx.enter_context(nc.sbuf_tensor("xnsb", [128, NCH * TPC], BF16))
        wkvb = _ctx.enter_context(nc.sbuf_tensor("wkvb", [128, NCH * 2048], BF16))
        wqb = _ctx.enter_context(nc.sbuf_tensor("wqb", [128, NCH * 1024], BF16))
        wpb = _ctx.enter_context(nc.sbuf_tensor("wpb", [128, NCH * 1024], BF16))
        bkvb = _ctx.enter_context(nc.sbuf_tensor("bkvb", [128, 2048], F32))
        bqb = _ctx.enter_context(nc.sbuf_tensor("bqb", [128, 1024], F32))
        bpc = _ctx.enter_context(nc.sbuf_tensor("bpc", [128, NCH], F32))
        brow = _ctx.enter_context(nc.sbuf_tensor("brow", [1, 2048], F32))
        onesr = _ctx.enter_context(nc.sbuf_tensor("onesr", [1, 128], F32))
        identb = _ctx.enter_context(nc.sbuf_tensor("identb", [128, 128], BF16))
        qtok = _ctx.enter_context(nc.sbuf_tensor("qtok", [128, 4 * 1024], BF16))
        ktok = _ctx.enter_context(nc.sbuf_tensor("ktok2", [128, 4 * 1024], BF16))
        vtok = _ctx.enter_context(nc.sbuf_tensor("vtok2", [128, 4 * 1024], BF16))
        atile = _ctx.enter_context(nc.sbuf_tensor("atile", [128, 4 * 16 * 4], F32))
        amx = _ctx.enter_context(nc.sbuf_tensor("amx", [128, 4 * 16], F32))
        tmp1 = _ctx.enter_context(nc.sbuf_tensor("tmp1", [128, 1024], F32))
        avt = _ctx.enter_context(nc.sbuf_tensor("avt", [128, 4 * 1024], BF16))
        avT = _ctx.enter_context(nc.sbuf_tensor("avT", [128, NCH * TPC], BF16))
        xbu = _ctx.enter_context(nc.sbuf_tensor("xbu", [128, 512], F32))
        xao = _ctx.enter_context(nc.sbuf_tensor("xao", [128, 512], F32))
        psA = _ctx.enter_context(nc.psum_tensor("psA", [128, 512], F32))
        psB2 = _ctx.enter_context(nc.psum_tensor("psB2", [128, 512], F32))
        psC2 = _ctx.enter_context(nc.psum_tensor("psC2", [128, 512], F32))
        psD2 = _ctx.enter_context(nc.psum_tensor("psD2", [128, 512], F32))
        ptr = _ctx.enter_context(nc.psum_tensor("ptr", [128, 512], BF16))
        blk = _ctx.enter_context(nc.Block())

        sq = Seq(nc)
        xnv = xnsb.ap().rearrange("p (c t) -> p c t", c=NCH)
        wkvv = wkvb.ap().rearrange("p (c t) -> p c t", c=NCH)
        wqv = wqb.ap().rearrange("p (c t) -> p c t", c=NCH)
        wpv = wpb.ap().rearrange("p (c t) -> p c t", c=NCH)
        qtv = qtok.ap().rearrange("p (b f) -> p b f", b=4)
        ktv = ktok.ap().rearrange("p (b f) -> p b f", b=4)
        vtv = vtok.ap().rearrange("p (b f) -> p b f", b=4)
        avv = avt.ap().rearrange("p (b f) -> p b f", b=4)
        avTv = avT.ap().rearrange("p (c t) -> p c t", c=NCH)
        av_ = atile.ap().rearrange("p (s h t) -> p s h t", s=4, h=16)
        amv = amx.ap().rearrange("p (s h) -> p s h", s=4)

        @sq.step(blk, "gpsimd")
        def _(eng, st):
            for c in range(NCH):
                st.dma(xnv[:, c], xn1T[128 * c:128 * (c + 1), :])
                st.dma(wkvv[:, c], wkv[128 * c:128 * (c + 1), :])
                st.dma(wqv[:, c], wq1[128 * c:128 * (c + 1), :])
                st.dma(wpv[:, c], wp1[128 * c:128 * (c + 1), :])
            for c in range(NCH):
                st.dma(bpc[:, c:c + 1], bp1[0:1, 128 * c:128 * (c + 1)].rearrange("o p -> p o"))

        @sq.step(blk, "gpsimd")
        def _(eng, st):
            return make_ident(nc, eng, identb[:, :], st)

        @sq.step(blk, "sync")
        def _(eng, st):
            st.dma(brow[0:1, :], bkv[0:1, :])

        @sq.step(blk, "vector")
        def _(eng, st):
            return eng.memset(onesr[:, :], 1.0)

        for i in range(4):
            @sq.step(blk, "tensor")
            def _(eng, st, i=i):
                return eng.matmul(psA[:, :], onesr[:, :],
                                  brow[0:1, 512 * i:512 * (i + 1)],
                                  start=True, stop=True)

            @sq.step(blk, "vector")
            def _(eng, st, i=i):
                return eng.tensor_copy(out=bkvb[:, 512 * i:512 * (i + 1)], in_=psA[:, :])

        @sq.step(blk, "sync")
        def _(eng, st):
            st.dma(brow[0:1, 0:1024], bq1[0:1, :])

        for i in range(2):
            @sq.step(blk, "tensor")
            def _(eng, st, i=i):
                return eng.matmul(psA[:, :], onesr[:, :],
                                  brow[0:1, 512 * i:512 * (i + 1)],
                                  start=True, stop=True)

            @sq.step(blk, "vector")
            def _(eng, st, i=i):
                return eng.tensor_copy(out=bqb[:, 512 * i:512 * (i + 1)], in_=psA[:, :])

        for jb in range(4):
            for b in range(4):
                tsl = slice(512 * b + 128 * jb, 512 * b + 128 * (jb + 1))
                for fg in range(4):
                    @sq.step(blk, "tensor")
                    def _(eng, st, fg=fg, tsl=tsl):
                        last = None
                        for k in range(NCH):
                            last = eng.matmul(psA[:, :],
                                              xnv[:, k][:, tsl],
                                              wkvv[:, k][:, 512 * fg:512 * (fg + 1)],
                                              start=(k == 0), stop=(k == NCH - 1))
                        return last

                    @sq.step(blk, "vector")
                    def _(eng, st, fg=fg, b=b):
                        dst = ktv[:, b][:, 512 * fg:512 * (fg + 1)] if fg < 2 else \
                            vtv[:, b][:, 512 * (fg - 2):512 * (fg - 1)]
                        return eng.tensor_tensor(out=dst, in0=psA[:, :],
                                                 in1=bkvb[:, 512 * fg:512 * (fg + 1)],
                                                 op=ALU.add)

                for fg in range(2):
                    @sq.step(blk, "tensor")
                    def _(eng, st, fg=fg, tsl=tsl):
                        last = None
                        for k in range(NCH):
                            last = eng.matmul(psA[:, :],
                                              xnv[:, k][:, tsl],
                                              wqv[:, k][:, 512 * fg:512 * (fg + 1)],
                                              start=(k == 0), stop=(k == NCH - 1))
                        return last

                    @sq.step(blk, "vector")
                    def _(eng, st, fg=fg, b=b):
                        return eng.tensor_tensor(out=qtv[:, b][:, 512 * fg:512 * (fg + 1)],
                                                 in0=psA[:, :],
                                                 in1=bqb[:, 512 * fg:512 * (fg + 1)],
                                                 op=ALU.add)

            for s in range(4):
                for t in range(4):
                    @sq.step(blk, "vector")
                    def _(eng, st, s=s, t=t):
                        st.chain(eng.tensor_tensor(out=tmp1[:, :], in0=qtv[:, s],
                                                   in1=ktv[:, t], op=ALU.mult))
                        return eng.reduce_sum(
                            out=av_[:, s, :, t].unsqueeze(-1),
                            in_=tmp1.ap().rearrange("p (h d) -> p h d", h=16),
                            axis=mybir.AxisListType.X)

            @sq.step(blk, "vector")
            def _(eng, st):
                return eng.reduce_max(out=amx[:, :], in_=atile[:, :].rearrange(
                    "p (s h t) -> p (s h) t", s=4, h=16), axis=mybir.AxisListType.X)

            @sq.step(blk, "vector")
            def _(eng, st):
                return eng.tensor_tensor(
                    out=atile[:, :].rearrange("p (s h t) -> p (s h) t", s=4, h=16),
                    in0=atile[:, :].rearrange("p (s h t) -> p (s h) t", s=4, h=16),
                    in1=amx[:, :].unsqueeze(-1).to_broadcast([128, 64, 4]),
                    op=ALU.subtract)

            @sq.step(blk, "scalar")
            def _(eng, st):
                return eng.activation(out=atile[:, :], in_=atile[:, :], func=AF.Exp,
                                      scale=1.0 / 8.0)

            @sq.step(blk, "vector")
            def _(eng, st):
                return eng.reduce_sum(out=amx[:, :], in_=atile[:, :].rearrange(
                    "p (s h t) -> p (s h) t", s=4, h=16), axis=mybir.AxisListType.X)

            @sq.step(blk, "vector")
            def _(eng, st):
                st.chain(eng.reciprocal(out=amx[:, :], in_=amx[:, :]))
                return eng.tensor_tensor(
                    out=atile[:, :].rearrange("p (s h t) -> p (s h) t", s=4, h=16),
                    in0=atile[:, :].rearrange("p (s h t) -> p (s h) t", s=4, h=16),
                    in1=amx[:, :].unsqueeze(-1).to_broadcast([128, 64, 4]),
                    op=ALU.mult)

            for s in range(4):
                for t in range(4):
                    @sq.step(blk, "vector")
                    def _(eng, st, s=s, t=t):
                        pview = av_[:, s, :, t].unsqueeze(-1).to_broadcast([128, 16, 64])
                        if t == 0:
                            return eng.tensor_tensor(
                                out=avv[:, s].rearrange("p (h d) -> p h d", h=16),
                                in0=vtv[:, t].rearrange("p (h d) -> p h d", h=16),
                                in1=pview, op=ALU.mult)
                        st.chain(eng.tensor_tensor(
                            out=tmp1.ap().rearrange("p (h d) -> p h d", h=16),
                            in0=vtv[:, t].rearrange("p (h d) -> p h d", h=16),
                            in1=pview, op=ALU.mult))
                        return eng.tensor_tensor(out=avv[:, s], in0=avv[:, s],
                                                 in1=tmp1[:, :], op=ALU.add)

            for s in range(4):
                for fg in range(2):
                    @sq.step(blk, "tensor")
                    def _(eng, st, s=s, fg=fg):
                        last = None
                        for u in range(4):
                            fc = 4 * fg + u
                            last = eng.transpose(ptr[:, 128 * u:128 * (u + 1)],
                                                 avv[:, s][:, 128 * fc:128 * (fc + 1)],
                                                 identb[:, :])
                        return last

                    @sq.step(blk, "vector")
                    def _(eng, st, s=s, fg=fg, jb=jb):
                        last = None
                        for u in range(4):
                            fc = 4 * fg + u
                            last = eng.tensor_copy(
                                out=avTv[:, fc][:, 512 * s + 128 * jb:512 * s + 128 * (jb + 1)],
                                in_=ptr[:, 128 * u:128 * (u + 1)])
                        return last

        for oc in range(NCH):
            for tt in range(NTT):
                @sq.step(blk, "tensor")
                def _(eng, st, oc=oc, tt=tt):
                    last = None
                    for k in range(NCH):
                        last = eng.matmul(psD2[:, :],
                                          wpv[:, k][:, 128 * oc:128 * (oc + 1)],
                                          avTv[:, k][:, 512 * tt:512 * (tt + 1)],
                                          start=(k == 0), stop=(k == NCH - 1))
                    return last

                @sq.step(blk, "sync")
                def _(eng, st, oc=oc, tt=tt):
                    st.dma(xbu[:, :], x1T[128 * oc:128 * (oc + 1), 512 * tt:512 * (tt + 1)])

                @sq.step(blk, "vector")
                def _(eng, st, oc=oc):
                    st.chain(eng.tensor_scalar(out=xao[:, :], in0=psD2[:, :],
                                               scalar1=bpc[:, oc:oc + 1], scalar2=None,
                                               op0=ALU.add))
                    return eng.tensor_tensor(out=xao[:, :], in0=xao[:, :],
                                             in1=xbu[:, :], op=ALU.add)

                @sq.step(blk, "sync")
                def _(eng, st, oc=oc, tt=tt):
                    st.dma(xBT[128 * oc:128 * (oc + 1), 512 * tt:512 * (tt + 1)],
                           xao[:, :])

        @sq.step(blk, "sync")
        def _(eng, st):
            return eng.nop()

    return nc




def emit_round_inplace(eng, st, r, ti, tf, td):
    st.chain(eng.tensor_copy(out=ti, in_=r))
    st.chain(eng.tensor_copy(out=tf, in_=ti))
    st.chain(eng.tensor_tensor(out=td, in0=r, in1=tf, op=ALU.subtract))
    st.chain(eng.tensor_scalar(out=r, in0=td, scalar1=0.5, scalar2=None, op0=ALU.is_gt))
    st.chain(eng.tensor_tensor(out=tf, in0=tf, in1=r, op=ALU.add))
    st.chain(eng.tensor_scalar(out=r, in0=td, scalar1=-0.5, scalar2=None, op0=ALU.is_lt))
    return eng.tensor_tensor(out=r, in0=tf, in1=r, op=ALU.subtract)


def emit_qdq(eng, st, out_ap, x_ap, inv_col, s_col, r, ti, tf, td):
    st.chain(eng.tensor_scalar(out=r, in0=x_ap, scalar1=inv_col, scalar2=None,
                               op0=ALU.mult))
    st.chain(emit_round_inplace(eng, st, r, ti, tf, td))
    return eng.tensor_scalar(out=out_ap, in0=r, scalar1=s_col, scalar2=None,
                             op0=ALU.mult)


def emit_scale_from_max(sq, blk, nc, onesr, m_in_row, n_in, qmax, srow, scol, icol, ps):
    @sq.step(blk, "vector")
    def _(eng, st):
        st.chain(eng.reduce_max(out=srow[0:1, 0:1], in_=m_in_row[0:1, 0:n_in],
                                axis=mybir.AxisListType.X))
        st.chain(eng.tensor_scalar(out=srow[0:1, 1:2], in0=srow[0:1, 0:1],
                                   scalar1=1.0 / qmax, scalar2=1e-12,
                                   op0=ALU.mult, op1=ALU.max))
        return eng.reciprocal(out=srow[0:1, 2:3], in_=srow[0:1, 1:2])

    @sq.step(blk, "tensor")
    def _(eng, st):
        eng.matmul(ps[:, 0:1], onesr[:, :], srow[0:1, 1:2], start=True, stop=True)
        return eng.matmul(ps[:, 1:2], onesr[:, :], srow[0:1, 2:3], start=True, stop=True)

    @sq.step(blk, "vector")
    def _(eng, st):
        st.chain(eng.tensor_copy(out=scol, in_=ps[:, 0:1]))
        return eng.tensor_copy(out=icol, in_=ps[:, 1:2])


def emit_colmax_to_scalar(sq, blk, nc, mcol, identf, psT, mrow, out11):
    @sq.step(blk, "tensor")
    def _(eng, st):
        return eng.transpose(psT[0:1, 0:128], mcol, identf[:, :])

    @sq.step(blk, "vector")
    def _(eng, st):
        st.chain(eng.tensor_copy(out=mrow[0:1, :], in_=psT[0:1, 0:128]))
        return eng.reduce_max(out=out11, in_=mrow[0:1, :], axis=mybir.AxisListType.X)


def emit_dequant_chunk(eng, st, w_ap, wq_ap, nblk, sw, swr, r, ti, tf, td):
    wv = w_ap.rearrange("p (b i) -> p b i", i=32)
    st.chain(eng.reduce_max(out=sw, in_=wv, axis=mybir.AxisListType.X,
                            apply_absolute_value=True))
    st.chain(eng.tensor_scalar(out=sw, in0=sw, scalar1=1.0 / 7.0, scalar2=1e-12,
                               op0=ALU.mult, op1=ALU.max))
    st.chain(eng.reciprocal(out=swr, in_=sw))
    rv = r.rearrange("p (b i) -> p b i", i=32)
    st.chain(eng.tensor_tensor(out=rv, in0=wv,
                               in1=swr.unsqueeze(-1).to_broadcast(list(sw.shape) + [32]),
                               op=ALU.mult))
    st.chain(emit_round_inplace(eng, st, r, ti, tf, td))
    return eng.tensor_tensor(out=wq_ap.rearrange("p (b i) -> p b i", i=32), in0=rv,
                             in1=sw.unsqueeze(-1).to_broadcast(list(sw.shape) + [32]),
                             op=ALU.mult)



def build_E1():
    nc = bass.Bass("TRN2", target_bir_lowering=False)
    xn1F = nc.dram_tensor("xn1F", [C, TPC], F32, kind="ExternalInput").ap()
    xmax8 = nc.dram_tensor("xmax8", [1, 8], F32, kind="ExternalInput").ap()
    wgn = nc.dram_tensor("wgn", [I_L1, C], F32, kind="ExternalInput").ap()
    wun = nc.dram_tensor("wun", [I_L1, C], F32, kind="ExternalInput").ap()
    upD = nc.dram_tensor("upD", [I_L1, TPC], BF16, kind="ExternalOutput").ap()
    gateD = nc.dram_tensor("gateD", [I_L1, TPC], BF16, kind="ExternalOutput").ap()
    mups = nc.dram_tensor("mups", [1, 1], F32, kind="ExternalOutput").ap()
    mgts = nc.dram_tensor("mgts", [1, 1], F32, kind="ExternalOutput").ap()

    NIC = I_L1 // 128

    from contextlib import ExitStack
    with ExitStack() as _ctx:
        xqsb = _ctx.enter_context(nc.sbuf_tensor("xqsb", [128, NCH * TPC], BF16))
        wgT = _ctx.enter_context(nc.sbuf_tensor("wgT", [128, NCH * I_L1], BF16))
        wuT = _ctx.enter_context(nc.sbuf_tensor("wuT", [128, NCH * I_L1], BF16))
        wch = _ctx.enter_context(nc.sbuf_tensor("wch", [128, 1024], F32))
        wqc = _ctx.enter_context(nc.sbuf_tensor("wqc", [128, 1024], BF16))
        sw = _ctx.enter_context(nc.sbuf_tensor("sw", [128, 32], F32))
        swr = _ctx.enter_context(nc.sbuf_tensor("swr", [128, 32], F32))
        rr = _ctx.enter_context(nc.sbuf_tensor("rr", [128, 2048], F32))
        ti = _ctx.enter_context(nc.sbuf_tensor("ti", [128, 2048], I32))
        tf = _ctx.enter_context(nc.sbuf_tensor("tf", [128, 2048], F32))
        td = _ctx.enter_context(nc.sbuf_tensor("td", [128, 2048], F32))
        xch = _ctx.enter_context(nc.sbuf_tensor("xch", [128, 2048], F32))
        urow = _ctx.enter_context(nc.sbuf_tensor("urow", [128, 512], BF16))
        onesr = _ctx.enter_context(nc.sbuf_tensor("onesr", [1, 128], F32))
        identf = _ctx.enter_context(nc.sbuf_tensor("identf", [128, 128], F32))
        identb = _ctx.enter_context(nc.sbuf_tensor("identb", [128, 128], BF16))
        srow = _ctx.enter_context(nc.sbuf_tensor("srow", [1, 8], F32))
        mrow = _ctx.enter_context(nc.sbuf_tensor("mrow", [1, 128], F32))
        scol = _ctx.enter_context(nc.sbuf_tensor("scol", [128, 1], F32))
        icol = _ctx.enter_context(nc.sbuf_tensor("icol", [128, 1], F32))
        mxu = _ctx.enter_context(nc.sbuf_tensor("mxu", [128, 3], F32))
        mxg = _ctx.enter_context(nc.sbuf_tensor("mxg", [128, 3], F32))
        psA = _ctx.enter_context(nc.psum_tensor("psA", [128, 512], F32))
        psW = _ctx.enter_context(nc.psum_tensor("psW", [128, 512], BF16))
        psT = _ctx.enter_context(nc.psum_tensor("psT", [128, 128], F32))
        blk = _ctx.enter_context(nc.Block())

        sq = Seq(nc)
        xqv = xqsb.ap().rearrange("p (c t) -> p c t", c=NCH)
        wgv = wgT.ap().rearrange("p (c t) -> p c t", c=NCH)
        wuv = wuT.ap().rearrange("p (c t) -> p c t", c=NCH)

        @sq.step(blk, "gpsimd")
        def _(eng, st):
            st.dma(srow[0:1, :], xmax8[0:1, :])

        @sq.step(blk, "gpsimd")
        def _(eng, st):
            st.chain(make_ident(nc, eng, identf[:, :], st))
            return make_ident(nc, eng, identb[:, :], st)

        @sq.step(blk, "vector")
        def _(eng, st):
            eng.memset(onesr[:, :], 1.0)
            eng.memset(mxu[:, 0:1], 0.0)
            return eng.memset(mxg[:, 0:1], 0.0)

        emit_scale_from_max(sq, blk, nc, onesr, srow, 8, 32767.0, srow, scol[:, 0:1], icol[:, 0:1], psT)

        for c in range(NCH):
            @sq.step(blk, "sync")
            def _(eng, st, c=c):
                st.dma(xch[:, :], xn1F[128 * c:128 * (c + 1), :])

            @sq.step(blk, "vector")
            def _(eng, st, c=c):
                return emit_qdq(eng, st, xqv[:, c], xch[:, :], icol[:, 0:1],
                                scol[:, 0:1], rr[:, :], ti[:, :], tf[:, :], td[:, :])

        for which, wn, wTv in (("g", wgn, wgv), ("u", wun, wuv)):
            for oc in range(NIC):
                @sq.step(blk, "sync")
                def _(eng, st, wn=wn, oc=oc):
                    st.dma(wch[:, :], wn[128 * oc:128 * (oc + 1), :])

                @sq.step(blk, "vector")
                def _(eng, st):
                    return emit_dequant_chunk(eng, st, wch[:, :], wqc[:, :], 32,
                                              sw[:, :], swr[:, :], rr[:, 0:1024],
                                              ti[:, 0:1024], tf[:, 0:1024], td[:, 0:1024])

                for hc in range(NCH):
                    @sq.step(blk, "tensor")
                    def _(eng, st, hc=hc):
                        return eng.transpose(psW[:, 0:128],
                                             wqc[:, 128 * hc:128 * (hc + 1)],
                                             identb[:, :])

                    @sq.step(blk, "vector")
                    def _(eng, st, hc=hc, oc=oc, wTv=wTv):
                        return eng.tensor_copy(out=wTv[:, hc][:, 128 * oc:128 * (oc + 1)],
                                               in_=psW[:, 0:128])

        for which, wTv, outD, mx in (("g", wgv, gateD, mxg), ("u", wuv, upD, mxu)):
            for ic in range(NIC):
                for tt in range(NTT):
                    @sq.step(blk, "tensor")
                    def _(eng, st, ic=ic, tt=tt, wTv=wTv):
                        last = None
                        for k in range(NCH):
                            last = eng.matmul(psA[:, :],
                                              wTv[:, k][:, 128 * ic:128 * (ic + 1)],
                                              xqv[:, k][:, 512 * tt:512 * (tt + 1)],
                                              start=(k == 0), stop=(k == NCH - 1))
                        return last

                    @sq.step(blk, "vector")
                    def _(eng, st, mx=mx):
                        st.chain(eng.tensor_copy(out=urow[:, :], in_=psA[:, :]))
                        st.chain(eng.reduce_max(out=mx[:, 1:2], in_=psA[:, :],
                                                axis=mybir.AxisListType.X,
                                                apply_absolute_value=True))
                        return eng.tensor_tensor(out=mx[:, 0:1], in0=mx[:, 0:1],
                                                 in1=mx[:, 1:2], op=ALU.max)

                    @sq.step(blk, "sync")
                    def _(eng, st, ic=ic, tt=tt, outD=outD):
                        st.dma(outD[128 * ic:128 * (ic + 1), 512 * tt:512 * (tt + 1)],
                               urow[:, :])

        emit_colmax_to_scalar(sq, blk, nc, mxg[:, 0:1], identf, psT, mrow, srow[0:1, 3:4])
        emit_colmax_to_scalar(sq, blk, nc, mxu[:, 0:1], identf, psT, mrow, srow[0:1, 4:5])

        @sq.step(blk, "sync")
        def _(eng, st):
            st.dma(mgts[0:1, 0:1], srow[0:1, 3:4])
            st.dma(mups[0:1, 0:1], srow[0:1, 4:5])

        @sq.step(blk, "sync")
        def _(eng, st):
            return eng.nop()

    return nc



def build_E2():
    nc = bass.Bass("TRN2", target_bir_lowering=False)
    upD = nc.dram_tensor("upD", [I_L1, TPC], BF16, kind="ExternalInput").ap()
    gateD = nc.dram_tensor("gateD", [I_L1, TPC], BF16, kind="ExternalInput").ap()
    mup8 = nc.dram_tensor("mup8", [1, 8], F32, kind="ExternalInput").ap()
    mgt8 = nc.dram_tensor("mgt8", [1, 8], F32, kind="ExternalInput").ap()
    upqD = nc.dram_tensor("upqD", [I_L1, TPC], BF16, kind="ExternalOutput").ap()
    g2D = nc.dram_tensor("g2D", [I_L1, TPC], BF16, kind="ExternalOutput").ap()
    mg2s = nc.dram_tensor("mg2s", [1, 1], F32, kind="ExternalOutput").ap()

    NIC = I_L1 // 128

    from contextlib import ExitStack
    with ExitStack() as _ctx:
        ch = _ctx.enter_context(nc.sbuf_tensor("ch", [128, TPC], F32))
        ch2 = _ctx.enter_context(nc.sbuf_tensor("ch2", [128, TPC], F32))
        chb = _ctx.enter_context(nc.sbuf_tensor("chb", [128, TPC], BF16))
        rr = _ctx.enter_context(nc.sbuf_tensor("rr", [128, TPC], F32))
        ti = _ctx.enter_context(nc.sbuf_tensor("ti", [128, TPC], I32))
        tf = _ctx.enter_context(nc.sbuf_tensor("tf", [128, TPC], F32))
        td = _ctx.enter_context(nc.sbuf_tensor("td", [128, TPC], F32))
        onesr = _ctx.enter_context(nc.sbuf_tensor("onesr", [1, 128], F32))
        identf = _ctx.enter_context(nc.sbuf_tensor("identf", [128, 128], F32))
        srow = _ctx.enter_context(nc.sbuf_tensor("srow", [1, 8], F32))
        mrow = _ctx.enter_context(nc.sbuf_tensor("mrow", [1, 128], F32))
        sgc = _ctx.enter_context(nc.sbuf_tensor("sgc", [128, 2], F32))
        suc = _ctx.enter_context(nc.sbuf_tensor("suc", [128, 2], F32))
        mxc = _ctx.enter_context(nc.sbuf_tensor("mxc", [128, 2], F32))
        psT = _ctx.enter_context(nc.psum_tensor("psT", [128, 128], F32))
        blk = _ctx.enter_context(nc.Block())

        sq = Seq(nc)

        @sq.step(blk, "gpsimd")
        def _(eng, st):
            st.dma(srow[0:1, 0:8], mgt8[0:1, :])

        @sq.step(blk, "gpsimd")
        def _(eng, st):
            return make_ident(nc, eng, identf[:, :], st)

        @sq.step(blk, "vector")
        def _(eng, st):
            eng.memset(onesr[:, :], 1.0)
            return eng.memset(mxc[:, 0:1], 0.0)

        emit_scale_from_max(sq, blk, nc, onesr, srow, 8, 32767.0, srow,
                            sgc[:, 0:1], sgc[:, 1:2], psT)

        @sq.step(blk, "gpsimd")
        def _(eng, st):
            st.dma(srow[0:1, 0:8], mup8[0:1, :])

        emit_scale_from_max(sq, blk, nc, onesr, srow, 8, 32767.0, srow,
                            suc[:, 0:1], suc[:, 1:2], psT)

        for ic in range(NIC):
            @sq.step(blk, "sync")
            def _(eng, st, ic=ic):
                st.dma(chb[:, :], gateD[128 * ic:128 * (ic + 1), :])

            @sq.step(blk, "vector")
            def _(eng, st):
                st.chain(eng.tensor_copy(out=ch[:, :], in_=chb[:, :]))
                return emit_qdq(eng, st, ch[:, :], ch[:, :], sgc[:, 1:2],
                                sgc[:, 0:1], rr[:, :], ti[:, :], tf[:, :], td[:, :])

            @sq.step(blk, "scalar")
            def _(eng, st):
                return eng.activation(out=ch2[:, :], in_=ch[:, :], func=AF.Sigmoid)

            @sq.step(blk, "vector")
            def _(eng, st):
                st.chain(eng.tensor_scalar(out=rr[:, :], in0=ch2[:, :],
                                           scalar1=65536.0, scalar2=None, op0=ALU.mult))
                st.chain(emit_round_inplace(eng, st, rr[:, :], ti[:, :], tf[:, :], td[:, :]))
                st.chain(eng.tensor_scalar(out=rr[:, :], in0=rr[:, :], scalar1=65535.0,
                                           scalar2=1.0 / 65536.0, op0=ALU.min, op1=ALU.mult))
                st.chain(eng.tensor_tensor(out=ch[:, :], in0=ch[:, :], in1=rr[:, :],
                                           op=ALU.mult))
                st.chain(eng.tensor_copy(out=chb[:, :], in_=ch[:, :]))
                st.chain(eng.reduce_max(out=mxc[:, 1:2], in_=ch[:, :],
                                        axis=mybir.AxisListType.X,
                                        apply_absolute_value=True))
                return eng.tensor_tensor(out=mxc[:, 0:1], in0=mxc[:, 0:1],
                                         in1=mxc[:, 1:2], op=ALU.max)

            @sq.step(blk, "sync")
            def _(eng, st, ic=ic):
                st.dma(g2D[128 * ic:128 * (ic + 1), :], chb[:, :])

            @sq.step(blk, "sync")
            def _(eng, st, ic=ic):
                st.dma(chb[:, :], upD[128 * ic:128 * (ic + 1), :])

            @sq.step(blk, "vector")
            def _(eng, st):
                st.chain(eng.tensor_copy(out=ch[:, :], in_=chb[:, :]))
                st.chain(emit_qdq(eng, st, ch[:, :], ch[:, :], suc[:, 1:2],
                                  suc[:, 0:1], rr[:, :], ti[:, :], tf[:, :], td[:, :]))
                return eng.tensor_copy(out=chb[:, :], in_=ch[:, :])

            @sq.step(blk, "sync")
            def _(eng, st, ic=ic):
                st.dma(upqD[128 * ic:128 * (ic + 1), :], chb[:, :])

        emit_colmax_to_scalar(sq, blk, nc, mxc[:, 0:1], identf, psT, mrow, srow[0:1, 3:4])

        @sq.step(blk, "sync")
        def _(eng, st):
            st.dma(mg2s[0:1, 0:1], srow[0:1, 3:4])

        @sq.step(blk, "sync")
        def _(eng, st):
            return eng.nop()

    return nc



def build_E3():
    nc = bass.Bass("TRN2", target_bir_lowering=False)
    upqD = nc.dram_tensor("upqD", [I_L1, TPC], BF16, kind="ExternalInput").ap()
    g2D = nc.dram_tensor("g2D", [I_L1, TPC], BF16, kind="ExternalInput").ap()
    mg28 = nc.dram_tensor("mg28", [1, 8], F32, kind="ExternalInput").ap()
    oD = nc.dram_tensor("oD", [I_L1, TPC], BF16, kind="ExternalOutput").ap()
    mos = nc.dram_tensor("mos", [1, 1], F32, kind="ExternalOutput").ap()

    NIC = I_L1 // 128

    from contextlib import ExitStack
    with ExitStack() as _ctx:
        ch = _ctx.enter_context(nc.sbuf_tensor("ch", [128, TPC], F32))
        chb = _ctx.enter_context(nc.sbuf_tensor("chb", [128, TPC], BF16))
        chb2 = _ctx.enter_context(nc.sbuf_tensor("chb2", [128, TPC], BF16))
        rr = _ctx.enter_context(nc.sbuf_tensor("rr", [128, TPC], F32))
        ti = _ctx.enter_context(nc.sbuf_tensor("ti", [128, TPC], I32))
        tf = _ctx.enter_context(nc.sbuf_tensor("tf", [128, TPC], F32))
        td = _ctx.enter_context(nc.sbuf_tensor("td", [128, TPC], F32))
        onesr = _ctx.enter_context(nc.sbuf_tensor("onesr", [1, 128], F32))
        identf = _ctx.enter_context(nc.sbuf_tensor("identf", [128, 128], F32))
        srow = _ctx.enter_context(nc.sbuf_tensor("srow", [1, 8], F32))
        mrow = _ctx.enter_context(nc.sbuf_tensor("mrow", [1, 128], F32))
        sgc = _ctx.enter_context(nc.sbuf_tensor("sgc", [128, 2], F32))
        mxc = _ctx.enter_context(nc.sbuf_tensor("mxc", [128, 2], F32))
        psT = _ctx.enter_context(nc.psum_tensor("psT", [128, 128], F32))
        blk = _ctx.enter_context(nc.Block())

        sq = Seq(nc)

        @sq.step(blk, "gpsimd")
        def _(eng, st):
            st.dma(srow[0:1, 0:8], mg28[0:1, :])

        @sq.step(blk, "gpsimd")
        def _(eng, st):
            return make_ident(nc, eng, identf[:, :], st)

        @sq.step(blk, "vector")
        def _(eng, st):
            eng.memset(onesr[:, :], 1.0)
            return eng.memset(mxc[:, 0:1], 0.0)

        emit_scale_from_max(sq, blk, nc, onesr, srow, 8, 32767.0, srow,
                            sgc[:, 0:1], sgc[:, 1:2], psT)

        for ic in range(NIC):
            @sq.step(blk, "sync")
            def _(eng, st, ic=ic):
                st.dma(chb[:, :], g2D[128 * ic:128 * (ic + 1), :])
                st.dma(chb2[:, :], upqD[128 * ic:128 * (ic + 1), :])

            @sq.step(blk, "vector")
            def _(eng, st):
                st.chain(eng.tensor_copy(out=ch[:, :], in_=chb[:, :]))
                st.chain(emit_qdq(eng, st, ch[:, :], ch[:, :], sgc[:, 1:2],
                                  sgc[:, 0:1], rr[:, :], ti[:, :], tf[:, :], td[:, :]))
                st.chain(eng.tensor_tensor(out=ch[:, :], in0=ch[:, :], in1=chb2[:, :],
                                           op=ALU.mult))
                st.chain(eng.tensor_copy(out=chb[:, :], in_=ch[:, :]))
                st.chain(eng.reduce_max(out=mxc[:, 1:2], in_=ch[:, :],
                                        axis=mybir.AxisListType.X,
                                        apply_absolute_value=True))
                return eng.tensor_tensor(out=mxc[:, 0:1], in0=mxc[:, 0:1],
                                         in1=mxc[:, 1:2], op=ALU.max)

            @sq.step(blk, "sync")
            def _(eng, st, ic=ic):
                st.dma(oD[128 * ic:128 * (ic + 1), :], chb[:, :])

        emit_colmax_to_scalar(sq, blk, nc, mxc[:, 0:1], identf, psT, mrow, srow[0:1, 3:4])

        @sq.step(blk, "sync")
        def _(eng, st):
            st.dma(mos[0:1, 0:1], srow[0:1, 3:4])

        @sq.step(blk, "sync")
        def _(eng, st):
            return eng.nop()

    return nc



def build_E4():
    nc = bass.Bass("TRN2", target_bir_lowering=False)
    oD = nc.dram_tensor("oD", [I_L1, TPC], BF16, kind="ExternalInput").ap()
    mo8 = nc.dram_tensor("mo8", [1, 8], F32, kind="ExternalInput").ap()
    wdn = nc.dram_tensor("wdn", [C, I_L1], F32, kind="ExternalInput").ap()
    xBT = nc.dram_tensor("xBT", [C, TPC], F32, kind="ExternalInput").ap()
    xoutT = nc.dram_tensor("xoutT", [C, TPC], F32, kind="ExternalOutput").ap()

    NIC = I_L1 // 128

    from contextlib import ExitStack
    with ExitStack() as _ctx:
        oq = _ctx.enter_context(nc.sbuf_tensor("oq", [128, NIC * TPC // 4], BF16))
        wdT = _ctx.enter_context(nc.sbuf_tensor("wdT", [128, NIC * 1024], BF16))
        wch = _ctx.enter_context(nc.sbuf_tensor("wch", [128, I_L1], F32))
        wqc = _ctx.enter_context(nc.sbuf_tensor("wqc", [128, I_L1], BF16))
        sw = _ctx.enter_context(nc.sbuf_tensor("sw", [128, 96], F32))
        swr = _ctx.enter_context(nc.sbuf_tensor("swr", [128, 96], F32))
        rr = _ctx.enter_context(nc.sbuf_tensor("rr", [128, I_L1], F32))
        ti = _ctx.enter_context(nc.sbuf_tensor("ti", [128, I_L1], I32))
        tf = _ctx.enter_context(nc.sbuf_tensor("tf", [128, I_L1], F32))
        td = _ctx.enter_context(nc.sbuf_tensor("td", [128, I_L1], F32))
        chb = _ctx.enter_context(nc.sbuf_tensor("chb", [128, TPC], BF16))
        ch = _ctx.enter_context(nc.sbuf_tensor("ch", [128, TPC], F32))
        onesr = _ctx.enter_context(nc.sbuf_tensor("onesr", [1, 128], F32))
        identb = _ctx.enter_context(nc.sbuf_tensor("identb", [128, 128], BF16))
        srow = _ctx.enter_context(nc.sbuf_tensor("srow", [1, 8], F32))
        scol = _ctx.enter_context(nc.sbuf_tensor("scol", [128, 1], F32))
        icol = _ctx.enter_context(nc.sbuf_tensor("icol", [128, 1], F32))
        xbu = _ctx.enter_context(nc.sbuf_tensor("xbu", [128, 512], F32))
        xao = _ctx.enter_context(nc.sbuf_tensor("xao", [128, 512], F32))
        psA = _ctx.enter_context(nc.psum_tensor("psA", [128, 512], F32))
        psW = _ctx.enter_context(nc.psum_tensor("psW", [128, 128], BF16))
        psT = _ctx.enter_context(nc.psum_tensor("psT", [128, 128], F32))
        blk = _ctx.enter_context(nc.Block())

        sq = Seq(nc)
        oqv = oq.ap().rearrange("p (c t) -> p c t", c=NIC)
        wdv = wdT.ap().rearrange("p (c t) -> p c t", c=NIC)

        @sq.step(blk, "gpsimd")
        def _(eng, st):
            st.dma(srow[0:1, 0:8], mo8[0:1, :])

        @sq.step(blk, "gpsimd")
        def _(eng, st):
            return make_ident(nc, eng, identb[:, :], st)

        @sq.step(blk, "vector")
        def _(eng, st):
            return eng.memset(onesr[:, :], 1.0)

        emit_scale_from_max(sq, blk, nc, onesr, srow, 8, 32767.0, srow, scol[:, 0:1], icol[:, 0:1], psT)

        for hc in range(NCH):
            @sq.step(blk, "sync")
            def _(eng, st, hc=hc):
                st.dma(wch[:, :], wdn[128 * hc:128 * (hc + 1), :])

            @sq.step(blk, "vector")
            def _(eng, st):
                return emit_dequant_chunk(eng, st, wch[:, :], wqc[:, :], 96,
                                          sw[:, :], swr[:, :], rr[:, :],
                                          ti[:, :], tf[:, :], td[:, :])

            for ic in range(NIC):
                @sq.step(blk, "tensor")
                def _(eng, st, ic=ic):
                    return eng.transpose(psW[:, 0:128],
                                         wqc[:, 128 * ic:128 * (ic + 1)],
                                         identb[:, :])

                @sq.step(blk, "vector")
                def _(eng, st, ic=ic, hc=hc):
                    return eng.tensor_copy(out=wdv[:, ic][:, 128 * hc:128 * (hc + 1)],
                                           in_=psW[:, 0:128])

        for tt in range(NTT):
            for ic in range(NIC):
                @sq.step(blk, "sync")
                def _(eng, st, ic=ic, tt=tt):
                    st.dma(chb[:, 0:512], oD[128 * ic:128 * (ic + 1),
                                             512 * tt:512 * (tt + 1)])

                @sq.step(blk, "vector")
                def _(eng, st, ic=ic):
                    st.chain(eng.tensor_copy(out=ch[:, 0:512], in_=chb[:, 0:512]))
                    return emit_qdq(eng, st, oqv[:, ic], ch[:, 0:512], icol[:, 0:1],
                                    scol[:, 0:1], rr[:, 0:512], ti[:, 0:512],
                                    tf[:, 0:512], td[:, 0:512])

            for oc in range(NCH):
                @sq.step(blk, "tensor")
                def _(eng, st, oc=oc):
                    last = None
                    for k in range(NIC):
                        last = eng.matmul(psA[:, :],
                                          wdv[:, k][:, 128 * oc:128 * (oc + 1)],
                                          oqv[:, k], start=(k == 0), stop=(k == NIC - 1))
                    return last

                @sq.step(blk, "sync")
                def _(eng, st, oc=oc, tt=tt):
                    st.dma(xbu[:, :], xBT[128 * oc:128 * (oc + 1), 512 * tt:512 * (tt + 1)])

                @sq.step(blk, "vector")
                def _(eng, st):
                    return eng.tensor_tensor(out=xao[:, :], in0=psA[:, :],
                                             in1=xbu[:, :], op=ALU.add)

                @sq.step(blk, "sync")
                def _(eng, st, oc=oc, tt=tt):
                    st.dma(xoutT[128 * oc:128 * (oc + 1), 512 * tt:512 * (tt + 1)],
                           xao[:, :])

        @sq.step(blk, "sync")
        def _(eng, st):
            return eng.nop()

    return nc



def _shard_x_T(x):
    out = []
    for c in range(NCORE):
        sl = x[:, 512 * c:512 * (c + 1), :]
        sl = sl.reshape(TPC, C).T
        out.append(np.ascontiguousarray(sl, dtype=np.float32))
    return out


def run_A(inputs):
    key = "A"
    if key not in _prog_cache:
        _prog_cache[key] = build_A()
    nc = _prog_cache[key]
    xs = _shard_x_T(inputs["L0_attn_hidden_states"])
    base = {
        "wq": np.ascontiguousarray(inputs["L0_attn_Wq"], np.float32),
        "wk": np.ascontiguousarray(inputs["L0_attn_Wk"], np.float32),
        "wvv": np.ascontiguousarray(inputs["L0_attn_Wv"], np.float32),
        "wvg": np.ascontiguousarray(inputs["L0_attn_Wvg"], np.float32),
        "wqi": np.ascontiguousarray(inputs["L0_attn_Wqi"], np.float32),
        "wki": np.ascontiguousarray(inputs["L0_attn_Wki"], np.float32),
        "wwi": np.ascontiguousarray(inputs["L0_attn_Wwi"], np.float32),
        "wog": np.ascontiguousarray(inputs["L0_attn_Wog"], np.float32),
        "bvg": np.ascontiguousarray(inputs["L0_attn_bvg"].reshape(1, 256), np.float32),
        "bog": np.ascontiguousarray(inputs["L0_attn_bog"].reshape(1, 1024), np.float32),
    }
    in_maps = [dict(base, xT=xs[c],
                    posoff=np.full((128, 1), 512.0 * c, np.float32))
               for c in range(NCORE)]
    res = run_bass_kernel_spmd(nc, in_maps, core_ids=list(range(NCORE)))
    return res.results


def assemble_kv_all(kv_locs):
    kv = np.empty((B * L, 576), kv_locs[0].dtype)
    for c in range(NCORE):
        for b in range(B):
            kv[b * L + 512 * c: b * L + 512 * (c + 1)] = \
                kv_locs[c][b * 512:(b + 1) * 512]
    return kv


def run_B(inputs, a_results, n_batches=4):
    key = ("B", n_batches)
    if key not in _prog_cache:
        _prog_cache[key] = build_B(n_batches=n_batches)
    nc = _prog_cache[key]
    xs = _shard_x_T(inputs["L0_attn_hidden_states"])
    kv = assemble_kv_all([r["kv_loc"] for r in a_results])
    base = {
        "kv_all": kv,
        "bi_in": np.ascontiguousarray(inputs["L0_attn_bi"].reshape(1, 4), np.float32),
        "wo": np.ascontiguousarray(inputs["L0_attn_Wo"], np.float32),
    }
    in_maps = [dict(base, xT=xs[c], qrT=a_results[c]["qrT"],
                    qiT=a_results[c]["qiT"], g1T=a_results[c]["g1T"],
                    wI_tok=a_results[c]["wI_tok"]) for c in range(NCORE)]
    res = run_bass_kernel_spmd(nc, in_maps, core_ids=list(range(NCORE)))
    return res.results


def _c_base_inputs(inputs):
    tr = lambda a, ax: np.ascontiguousarray(np.transpose(np.asarray(a, np.float32), ax))
    return {
        "gwT": tr(inputs["L0_ffn_gate_w"], (1, 0)),
        "ebias": np.ascontiguousarray(np.asarray(inputs["L0_ffn_expert_bias"], np.float32).reshape(1, 16)),
        "wgT": tr(inputs["L0_ffn_w_gate"], (0, 2, 1)),
        "wuT": tr(inputs["L0_ffn_w_up"], (0, 2, 1)),
        "wdT": tr(inputs["L0_ffn_w_down"], (0, 2, 1)),
        "sgT": tr(inputs["L0_ffn_shared_gate_w"], (1, 0)),
        "suT": tr(inputs["L0_ffn_shared_up_w"], (1, 0)),
        "sdT": tr(inputs["L0_ffn_shared_down_w"], (1, 0)),
    }


def run_C(inputs, a_results, b_results):
    key = "C"
    if key not in _prog_cache:
        _prog_cache[key] = build_C()
    nc = _prog_cache[key]
    base = _c_base_inputs(inputs)
    in_maps = [dict(base, xnT=a_results[c]["xnT"], xAT=b_results[c]["xAT"])
               for c in range(NCORE)]
    res = run_bass_kernel_spmd(nc, in_maps, core_ids=list(range(NCORE)))
    return res.results


def _d_base_inputs(inputs):
    return {
        "wkv": np.ascontiguousarray(inputs["L1_attn_Wkv"], np.float32),
        "bkv": np.ascontiguousarray(np.asarray(inputs["L1_attn_bkv"], np.float32).reshape(1, 2048)),
        "wq1": np.ascontiguousarray(inputs["L1_attn_Wq"], np.float32),
        "bq1": np.ascontiguousarray(np.asarray(inputs["L1_attn_bq"], np.float32).reshape(1, 1024)),
        "wp1": np.ascontiguousarray(inputs["L1_attn_Wp"], np.float32),
        "bp1": np.ascontiguousarray(np.asarray(inputs["L1_attn_bp"], np.float32).reshape(1, 1024)),
    }


def run_D(inputs, c_results):
    key = "D"
    if key not in _prog_cache:
        _prog_cache[key] = build_D()
    nc = _prog_cache[key]
    base = _d_base_inputs(inputs)
    in_maps = [dict(base, xn1T=c_results[c]["xn1T"], x1T=c_results[c]["x1T"])
               for c in range(NCORE)]
    res = run_bass_kernel_spmd(nc, in_maps, core_ids=list(range(NCORE)))
    return res.results


def run_E1(inputs, c_results):
    if "E1" not in _prog_cache:
        _prog_cache["E1"] = build_E1()
    nc = _prog_cache["E1"]
    xmax8 = np.ascontiguousarray(
        np.stack([c_results[c]["xmax"][0, 0] for c in range(NCORE)]).reshape(1, 8),
        np.float32)
    base = {
        "xmax8": xmax8,
        "wgn": np.ascontiguousarray(inputs["L1_ffn_w_gate"], np.float32),
        "wun": np.ascontiguousarray(inputs["L1_ffn_w_up"], np.float32),
    }
    in_maps = [dict(base, xn1F=c_results[c]["xn1F"]) for c in range(NCORE)]
    return run_bass_kernel_spmd(nc, in_maps, core_ids=list(range(NCORE))).results


def run_E2(e1_results):
    if "E2" not in _prog_cache:
        _prog_cache["E2"] = build_E2()
    nc = _prog_cache["E2"]
    mup8 = np.ascontiguousarray(
        np.stack([e1_results[c]["mups"][0, 0] for c in range(NCORE)]).reshape(1, 8), np.float32)
    mgt8 = np.ascontiguousarray(
        np.stack([e1_results[c]["mgts"][0, 0] for c in range(NCORE)]).reshape(1, 8), np.float32)
    in_maps = [dict(mup8=mup8, mgt8=mgt8, upD=e1_results[c]["upD"],
                    gateD=e1_results[c]["gateD"]) for c in range(NCORE)]
    return run_bass_kernel_spmd(nc, in_maps, core_ids=list(range(NCORE))).results


def run_E3(e2_results):
    if "E3" not in _prog_cache:
        _prog_cache["E3"] = build_E3()
    nc = _prog_cache["E3"]
    mg28 = np.ascontiguousarray(
        np.stack([e2_results[c]["mg2s"][0, 0] for c in range(NCORE)]).reshape(1, 8), np.float32)
    in_maps = [dict(mg28=mg28, upqD=e2_results[c]["upqD"],
                    g2D=e2_results[c]["g2D"]) for c in range(NCORE)]
    return run_bass_kernel_spmd(nc, in_maps, core_ids=list(range(NCORE))).results


def run_E4(inputs, e3_results, d_results):
    if "E4" not in _prog_cache:
        _prog_cache["E4"] = build_E4()
    nc = _prog_cache["E4"]
    mo8 = np.ascontiguousarray(
        np.stack([e3_results[c]["mos"][0, 0] for c in range(NCORE)]).reshape(1, 8), np.float32)
    base = {"mo8": mo8,
            "wdn": np.ascontiguousarray(inputs["L1_ffn_w_down"], np.float32)}
    in_maps = [dict(base, oD=e3_results[c]["oD"], xBT=d_results[c]["xBT"])
               for c in range(NCORE)]
    return run_bass_kernel_spmd(nc, in_maps, core_ids=list(range(NCORE))).results


def kernel(**inputs):
    inputs = {k: np.asarray(v) for k, v in inputs.items()}
    ra = run_A(inputs)
    rb = run_B(inputs, ra)
    rc = run_C(inputs, ra, rb)
    rd = run_D(inputs, rc)
    re1 = run_E1(inputs, rc)
    re2 = run_E2(re1)
    re3 = run_E3(re2)
    re4 = run_E4(inputs, re3, rd)
    out = np.empty((B, L, C), np.float32)
    for c in range(NCORE):
        xo = re4[c]["xoutT"]
        out[:, 512 * c:512 * (c + 1), :] = xo.T.reshape(B, 512, C)
    return out


# version 05
# speedup vs baseline: 1.1842x; 1.1842x over previous
import math
import numpy as np

import concourse.bass as bass
import concourse.mybir as mybir
from concourse.bass_utils import run_bass_kernel_spmd

F32 = mybir.dt.float32
F32R = mybir.dt.float32r
BF16 = mybir.dt.bfloat16
I32 = mybir.dt.int32
AF = mybir.ActivationFunctionType
ALU = mybir.AluOpType

NCORE = 8
B, L, C = 4, 4096, 1024
TPC = B * L // NCORE
PPC = L // NCORE
NCH = C // 128
NTT = TPC // 512
H, H_KV, D = 16, 4, 64
K_SEL = 256
H_I, D_I = 4, 64
ROPE_BASE = 10000.0
SM_SCALE = D ** -0.5
I_L1 = 3072
LN_EPS = 1e-5
NEG_BIG = -30000.0
TWO_PI = 2.0 * math.pi

ENGINES = ("sync", "tensor", "vector", "scalar", "gpsimd")

_prog_cache = {}


class Seq:

    def __init__(self, nc):
        self.nc = nc
        keys = list(ENGINES) + [e + "_dma" for e in ENGINES]
        self.sems = {e: nc.alloc_semaphore(f"sq_{e}") for e in keys}
        self.cnt = {e: 0 for e in keys}

    def step(self, blk, ename):
        waits = [(self.sems[e], self.cnt[e]) for e in self.cnt if self.cnt[e] > 0]
        waits = waits + getattr(self, "extra_waits", [])
        self.extra_waits = []

        def deco(f):
            seq = self

            def body(eng):
                for s, val in waits:
                    eng.wait_ge(s, val)
                st = _Step(seq, ename, eng)
                last = f(eng, st)
                if last is not None:
                    last.then_inc(seq.sems[ename], 1)
                    seq.cnt[ename] += 1
                else:
                    assert st.marked, "step emitted nothing trackable"

            getattr(blk, ename)(body)

        return deco


class _Step:
    def __init__(self, seq, ename, eng):
        self.seq = seq
        self.ename = ename
        self.eng = eng
        self.marked = False

    def dma(self, out, in_, **kw):
        key = self.ename + "_dma"
        inst = self.eng.dma_start(out=out, in_=in_, **kw)
        inst.then_inc(self.seq.sems[key], 16)
        self.seq.cnt[key] += 16
        self.marked = True
        return inst

    def chain(self, inst):
        inst.then_inc(self.seq.sems[self.ename], 1)
        self.seq.cnt[self.ename] += 1
        self.eng.wait_ge(self.seq.sems[self.ename], self.seq.cnt[self.ename])
        self.marked = True
        return inst


def make_rot_T(nc, g, rt, st):
    st.chain(g.memset(rt, 0.0))
    last = None
    for blk_ in (0, 64):
        g.affine_select(out=rt[:, blk_:blk_ + 32], in_=rt[:, blk_:blk_ + 32],
                        compare_op=ALU.not_equal, fill=-1.0,
                        base=-(blk_ + 32), pattern=[[-1, 32]], channel_multiplier=1)
        last = g.affine_select(out=rt[:, blk_ + 32:blk_ + 64], in_=rt[:, blk_ + 32:blk_ + 64],
                               compare_op=ALU.not_equal, fill=1.0,
                               base=-blk_, pattern=[[-1, 32]], channel_multiplier=1)
    return last


def make_ident(nc, g, ident, st):
    st.chain(g.memset(ident, 0.0))
    return g.affine_select(out=ident, in_=ident, compare_op=ALU.not_equal, fill=1.0,
                           base=0, pattern=[[-1, ident.shape[1]]], channel_multiplier=1)



def build_A():
    nc = bass.Bass("TRN2", target_bir_lowering=False)
    xT = nc.dram_tensor("xT", [C, TPC], F32, kind="ExternalInput").ap()
    win = {}
    for name, cols in [("wq", 1024), ("wk", 256), ("wvv", 256), ("wvg", 256),
                       ("wqi", 256), ("wki", 64), ("wwi", 4), ("wog", 1024)]:
        win[name] = (nc.dram_tensor(name, [C, cols], F32, kind="ExternalInput").ap(), cols)
    posoff = nc.dram_tensor("posoff", [128, 1], F32, kind="ExternalInput").ap()
    bvg = nc.dram_tensor("bvg", [1, 256], F32, kind="ExternalInput").ap()
    bog = nc.dram_tensor("bog", [1, 1024], F32, kind="ExternalInput").ap()

    kv_loc = nc.dram_tensor("kv_loc", [TPC, 576], BF16, kind="ExternalOutput").ap()
    qrT = nc.dram_tensor("qrT", [C, TPC], BF16, kind="ExternalOutput").ap()
    qiT = nc.dram_tensor("qiT", [256, TPC], BF16, kind="ExternalOutput").ap()
    g1T = nc.dram_tensor("g1T", [C, TPC], BF16, kind="ExternalOutput").ap()
    xnT = nc.dram_tensor("xnT", [C, TPC], BF16, kind="ExternalOutput").ap()
    wI_tok = nc.dram_tensor("wI_tok", [TPC, 4], F32, kind="ExternalOutput").ap()

    woff = {}
    off = 0
    for name in ("wq", "wk", "wvv", "wvg", "wqi", "wki", "wwi", "wog"):
        woff[name] = off
        off += win[name][1]
    WCOLS = off

    from contextlib import ExitStack
    with ExitStack() as _ctx:
        wsb = _ctx.enter_context(nc.sbuf_tensor("wsb", [128, NCH * WCOLS], BF16))
        xtt = _ctx.enter_context(nc.sbuf_tensor("xtt", [128, NCH * 512], F32))
        xntt = _ctx.enter_context(nc.sbuf_tensor("xntt", [128, NCH * 512], BF16))
        x2 = _ctx.enter_context(nc.sbuf_tensor("x2", [128, 512], F32))
        onesf = _ctx.enter_context(nc.sbuf_tensor("onesf", [128, 1], F32))
        identb = _ctx.enter_context(nc.sbuf_tensor("identb", [128, 128], BF16))
        srow = _ctx.enter_context(nc.sbuf_tensor("srow", [1, 3 * 512], F32))
        mb = _ctx.enter_context(nc.sbuf_tensor("mb", [128, 512], F32))
        rb = _ctx.enter_context(nc.sbuf_tensor("rb", [128, 512], F32))
        rotT = _ctx.enter_context(nc.sbuf_tensor("rotT", [128, 128], BF16))
        costab = _ctx.enter_context(nc.sbuf_tensor("costab", [128, 512], BF16))
        sintab = _ctx.enter_context(nc.sbuf_tensor("sintab", [128, 512], BF16))
        tsc = _ctx.enter_context(nc.sbuf_tensor("tsc", [128, 512], F32))
        tsc2 = _ctx.enter_context(nc.sbuf_tensor("tsc2", [128, 512], F32))
        tin = _ctx.enter_context(nc.sbuf_tensor("tin", [128, 512], I32))
        ifr = _ctx.enter_context(nc.sbuf_tensor("ifr", [128, 1], F32))
        pidf = _ctx.enter_context(nc.sbuf_tensor("pidf", [128, 1], F32))
        epsc = _ctx.enter_context(nc.sbuf_tensor("epsc", [1, 1], F32))
        onesr = _ctx.enter_context(nc.sbuf_tensor("onesr", [1, 128], F32))
        qsb = _ctx.enter_context(nc.sbuf_tensor("qsb", [128, 512], BF16))
        rosb = _ctx.enter_context(nc.sbuf_tensor("rosb", [128, 512], BF16))
        t1 = _ctx.enter_context(nc.sbuf_tensor("t1", [128, 512], F32))
        gsb = _ctx.enter_context(nc.sbuf_tensor("gsb", [128, 512], BF16))
        biasb = _ctx.enter_context(nc.sbuf_tensor("biasb", [128, 10], F32))
        wisb = _ctx.enter_context(nc.sbuf_tensor("wisb", [128, 4], F32))
        ktt = _ctx.enter_context(nc.sbuf_tensor("ktt", [128, 5 * 512], BF16))
        kvsb = _ctx.enter_context(nc.sbuf_tensor("kvsb", [128, 4 * 576], BF16))
        ps0 = _ctx.enter_context(nc.psum_tensor("ps0", [128, 512], F32))
        ps1 = _ctx.enter_context(nc.psum_tensor("ps1", [128, 512], F32))
        ps2 = _ctx.enter_context(nc.psum_tensor("ps2", [128, 512], F32))
        prow = _ctx.enter_context(nc.psum_tensor("prow", [1, 1024], F32))
        pqt = _ctx.enter_context(nc.psum_tensor("pqt", [128, 4], F32))
        ptr = _ctx.enter_context(nc.psum_tensor("ptr", [128, 640], BF16))
        blk = _ctx.enter_context(nc.Block())
        sq = Seq(nc)
        wv_ = wsb.ap().rearrange("p (k c) -> p k c", k=NCH)
        xv_ = xtt.ap().rearrange("p (c t) -> p c t", c=NCH)
        xnv = xntt.ap().rearrange("p (c t) -> p c t", c=NCH)
        ktv = ktt.ap().rearrange("p (c t) -> p c t", c=5)

        def W(name, k, c0, c1):
            return wv_[:, k, woff[name] + c0:woff[name] + c1]

        @sq.step(blk, "gpsimd")
        def _(eng, st):
            for name in woff:
                wap, cols = win[name]
                for k in range(NCH):
                    st.dma(wv_[:, k, woff[name]:woff[name] + cols],
                           wap[128 * k:128 * (k + 1), :])
            st.dma(pidf[:, :], posoff[:, :])
            st.dma(biasb[:, 0:1], bvg[0:1, 0:128].rearrange("o p -> p o"))
            st.dma(biasb[:, 1:2], bvg[0:1, 128:256].rearrange("o p -> p o"))
            for cc in range(8):
                st.dma(biasb[:, 2 + cc:3 + cc],
                       bog[0:1, 128 * cc:128 * (cc + 1)].rearrange("o p -> p o"))

        @sq.step(blk, "gpsimd")
        def _(eng, st):
            make_rot_T(nc, eng, rotT[:, :], st)
            make_ident(nc, eng, identb[:, :], st)
            eng.iota(tin[:, 0:1], pattern=[[0, 1]], base=0, channel_multiplier=1)
            return eng.iota(tin[:, 1:2], pattern=[[0, 1]], base=0, channel_multiplier=1)

        @sq.step(blk, "vector")
        def _(eng, st):
            eng.memset(onesf[:, :], 1.0)
            eng.memset(epsc[:, :], LN_EPS)
            eng.memset(onesr[:, :], 1.0)
            st.chain(eng.tensor_copy(out=t1[:, 0:1], in_=tin[:, 0:1]))
            return eng.tensor_copy(out=ifr[:, :], in_=t1[:, 0:1])

        for thr in (32.0, 64.0, 96.0):
            @sq.step(blk, "vector")
            def _(eng, st, thr=thr):
                st.chain(eng.tensor_scalar(out=tsc[:, 0:1], in0=t1[:, 0:1], scalar1=thr,
                                           scalar2=-32.0, op0=ALU.is_ge, op1=ALU.mult))
                return eng.tensor_tensor(out=ifr[:, :], in0=ifr[:, :],
                                         in1=tsc[:, 0:1], op=ALU.add)

        @sq.step(blk, "scalar")
        def _(eng, st):
            return eng.activation(out=ifr[:, :], in_=ifr[:, :], func=AF.Exp,
                                  scale=-math.log(ROPE_BASE) / 32.0)

        @sq.step(blk, "gpsimd")
        def _(eng, st):
            return eng.iota(tin[:, :], pattern=[[1, 512]], base=0, channel_multiplier=0)

        @sq.step(blk, "vector")
        def _(eng, st):
            return eng.tensor_copy(out=tsc[:, :], in_=tin[:, :])

        @sq.step(blk, "vector")
        def _(eng, st):
            return eng.tensor_scalar(out=tsc[:, :], in0=tsc[:, :],
                                     scalar1=pidf[:, 0:1], scalar2=None, op0=ALU.add)

        @sq.step(blk, "vector")
        def _(eng, st):
            st.chain(eng.tensor_scalar(out=tsc[:, :], in0=tsc[:, :], scalar1=ifr[:, 0:1],
                                       scalar2=None, op0=ALU.mult))
            return eng.tensor_scalar(out=tsc2[:, :], in0=tsc[:, :],
                                     scalar1=1.0 / TWO_PI, scalar2=None, op0=ALU.mult)

        @sq.step(blk, "vector")
        def _(eng, st):
            return eng.tensor_copy(out=tin[:, :], in_=tsc2[:, :])

        @sq.step(blk, "vector")
        def _(eng, st):
            return eng.tensor_copy(out=tsc[:, :], in_=tin[:, :])

        @sq.step(blk, "vector")
        def _(eng, st):
            st.chain(eng.tensor_tensor(out=tsc2[:, :], in0=tsc2[:, :], in1=tsc[:, :],
                                       op=ALU.subtract))
            st.chain(eng.tensor_scalar(out=t1[:, :], in0=tsc2[:, :], scalar1=0.5,
                                       scalar2=None, op0=ALU.is_gt))
            st.chain(eng.tensor_tensor(out=tsc2[:, :], in0=tsc2[:, :], in1=t1[:, :],
                                       op=ALU.subtract))
            st.chain(eng.tensor_scalar(out=t1[:, :], in0=tsc2[:, :], scalar1=-0.5,
                                       scalar2=None, op0=ALU.is_lt))
            st.chain(eng.tensor_tensor(out=tsc2[:, :], in0=tsc2[:, :], in1=t1[:, :],
                                       op=ALU.add))
            return eng.tensor_scalar(out=tsc[:, :], in0=tsc2[:, :],
                                     scalar1=TWO_PI, scalar2=None, op0=ALU.mult)

        @sq.step(blk, "scalar")
        def _(eng, st):
            return eng.activation(out=sintab[:, :], in_=tsc[:, :], func=AF.Sin)

        @sq.step(blk, "vector")
        def _(eng, st):
            st.chain(eng.tensor_scalar(out=tsc2[:, :], in0=tsc[:, :], scalar1=-1.0,
                                       scalar2=math.pi / 2.0, op0=ALU.mult, op1=ALU.add))
            st.chain(eng.tensor_scalar(out=tsc[:, :], in0=tsc2[:, :], scalar1=math.pi,
                                       scalar2=-TWO_PI, op0=ALU.is_gt, op1=ALU.mult))
            return eng.tensor_tensor(out=tsc2[:, :], in0=tsc2[:, :], in1=tsc[:, :],
                                     op=ALU.add)

        @sq.step(blk, "scalar")
        def _(eng, st):
            return eng.activation(out=costab[:, :], in_=tsc2[:, :], func=AF.Sin)

        for tt in range(NTT):
            @sq.step(blk, "sync")
            def _(eng, st, tt=tt):
                for c in range(NCH):
                    st.dma(xv_[:, c], xT[128 * c:128 * (c + 1), 512 * tt:512 * (tt + 1)])

            for c in range(NCH):
                @sq.step(blk, "vector")
                def _(eng, st, c=c):
                    return eng.tensor_tensor(out=x2[:, :], in0=xv_[:, c],
                                             in1=xv_[:, c], op=ALU.mult)

                @sq.step(blk, "tensor")
                def _(eng, st, c=c):
                    eng.matmul(prow[:, 0:512], onesf[:, :],
                               xv_[:, c],
                               start=(c == 0), stop=(c == NCH - 1))
                    return eng.matmul(prow[:, 512:1024], onesf[:, :],
                                      x2[:, :],
                                      start=(c == 0), stop=(c == NCH - 1))

            @sq.step(blk, "vector")
            def _(eng, st):
                eng.tensor_scalar(out=srow[:, 0:512], in0=prow[:, 0:512],
                                  scalar1=1.0 / C, scalar2=None, op0=ALU.mult)
                st.chain(eng.tensor_scalar(out=srow[:, 512:1024], in0=prow[:, 512:1024],
                                           scalar1=1.0 / C, scalar2=None, op0=ALU.mult))
                st.chain(eng.tensor_tensor(out=srow[:, 1024:1536], in0=srow[:, 0:512],
                                           in1=srow[:, 0:512], op=ALU.mult))
                return eng.tensor_tensor(out=srow[:, 512:1024], in0=srow[:, 512:1024],
                                         in1=srow[:, 1024:1536], op=ALU.subtract)

            @sq.step(blk, "scalar")
            def _(eng, st):
                return eng.activation(out=srow[:, 1024:1536], in_=srow[:, 512:1024],
                                      func=AF.Sqrt, bias=epsc[0:1, 0:1])

            @sq.step(blk, "vector")
            def _(eng, st):
                return eng.reciprocal(out=srow[:, 512:1024], in_=srow[:, 1024:1536])

            @sq.step(blk, "tensor")
            def _(eng, st):
                eng.matmul(ps0[:, :], onesr[:, :],
                           srow[:, 0:512], start=True, stop=True)
                return eng.matmul(ps1[:, :], onesr[:, :],
                                  srow[:, 512:1024], start=True, stop=True)

            @sq.step(blk, "vector")
            def _(eng, st):
                eng.tensor_copy(out=mb[:, :], in_=ps0[:, :])
                return eng.tensor_copy(out=rb[:, :], in_=ps1[:, :])

            for c in range(NCH):
                @sq.step(blk, "vector")
                def _(eng, st, c=c):
                    st.chain(eng.tensor_tensor(out=x2[:, :], in0=xv_[:, c], in1=mb[:, :],
                                               op=ALU.subtract))
                    return eng.tensor_tensor(out=xnv[:, c], in0=x2[:, :], in1=rb[:, :],
                                             op=ALU.mult)

            @sq.step(blk, "sync")
            def _(eng, st, tt=tt):
                for c in range(NCH):
                    st.dma(xnT[128 * c:128 * (c + 1), 512 * tt:512 * (tt + 1)],
                           xnv[:, c])

            def proj_step(name, oc, psb, c0=None, c1=None):
                c0 = 128 * oc if c0 is None else c0
                c1 = 128 * (oc + 1) if c1 is None else c1

                @sq.step(blk, "tensor")
                def _(eng, st):
                    last = None
                    for k in range(NCH):
                        last = eng.matmul(psb[0:c1 - c0, 0:512], W(name, k, c0, c1),
                                          xnv[:, k], start=(k == 0), stop=(k == NCH - 1))
                    return last

            def rope_steps(oc, dst_ap, dma_to=None, tt=tt):
                @sq.step(blk, "scalar")
                def _(eng, st):
                    return eng.activation(out=qsb[:, :], in_=ps0[:, :], func=AF.Copy)

                @sq.step(blk, "tensor")
                def _(eng, st):
                    return eng.matmul(ps1[:, :], rotT[:, :], qsb[:, :],
                                      start=True, stop=True)

                @sq.step(blk, "scalar")
                def _(eng, st):
                    return eng.activation(out=rosb[:, :], in_=ps1[:, :], func=AF.Copy)

                @sq.step(blk, "vector")
                def _(eng, st):
                    eng.tensor_tensor(out=t1[:, :], in0=qsb[:, :], in1=costab[:, :],
                                      op=ALU.mult)
                    st.chain(eng.tensor_tensor(out=x2[:, :], in0=rosb[:, :], in1=sintab[:, :],
                                               op=ALU.mult))
                    return eng.tensor_tensor(out=dst_ap, in0=t1[:, :], in1=x2[:, :],
                                             op=ALU.add)

                if dma_to is not None:
                    @sq.step(blk, "sync")
                    def _(eng, st):
                        st.dma(dma_to, dst_ap)

            for oc in range(NCH):
                proj_step("wq", oc, ps0)
                rope_steps(oc, qsb[:, :],
                           dma_to=qrT[128 * oc:128 * (oc + 1), 512 * tt:512 * (tt + 1)])

            for oc in range(2):
                proj_step("wk", oc, ps0)
                rope_steps(oc, ktv[:, oc])

            for oc in range(2):
                proj_step("wvv", oc, ps0)
                proj_step("wvg", oc, ps1)

                @sq.step(blk, "scalar")
                def _(eng, st, oc=oc):
                    return eng.activation(out=gsb[:, :], in_=ps1[:, :],
                                          func=AF.Sigmoid, bias=biasb[:, oc:oc + 1])

                @sq.step(blk, "vector")
                def _(eng, st, oc=oc):
                    return eng.tensor_tensor(out=ktv[:, 2 + oc], in0=ps0[:, :],
                                             in1=gsb[:, :], op=ALU.mult)

            proj_step("wki", 0, ps0, 0, 64)

            @sq.step(blk, "vector")
            def _(eng, st):
                st.chain(eng.memset(ktv[:, 4], 0.0))
                return eng.tensor_copy(out=ktv[:, 4][0:64, :], in_=ps0[0:64, 0:512])

            for oc in range(2):
                proj_step("wqi", oc, ps2)

                @sq.step(blk, "scalar")
                def _(eng, st):
                    return eng.activation(out=gsb[:, :], in_=ps2[:, :], func=AF.Copy)

                @sq.step(blk, "sync")
                def _(eng, st, oc=oc, tt=tt):
                    st.dma(qiT[128 * oc:128 * (oc + 1), 512 * tt:512 * (tt + 1)],
                           gsb[:, :])

            for oc in range(NCH):
                proj_step("wog", oc, ps2)

                @sq.step(blk, "scalar")
                def _(eng, st, oc=oc):
                    return eng.activation(out=gsb[:, :], in_=ps2[:, :],
                                          func=AF.Sigmoid, bias=biasb[:, 2 + oc:3 + oc])

                @sq.step(blk, "sync")
                def _(eng, st, oc=oc, tt=tt):
                    st.dma(g1T[128 * oc:128 * (oc + 1), 512 * tt:512 * (tt + 1)],
                           gsb[:, :])

            for qs in range(4):
                @sq.step(blk, "tensor")
                def _(eng, st, qs=qs):
                    last = None
                    for k in range(NCH):
                        last = eng.matmul(pqt[:, :], xnv[:, k][:, 128 * qs:128 * (qs + 1)],
                                          W("wwi", k, 0, 4), start=(k == 0), stop=(k == NCH - 1))
                    return last

                @sq.step(blk, "scalar")
                def _(eng, st):
                    return eng.activation(out=wisb[:, :], in_=pqt[:, :], func=AF.Sigmoid)

                @sq.step(blk, "sync")
                def _(eng, st, qs=qs, tt=tt):
                    st.dma(wI_tok[512 * tt + 128 * qs:512 * tt + 128 * (qs + 1), :],
                           wisb[:, :])

            for ts in range(4):
                @sq.step(blk, "tensor")
                def _(eng, st, ts=ts):
                    last = None
                    for fc in range(5):
                        last = eng.transpose(ptr[:, 128 * fc:128 * (fc + 1)],
                                             ktv[:, fc][:, 128 * ts:128 * (ts + 1)],
                                             identb[:, :])
                    return last

                @sq.step(blk, "vector")
                def _(eng, st, ts=ts):
                    for fc in range(4):
                        eng.tensor_copy(out=kvsb[:, 576 * ts + 128 * fc:576 * ts + 128 * (fc + 1)],
                                        in_=ptr[:, 128 * fc:128 * (fc + 1)])
                    return eng.tensor_copy(out=kvsb[:, 576 * ts + 512:576 * ts + 576],
                                           in_=ptr[:, 512:576])

            @sq.step(blk, "gpsimd")
            def _(eng, st, tt=tt):
                for ts in range(4):
                    st.dma(kv_loc[512 * tt + 128 * ts:512 * tt + 128 * (ts + 1), :],
                           kvsb[:, 576 * ts:576 * (ts + 1)])

        @sq.step(blk, "sync")
        def _(eng, st):
            return eng.nop()

    return nc





def build_B(n_batches=4, bisect_iters=34):
    nc = bass.Bass("TRN2", target_bir_lowering=False)
    qrT = nc.dram_tensor("qrT", [C, TPC], BF16, kind="ExternalInput").ap()
    qiT = nc.dram_tensor("qiT", [256, TPC], BF16, kind="ExternalInput").ap()
    g1T = nc.dram_tensor("g1T", [C, TPC], BF16, kind="ExternalInput").ap()
    xT = nc.dram_tensor("xT", [C, TPC], F32, kind="ExternalInput").ap()
    wI_tok = nc.dram_tensor("wI_tok", [TPC, 4], F32, kind="ExternalInput").ap()
    kv_all = nc.dram_tensor("kv_all", [B * L, 576], BF16, kind="ExternalInput").ap()
    bi_in = nc.dram_tensor("bi_in", [1, 4], F32, kind="ExternalInput").ap()
    wo = nc.dram_tensor("wo", [1024, 1024], F32, kind="ExternalInput").ap()
    xAT = nc.dram_tensor("xAT", [C, TPC], F32, kind="ExternalOutput").ap()

    from contextlib import ExitStack
    with ExitStack() as _ctx:
        wosb = _ctx.enter_context(nc.sbuf_tensor("wosb", [128, NCH * 1024], BF16))
        identb = _ctx.enter_context(nc.sbuf_tensor("identb", [128, 128], BF16))
        onesr = _ctx.enter_context(nc.sbuf_tensor("onesr", [1, 128], F32))
        bib = _ctx.enter_context(nc.sbuf_tensor("bib", [128, 4], F32))
        kt64 = _ctx.enter_context(nc.sbuf_tensor("kt64", [64, 4 * L], BF16))
        ki64 = _ctx.enter_context(nc.sbuf_tensor("ki64", [64, L], BF16))
        vtok = _ctx.enter_context(nc.sbuf_tensor("vtok", [128, 32 * 260], BF16))
        qr64 = _ctx.enter_context(nc.sbuf_tensor("qr64", [64, H * 512], BF16))
        qi64 = _ctx.enter_context(nc.sbuf_tensor("qi64", [64, 4 * 512], BF16))
        wib = _ctx.enter_context(nc.sbuf_tensor("wib", [128, 4 * 4], F32))
        ktok = _ctx.enter_context(nc.sbuf_tensor("ktok", [128, 384], BF16))
        mtsb = _ctx.enter_context(nc.sbuf_tensor("mtsb", [128, 32 * 512], BF16))
        isb = _ctx.enter_context(nc.sbuf_tensor("isb", [128, L], F32))
        isb2 = _ctx.enter_context(nc.sbuf_tensor("isb2", [128, L], F32))
        scr = _ctx.enter_context(nc.sbuf_tensor("scr", [128, L], BF16))
        scr2 = _ctx.enter_context(nc.sbuf_tensor("scr2", [128, L], BF16))
        bst = _ctx.enter_context(nc.sbuf_tensor("bst", [128, 16], F32))
        sx0 = _ctx.enter_context(nc.sbuf_tensor("sx0", [128, 512], F32))
        sx1 = _ctx.enter_context(nc.sbuf_tensor("sx1", [128, 512], F32))
        pt0 = _ctx.enter_context(nc.sbuf_tensor("pt0", [128, 512], BF16))
        pt1 = _ctx.enter_context(nc.sbuf_tensor("pt1", [128, 512], BF16))
        drw = _ctx.enter_context(nc.sbuf_tensor("drw", [1, 512], F32))
        drb = _ctx.enter_context(nc.sbuf_tensor("drb", [64, 512], F32))
        att = _ctx.enter_context(nc.sbuf_tensor("att", [128, NCH * 512], BF16))
        g1b = _ctx.enter_context(nc.sbuf_tensor("g1b", [128, NCH * 512], BF16))
        xbu = _ctx.enter_context(nc.sbuf_tensor("xbu", [128, 512], F32))
        xao = _ctx.enter_context(nc.sbuf_tensor("xao", [128, 512], F32))
        psS0 = _ctx.enter_context(nc.psum_tensor("psS0", [128, 512], F32))
        psS1 = _ctx.enter_context(nc.psum_tensor("psS1", [128, 512], F32))
        psS2 = _ctx.enter_context(nc.psum_tensor("psS2", [128, 512], F32))
        psS3 = _ctx.enter_context(nc.psum_tensor("psS3", [128, 512], F32))
        psO0 = _ctx.enter_context(nc.psum_tensor("psO0", [65, 512], F32))
        psO1 = _ctx.enter_context(nc.psum_tensor("psO1", [65, 512], F32))
        psB = _ctx.enter_context(nc.psum_tensor("psB", [64, 512], F32))
        ptr = _ctx.enter_context(nc.psum_tensor("ptr", [128, 512], BF16))
        blk = _ctx.enter_context(nc.Block())

        sq = Seq(nc)
        semA = nc.alloc_semaphore("semA")
        semM = nc.alloc_semaphore("semM")
        semE = nc.alloc_semaphore("semE")
        semV = nc.alloc_semaphore("semV")
        semBA = nc.alloc_semaphore("semBA")
        semBB = nc.alloc_semaphore("semBB")
        actr = {"A": 0, "M": 0, "E": 0, "V": 0, "BA": 0, "BB": 0}

        psS = (psS0, psS1)
        psSi = (psS0, psS1, psS2, psS3)
        psOs = (psO0, psO1)
        sxs = (sx0, sx1)
        pts = (pt0, pt1)
        vtv = vtok.ap().rearrange("p (c t) -> p c t", c=32)
        mtv = mtsb.ap().rearrange("p (c t) -> p c t", c=32)
        attv = att.ap().rearrange("p (c t) -> p c t", c=NCH)
        g1v = g1b.ap().rearrange("p (c t) -> p c t", c=NCH)
        wov = wosb.ap().rearrange("p (c t) -> p c t", c=NCH)
        ktg = kt64.ap().rearrange("p (g t) -> p g t", g=4)
        qrh = qr64.ap().rearrange("p (h t) -> p h t", h=H)
        qih = qi64.ap().rearrange("p (h t) -> p h t", h=4)

        @sq.step(blk, "gpsimd")
        def _(eng, st):
            for k in range(NCH):
                st.dma(wov[:, k], wo[128 * k:128 * (k + 1), :])

        @sq.step(blk, "gpsimd")
        def _(eng, st):
            return make_ident(nc, eng, identb[:, :], st)

        @sq.step(blk, "vector")
        def _(eng, st):
            eng.memset(onesr[:, :], 1.0)
            return eng.memset(
                vtok.ap().rearrange("p (c g o) -> p c g o", c=32, g=4)[:, :, :, 64:65], 1.0)

        @sq.step(blk, "sync")
        def _(eng, st):
            st.dma(drw[0:1, 0:4], bi_in[0:1, :])

        @sq.step(blk, "tensor")
        def _(eng, st):
            return eng.matmul(psS0[:, 0:4], onesr[:, :],
                              drw[0:1, 0:4], start=True, stop=True)

        @sq.step(blk, "vector")
        def _(eng, st):
            return eng.tensor_copy(out=bib[:, :], in_=psS0[:, 0:4])

        for b in range(n_batches):
            @sq.step(blk, "sync")
            def _(eng, st, b=b):
                for h in range(H):
                    st.dma(qrh[0:64, h, :],
                           qrT[64 * h:64 * (h + 1), 512 * b:512 * (b + 1)])
                for h in range(4):
                    st.dma(qih[0:64, h, :],
                           qiT[64 * h:64 * (h + 1), 512 * b:512 * (b + 1)])
                for qt in range(4):
                    st.dma(wib[:, 4 * qt:4 * (qt + 1)],
                           wI_tok[512 * b + 128 * qt:512 * b + 128 * (qt + 1), :])
                for c in range(NCH):
                    st.dma(g1v[:, c], g1T[128 * c:128 * (c + 1), 512 * b:512 * (b + 1)])

            @sq.step(blk, "gpsimd")
            def _(eng, st, b=b):
                for j in range(32):
                    r0 = L * b + 128 * j
                    for g in range(4):
                        st.dma(vtv[:, j][:, 65 * g:65 * g + 64],
                               kv_all[r0:r0 + 128, 256 + 64 * g:256 + 64 * (g + 1)])

            for j in range(32):
                @sq.step(blk, "gpsimd")
                def _(eng, st, j=j, b=b):
                    r0 = L * b + 128 * j
                    st.dma(ktok[:, 0:256], kv_all[r0:r0 + 128, 0:256])
                    st.dma(ktok[:, 256:320], kv_all[r0:r0 + 128, 512:576])

                @sq.step(blk, "tensor")
                def _(eng, st, j=j):
                    last = None
                    for fc in range(3):
                        last = eng.transpose(ptr[:, 128 * fc:128 * (fc + 1)],
                                             ktok[:, 128 * fc:128 * (fc + 1)],
                                             identb[:, :])
                    return last

                @sq.step(blk, "vector")
                def _(eng, st, j=j):
                    sl = slice(128 * j, 128 * (j + 1))
                    eng.tensor_copy(out=ktg[0:64, 0, sl], in_=ptr[0:64, 0:128])
                    st.chain(eng.tensor_copy(out=ktg[0:64, 1, sl], in_=ptr[64:128, 0:128]))
                    eng.tensor_copy(out=ktg[0:64, 2, sl], in_=ptr[0:64, 128:256])
                    st.chain(eng.tensor_copy(out=ktg[0:64, 3, sl], in_=ptr[64:128, 128:256]))
                    return eng.tensor_copy(out=ki64[0:64, sl], in_=ptr[0:64, 256:384])

            for qp in range(2):
                qa, qb = 2 * qp, 2 * qp + 1
                for qt, dst in ((qa, isb), (qb, isb2)):
                    for h in range(H_I):
                        for kcg in range(2):
                            @sq.step(blk, "tensor")
                            def _(eng, st, h=h, kcg=kcg, qt=qt):
                                last = None
                                for u in range(4):
                                    kc = 4 * kcg + u
                                    last = eng.matmul(
                                        psSi[u][:, :],
                                        qih[0:64, h, 128 * qt:128 * (qt + 1)],
                                        ki64[0:64, 512 * kc:512 * (kc + 1)],
                                        start=True, stop=True)
                                return last

                            @sq.step(blk, "scalar")
                            def _(eng, st, h=h, kcg=kcg):
                                last = None
                                for u in range(4):
                                    kc = 4 * kcg + u
                                    last = eng.activation(
                                        out=scr[:, 512 * kc:512 * (kc + 1)],
                                        in_=psSi[u][:, :], func=AF.Sigmoid,
                                        bias=bib[:, h:h + 1])
                                return last

                        @sq.step(blk, "vector")
                        def _(eng, st, h=h, qt=qt, dst=dst):
                            if h == 0:
                                return eng.tensor_scalar(
                                    out=dst[:, :], in0=scr[:, :],
                                    scalar1=wib[:, 4 * qt + h:4 * qt + h + 1],
                                    scalar2=None, op0=ALU.mult)
                            return eng.scalar_tensor_tensor(
                                out=dst[:, :], in0=scr[:, :],
                                scalar=wib[:, 4 * qt + h:4 * qt + h + 1],
                                in1=dst[:, :], op0=ALU.mult, op1=ALU.add)

                @sq.step(blk, "vector")
                def _(eng, st):
                    eng.memset(bst[:, 0:1], 0.0)
                    eng.memset(bst[:, 1:2], 4.0)
                    eng.memset(bst[:, 8:9], 0.0)
                    return eng.memset(bst[:, 9:10], 4.0)

                entry = [(sq.sems[e], sq.cnt[e]) for e in sq.cnt if sq.cnt[e] > 0]
                a_ba, a_bb = actr["BA"], actr["BB"]

                def v_body(eng, entry=entry, a_ba=a_ba, a_bb=a_bb):
                    for s, v in entry:
                        eng.wait_ge(s, v)
                    ch = [0]

                    def chain(inst):
                        inst.then_inc(semBB, 1)
                        ch[0] += 1
                        eng.wait_ge(semBB, a_bb + ch[0])
                        return inst

                    for it in range(bisect_iters):
                        chain(eng.tensor_tensor(out=bst[:, 14:15], in0=bst[:, 8:9],
                                                in1=bst[:, 9:10], op=ALU.add))
                        chain(eng.tensor_scalar(out=bst[:, 10:11], in0=bst[:, 14:15],
                                                scalar1=0.5, scalar2=None, op0=ALU.mult))
                        chain(eng.tensor_scalar(out=bst[:, 11:12], in0=bst[:, 14:15],
                                                scalar1=-0.5, scalar2=None, op0=ALU.mult))
                        eng.sem_inc(semBB, 1)
                        ch[0] += 1
                        chain(eng.tensor_scalar(out=bst[:, 2:3], in0=bst[:, 1:2],
                                                scalar1=bst[:, 0:1], scalar2=0.5,
                                                op0=ALU.add, op1=ALU.mult))
                        chain(eng.tensor_scalar(out=scr[:, :], in0=isb[:, :],
                                                scalar1=bst[:, 2:3], scalar2=0.0,
                                                op0=ALU.is_gt, op1=ALU.add,
                                                accum_out=bst[:, 3:4]))
                        chain(eng.tensor_scalar(out=bst[:, 4:5], in0=bst[:, 3:4],
                                                scalar1=float(K_SEL), scalar2=None,
                                                op0=ALU.is_ge))
                        chain(eng.tensor_tensor(out=bst[:, 5:6], in0=bst[:, 2:3],
                                                in1=bst[:, 0:1], op=ALU.subtract))
                        chain(eng.scalar_tensor_tensor(out=bst[:, 0:1], in0=bst[:, 5:6],
                                                       scalar=bst[:, 4:5], in1=bst[:, 0:1],
                                                       op0=ALU.mult, op1=ALU.add))
                        chain(eng.tensor_tensor(out=bst[:, 5:6], in0=bst[:, 1:2],
                                                in1=bst[:, 2:3], op=ALU.subtract))
                        chain(eng.scalar_tensor_tensor(out=bst[:, 1:2], in0=bst[:, 5:6],
                                                       scalar=bst[:, 4:5], in1=bst[:, 2:3],
                                                       op0=ALU.mult, op1=ALU.add))
                        eng.wait_ge(semBA, a_ba + it + 1)
                        chain(eng.tensor_scalar(out=bst[:, 13:14], in0=bst[:, 12:13],
                                                scalar1=-3584.0, scalar2=None,
                                                op0=ALU.is_ge))
                        chain(eng.tensor_tensor(out=bst[:, 14:15], in0=bst[:, 10:11],
                                                in1=bst[:, 8:9], op=ALU.subtract))
                        chain(eng.scalar_tensor_tensor(out=bst[:, 8:9], in0=bst[:, 14:15],
                                                       scalar=bst[:, 13:14], in1=bst[:, 8:9],
                                                       op0=ALU.mult, op1=ALU.add))
                        chain(eng.tensor_tensor(out=bst[:, 14:15], in0=bst[:, 9:10],
                                                in1=bst[:, 10:11], op=ALU.subtract))
                        chain(eng.scalar_tensor_tensor(out=bst[:, 9:10], in0=bst[:, 14:15],
                                                       scalar=bst[:, 13:14], in1=bst[:, 10:11],
                                                       op0=ALU.mult, op1=ALU.add))
                    chain(eng.tensor_scalar(out=scr[:, :], in0=isb[:, :],
                                            scalar1=bst[:, 0:1], scalar2=NEG_BIG,
                                            op0=ALU.is_le, op1=ALU.mult))
                    chain(eng.tensor_scalar(out=scr2[:, :], in0=isb2[:, :],
                                            scalar1=bst[:, 8:9], scalar2=NEG_BIG,
                                            op0=ALU.is_le, op1=ALU.mult))
                    actr["BB"] = a_bb + ch[0]

                def a_body(eng, entry=entry, a_ba=a_ba, a_bb=a_bb):
                    for s, v in entry:
                        eng.wait_ge(s, v)
                    for it in range(bisect_iters):
                        eng.wait_ge(semBB, a_bb + 16 * it + 4)
                        eng.activation(out=scr2[:, :], in_=isb2[:, :], func=AF.Sign,
                                       bias=bst[:, 11:12],
                                       accum_out=bst[:, 12:13]).then_inc(semBA, 1)
                    actr["BA"] = a_ba + bisect_iters

                blk.vector(v_body)
                blk.scalar(a_body)
                sq.extra_waits = [(semBB, actr["BB"]), (semBA, actr["BA"])]

                for qt, msrc in ((qa, scr), (qb, scr2)):
                    for jg in range(8):
                        @sq.step(blk, "tensor")
                        def _(eng, st, jg=jg, msrc=msrc):
                            last = None
                            for u in range(4):
                                kc = 4 * jg + u
                                last = eng.transpose(ptr[:, 128 * u:128 * (u + 1)],
                                                     msrc[:, 128 * kc:128 * (kc + 1)],
                                                     identb[:, :])
                            return last

                        @sq.step(blk, "vector")
                        def _(eng, st, jg=jg, qt=qt):
                            last = None
                            for u in range(4):
                                kc = 4 * jg + u
                                last = eng.tensor_copy(
                                    out=mtv[:, kc][:, 128 * qt:128 * (qt + 1)],
                                    in_=ptr[:, 128 * u:128 * (u + 1)])
                            return last

            entry = [(sq.sems[e], sq.cnt[e]) for e in sq.cnt if sq.cnt[e] > 0]
            aA, aM, aE, aV = actr["A"], actr["M"], actr["E"], actr["V"]

            def t_body(eng, entry=entry, aA=aA, aM=aM, aE=aE, aV=aV):
                for s, v in entry:
                    eng.wait_ge(s, v)
                for h in range(H):
                    g = h // 4
                    po = psOs[h % 2]
                    for kc in range(32):
                        if not (h == 0 and kc < 2):
                            wv = (aM + 35 * h + kc - 1) if kc >= 2 else \
                                (aM + 35 * (h - 1) + 31 + kc)
                            eng.wait_ge(semM, wv)
                        eng.matmul(psS[kc % 2][:, :],
                                   ktg[0:64, g, 128 * kc:128 * (kc + 1)],
                                   qrh[0:64, h, :],
                                   start=True, stop=True).then_inc(semA, 1)
                        if kc >= 1:
                            eng.wait_ge(semE, aE + 32 * h + kc)
                            if kc == 1 and h >= 2:
                                eng.wait_ge(semM, aM + 35 * (h - 1))
                            eng.matmul(po[0:65, :],
                                       vtv[:, kc - 1][:, 65 * g:65 * (g + 1)],
                                       pts[(kc - 1) % 2][:, :],
                                       start=(kc == 1), stop=False).then_inc(semV, 1)
                    eng.wait_ge(semE, aE + 32 * h + 32)
                    eng.matmul(po[0:65, :], vtv[:, 31][:, 65 * g:65 * (g + 1)],
                               pts[31 % 2][:, :],
                               start=False, stop=True).then_inc(semV, 1)
                    eng.wait_ge(semM, aM + 35 * h + 33)
                    eng.matmul(psB[0:64, :], onesr[0:1, 0:64], drw[0:1, :],
                               start=True, stop=True).then_inc(semV, 1)
                actr["A"] = aA + 32 * H
                actr["V"] = aV + 33 * H

            def v_body2(eng, entry=entry, aA=aA, aM=aM, aE=aE, aV=aV):
                for s, v in entry:
                    eng.wait_ge(s, v)
                n = [0]

                def inc(inst):
                    inst.then_inc(semM, 1)
                    n[0] += 1
                    return inst

                for h in range(H):
                    po = psOs[h % 2]
                    for kc in range(32):
                        eng.wait_ge(semA, aA + 32 * h + kc + 1)
                        if kc >= 2:
                            eng.wait_ge(semE, aE + 32 * h + kc - 1)
                        inc(eng.scalar_tensor_tensor(
                            out=sxs[kc % 2][:, :], in0=psS[kc % 2][:, :],
                            scalar=SM_SCALE, in1=mtv[:, kc],
                            op0=ALU.mult, op1=ALU.add))
                    eng.wait_ge(semV, aV + 33 * h + 32)
                    inc(eng.reciprocal(out=drw[0:1, :], in_=po[64:65, :]))
                    eng.wait_ge(semV, aV + 33 * h + 33)
                    inc(eng.tensor_copy(out=drb[0:64, :], in_=psB[0:64, :]))
                    eng.wait_ge(semM, aM + n[0])
                    dst = attv[:, h // 2][64 * (h % 2):64 * (h % 2) + 64, :]
                    inc(eng.tensor_tensor(out=dst, in0=po[0:64, :],
                                          in1=drb[0:64, :], op=ALU.mult))
                actr["M"] = aM + n[0]

            def a_body2(eng, entry=entry, aA=aA, aM=aM, aE=aE, aV=aV):
                for s, v in entry:
                    eng.wait_ge(s, v)
                for h in range(H):
                    for kc in range(32):
                        eng.wait_ge(semM, aM + 35 * h + kc + 1)
                        if kc >= 2:
                            eng.wait_ge(semV, aV + 33 * h + kc - 1)
                        eng.activation(out=pts[kc % 2][:, :], in_=sxs[kc % 2][:, :],
                                       func=AF.Exp).then_inc(semE, 1)
                actr["E"] = aE + 32 * H

            blk.tensor(t_body)
            blk.vector(v_body2)
            blk.scalar(a_body2)
            sq.extra_waits = [(semA, actr["A"]), (semM, actr["M"]),
                              (semE, actr["E"]), (semV, actr["V"])]

            for c in range(NCH):
                @sq.step(blk, "vector")
                def _(eng, st, c=c):
                    return eng.tensor_tensor(out=attv[:, c], in0=attv[:, c],
                                             in1=g1v[:, c], op=ALU.mult)

            for oc in range(NCH):
                @sq.step(blk, "tensor")
                def _(eng, st, oc=oc):
                    last = None
                    for k in range(NCH):
                        last = eng.matmul(psS0[:, :],
                                          wov[:, k][:, 128 * oc:128 * (oc + 1)],
                                          attv[:, k], start=(k == 0), stop=(k == NCH - 1))
                    return last

                @sq.step(blk, "sync")
                def _(eng, st, oc=oc, b=b):
                    st.dma(xbu[:, :], xT[128 * oc:128 * (oc + 1), 512 * b:512 * (b + 1)])

                @sq.step(blk, "vector")
                def _(eng, st):
                    return eng.tensor_tensor(out=xao[:, :], in0=psS0[:, :],
                                             in1=xbu[:, :], op=ALU.add)

                @sq.step(blk, "sync")
                def _(eng, st, oc=oc, b=b):
                    st.dma(xAT[128 * oc:128 * (oc + 1), 512 * b:512 * (b + 1)],
                           xao[:, :])

        @sq.step(blk, "sync")
        def _(eng, st):
            return eng.nop()

    return nc



def build_C(n_experts=16):
    nc = bass.Bass("TRN2", target_bir_lowering=False)
    xnT = nc.dram_tensor("xnT", [C, TPC], BF16, kind="ExternalInput").ap()
    xAT = nc.dram_tensor("xAT", [C, TPC], F32, kind="ExternalInput").ap()
    gwT = nc.dram_tensor("gwT", [C, 16], F32, kind="ExternalInput").ap()
    ebias = nc.dram_tensor("ebias", [1, 16], F32, kind="ExternalInput").ap()
    wgT = nc.dram_tensor("wgT", [16, C, 1024], F32, kind="ExternalInput").ap()
    wuT = nc.dram_tensor("wuT", [16, C, 1024], F32, kind="ExternalInput").ap()
    wdT = nc.dram_tensor("wdT", [16, 1024, C], F32, kind="ExternalInput").ap()
    sgT = nc.dram_tensor("sgT", [C, 1024], F32, kind="ExternalInput").ap()
    suT = nc.dram_tensor("suT", [C, 1024], F32, kind="ExternalInput").ap()
    sdT = nc.dram_tensor("sdT", [1024, C], F32, kind="ExternalInput").ap()
    x1T = nc.dram_tensor("x1T", [C, TPC], F32, kind="ExternalOutput").ap()
    xn1T = nc.dram_tensor("xn1T", [C, TPC], BF16, kind="ExternalOutput").ap()
    xn1F = nc.dram_tensor("xn1F", [C, TPC], F32, kind="ExternalOutput").ap()
    xmax = nc.dram_tensor("xmax", [1, 1], F32, kind="ExternalOutput").ap()

    from contextlib import ExitStack
    with ExitStack() as _ctx:
        xnsb = _ctx.enter_context(nc.sbuf_tensor("xnsb", [128, NCH * TPC], BF16))
        ysb = _ctx.enter_context(nc.sbuf_tensor("ysb", [128, NCH * TPC], F32))
        wge = _ctx.enter_context(nc.sbuf_tensor("wge", [128, NCH * 1024], BF16))
        wue = _ctx.enter_context(nc.sbuf_tensor("wue", [128, NCH * 1024], BF16))
        wde = _ctx.enter_context(nc.sbuf_tensor("wde", [128, NCH * 1024], BF16))
        gwsb = _ctx.enter_context(nc.sbuf_tensor("gwsb", [128, NCH * 16], BF16))
        ebb = _ctx.enter_context(nc.sbuf_tensor("ebb", [128, 16], F32))
        cmb = _ctx.enter_context(nc.sbuf_tensor("cmb", [128, 16 * 16], F32))
        ctr_ = _ctx.enter_context(nc.sbuf_tensor("ctr_", [16, TPC], F32))
        crow = _ctx.enter_context(nc.sbuf_tensor("crow", [1, TPC], F32))
        cbf = _ctx.enter_context(nc.sbuf_tensor("cbf", [128, 512], F32))
        hsl = _ctx.enter_context(nc.sbuf_tensor("hsl", [128, 512], BF16))
        hsb = _ctx.enter_context(nc.sbuf_tensor("hsb", [128, NCH * 512], BF16))
        r16 = _ctx.enter_context(nc.sbuf_tensor("r16", [128, 16], F32))
        r16b = _ctx.enter_context(nc.sbuf_tensor("r16b", [128, 16], F32))
        r16c = _ctx.enter_context(nc.sbuf_tensor("r16c", [128, 16], F32))
        r8 = _ctx.enter_context(nc.sbuf_tensor("r8", [128, 8], F32))
        rcol = _ctx.enter_context(nc.sbuf_tensor("rcol", [128, 2], F32))
        onesr = _ctx.enter_context(nc.sbuf_tensor("onesr", [1, 128], F32))
        onesf = _ctx.enter_context(nc.sbuf_tensor("onesf", [128, 1], F32))
        identf = _ctx.enter_context(nc.sbuf_tensor("identf", [128, 128], F32))
        epsc = _ctx.enter_context(nc.sbuf_tensor("epsc", [1, 1], F32))
        xbu = _ctx.enter_context(nc.sbuf_tensor("xbu", [128, 512], F32))
        x1b = _ctx.enter_context(nc.sbuf_tensor("x1b", [128, NCH * 512], F32))
        x2 = _ctx.enter_context(nc.sbuf_tensor("x2", [128, 512], F32))
        srow = _ctx.enter_context(nc.sbuf_tensor("srow", [1, 3 * 512], F32))
        mb = _ctx.enter_context(nc.sbuf_tensor("mb", [128, 512], F32))
        rb = _ctx.enter_context(nc.sbuf_tensor("rb", [128, 512], F32))
        mxc = _ctx.enter_context(nc.sbuf_tensor("mxc", [128, 2], F32))
        mrow = _ctx.enter_context(nc.sbuf_tensor("mrow", [1, 128], F32))
        psG = _ctx.enter_context(nc.psum_tensor("psG", [128, 512], F32))
        psU = _ctx.enter_context(nc.psum_tensor("psU", [128, 512], F32))
        psD = _ctx.enter_context(nc.psum_tensor("psD", [128, 512], F32))
        psR = _ctx.enter_context(nc.psum_tensor("psR", [128, 16], F32))
        psB = _ctx.enter_context(nc.psum_tensor("psB", [128, 512], F32))
        psT = _ctx.enter_context(nc.psum_tensor("psT", [128, 128], F32))
        prow = _ctx.enter_context(nc.psum_tensor("prow", [1, 1024], F32))
        blk = _ctx.enter_context(nc.Block())

        sq = Seq(nc)
        xnv = xnsb.ap().rearrange("p (c t) -> p c t", c=NCH)
        yv = ysb.ap().rearrange("p (c t) -> p c t", c=NCH)
        x1v = x1b.ap().rearrange("p (c t) -> p c t", c=NCH)
        wgv = wge.ap().rearrange("p (c t) -> p c t", c=NCH)
        wuv = wue.ap().rearrange("p (c t) -> p c t", c=NCH)
        wdv = wde.ap().rearrange("p (c t) -> p c t", c=NCH)
        gwv = gwsb.ap().rearrange("p (c t) -> p c t", c=NCH)
        hsv = hsb.ap().rearrange("p (c t) -> p c t", c=NCH)

        @sq.step(blk, "gpsimd")
        def _(eng, st):
            for c in range(NCH):
                st.dma(xnv[:, c], xnT[128 * c:128 * (c + 1), :])
                st.dma(gwv[:, c], gwT[128 * c:128 * (c + 1), :])
            st.dma(srow[0:1, 0:16], ebias[0:1, :])

        @sq.step(blk, "gpsimd")
        def _(eng, st):
            return make_ident(nc, eng, identf[:, :], st)

        @sq.step(blk, "vector")
        def _(eng, st):
            eng.memset(onesr[:, :], 1.0)
            eng.memset(onesf[:, :], 1.0)
            return eng.memset(epsc[:, :], LN_EPS)

        @sq.step(blk, "tensor")
        def _(eng, st):
            return eng.matmul(psR[:, :], onesr[:, :], srow[0:1, 0:16],
                              start=True, stop=True)

        @sq.step(blk, "vector")
        def _(eng, st):
            return eng.tensor_copy(out=ebb[:, :], in_=psR[:, :])

        for qt in range(16):
            @sq.step(blk, "tensor")
            def _(eng, st, qt=qt):
                last = None
                for k in range(NCH):
                    last = eng.matmul(psR[:, :],
                                      xnv[:, k][:, 128 * qt:128 * (qt + 1)],
                                      gwv[:, k], start=(k == 0), stop=(k == NCH - 1))
                return last

            @sq.step(blk, "scalar")
            def _(eng, st):
                return eng.activation(out=r16[:, :], in_=psR[:, :], func=AF.Sigmoid)

            @sq.step(blk, "vector")
            def _(eng, st, qt=qt):
                st.chain(eng.tensor_tensor(out=r16b[:, :], in0=r16[:, :],
                                           in1=ebb[:, :], op=ALU.add))
                st.chain(eng.max(out=r8[:, :], in_=r16b[:, :]))
                st.chain(eng.match_replace(out=r16c[:, :], in_to_replace=r8[:, :],
                                           in_values=r16b[:, :], imm_value=-1e30))
                st.chain(eng.tensor_scalar(out=r16c[:, :], in0=r16c[:, :],
                                           scalar1=-1e29, scalar2=None, op0=ALU.is_lt))
                st.chain(eng.tensor_tensor(out=r16c[:, :], in0=r16c[:, :],
                                           in1=r16[:, :], op=ALU.mult))
                st.chain(eng.reduce_sum(out=rcol[:, 0:1], in_=r16c[:, :],
                                        axis=mybir.AxisListType.X))
                st.chain(eng.reciprocal(out=rcol[:, 1:2], in_=rcol[:, 0:1]))
                return eng.tensor_scalar(out=cmb[:, 16 * qt:16 * (qt + 1)],
                                         in0=r16c[:, :], scalar1=rcol[:, 1:2],
                                         scalar2=2.826, op0=ALU.mult, op1=ALU.mult)

            @sq.step(blk, "tensor")
            def _(eng, st, qt=qt):
                return eng.transpose(psT[0:16, :].bitcast(F32),
                                     cmb[:, 16 * qt:16 * (qt + 1)], identf[:, :])

            @sq.step(blk, "vector")
            def _(eng, st, qt=qt):
                return eng.tensor_copy(out=ctr_[0:16, 128 * qt:128 * (qt + 1)],
                                       in_=psT[0:16, 0:128].bitcast(F32))

        for e in range(n_experts + 1):
            shared = (e == n_experts)

            @sq.step(blk, "gpsimd")
            def _(eng, st, e=e, shared=shared):
                for c in range(NCH):
                    if shared:
                        st.dma(wgv[:, c], sgT[128 * c:128 * (c + 1), :])
                        st.dma(wuv[:, c], suT[128 * c:128 * (c + 1), :])
                        st.dma(wdv[:, c], sdT[128 * c:128 * (c + 1), :])
                    else:
                        st.dma(wgv[:, c], wgT[e, 128 * c:128 * (c + 1), :])
                        st.dma(wuv[:, c], wuT[e, 128 * c:128 * (c + 1), :])
                        st.dma(wdv[:, c], wdT[e, 128 * c:128 * (c + 1), :])

            if not shared:
                @sq.step(blk, "sync")
                def _(eng, st, e=e):
                    st.dma(crow[0:1, :], ctr_[e:e + 1, :])

            for tt in range(NTT):
                if not shared:
                    @sq.step(blk, "tensor")
                    def _(eng, st, e=e, tt=tt):
                        return eng.matmul(psB[:, :], onesr[:, :],
                                          crow[0:1, 512 * tt:512 * (tt + 1)],
                                          start=True, stop=True)

                    @sq.step(blk, "vector")
                    def _(eng, st):
                        return eng.tensor_copy(out=cbf[:, :], in_=psB[:, :])

                for ic in range(NCH):
                    @sq.step(blk, "tensor")
                    def _(eng, st, ic=ic, tt=tt):
                        for k in range(NCH):
                            eng.matmul(psG[:, :],
                                       wgv[:, k][:, 128 * ic:128 * (ic + 1)],
                                       xnv[:, k][:, 512 * tt:512 * (tt + 1)],
                                       start=(k == 0), stop=(k == NCH - 1))
                        last = None
                        for k in range(NCH):
                            last = eng.matmul(psU[:, :],
                                              wuv[:, k][:, 128 * ic:128 * (ic + 1)],
                                              xnv[:, k][:, 512 * tt:512 * (tt + 1)],
                                              start=(k == 0), stop=(k == NCH - 1))
                        return last

                    @sq.step(blk, "scalar")
                    def _(eng, st):
                        return eng.activation(out=hsl[:, :], in_=psG[:, :], func=AF.Silu)

                    @sq.step(blk, "vector")
                    def _(eng, st, ic=ic, shared=shared):
                        if shared:
                            return eng.tensor_tensor(out=hsv[:, ic], in0=hsl[:, :],
                                                     in1=psU[:, :], op=ALU.mult)
                        st.chain(eng.tensor_tensor(out=hsv[:, ic], in0=hsl[:, :],
                                                   in1=psU[:, :], op=ALU.mult))
                        return eng.tensor_tensor(out=hsv[:, ic], in0=hsv[:, ic],
                                                 in1=cbf[:, :], op=ALU.mult)

                for dc in range(NCH):
                    @sq.step(blk, "tensor")
                    def _(eng, st, dc=dc, tt=tt):
                        last = None
                        for k in range(NCH):
                            last = eng.matmul(psD[:, :],
                                              wdv[:, k][:, 128 * dc:128 * (dc + 1)],
                                              hsv[:, k], start=(k == 0),
                                              stop=(k == NCH - 1))
                        return last

                    @sq.step(blk, "vector")
                    def _(eng, st, e=e, dc=dc, tt=tt):
                        dst = yv[:, dc][:, 512 * tt:512 * (tt + 1)]
                        if e == 0:
                            return eng.tensor_copy(out=dst, in_=psD[:, :])
                        return eng.tensor_tensor(out=dst, in0=dst, in1=psD[:, :],
                                                 op=ALU.add)

        @sq.step(blk, "vector")
        def _(eng, st):
            return eng.memset(mxc[:, 0:1], 0.0)

        for tt in range(NTT):
            @sq.step(blk, "sync")
            def _(eng, st, tt=tt):
                for c in range(NCH):
                    st.dma(x1v[:, c],
                           xAT[128 * c:128 * (c + 1), 512 * tt:512 * (tt + 1)])

            for c in range(NCH):
                @sq.step(blk, "vector")
                def _(eng, st, c=c, tt=tt):
                    return eng.tensor_tensor(out=x1v[:, c], in0=x1v[:, c],
                                             in1=yv[:, c][:, 512 * tt:512 * (tt + 1)],
                                             op=ALU.add)

            @sq.step(blk, "sync")
            def _(eng, st, tt=tt):
                for c in range(NCH):
                    st.dma(x1T[128 * c:128 * (c + 1), 512 * tt:512 * (tt + 1)],
                           x1v[:, c])

            for c in range(NCH):
                @sq.step(blk, "vector")
                def _(eng, st, c=c):
                    return eng.tensor_tensor(out=x2[:, :], in0=x1v[:, c],
                                             in1=x1v[:, c], op=ALU.mult)

                @sq.step(blk, "tensor")
                def _(eng, st, c=c):
                    eng.matmul(prow[:, 0:512], onesf[:, :], x1v[:, c],
                               start=(c == 0), stop=(c == NCH - 1))
                    return eng.matmul(prow[:, 512:1024], onesf[:, :], x2[:, :],
                                      start=(c == 0), stop=(c == NCH - 1))

            @sq.step(blk, "vector")
            def _(eng, st):
                eng.tensor_scalar(out=srow[:, 0:512], in0=prow[:, 0:512],
                                  scalar1=1.0 / C, scalar2=None, op0=ALU.mult)
                st.chain(eng.tensor_scalar(out=srow[:, 512:1024], in0=prow[:, 512:1024],
                                           scalar1=1.0 / C, scalar2=None, op0=ALU.mult))
                st.chain(eng.tensor_tensor(out=srow[:, 1024:1536], in0=srow[:, 0:512],
                                           in1=srow[:, 0:512], op=ALU.mult))
                return eng.tensor_tensor(out=srow[:, 512:1024], in0=srow[:, 512:1024],
                                         in1=srow[:, 1024:1536], op=ALU.subtract)

            @sq.step(blk, "scalar")
            def _(eng, st):
                return eng.activation(out=srow[:, 1024:1536], in_=srow[:, 512:1024],
                                      func=AF.Sqrt, bias=epsc[0:1, 0:1])

            @sq.step(blk, "vector")
            def _(eng, st):
                return eng.reciprocal(out=srow[:, 512:1024], in_=srow[:, 1024:1536])

            @sq.step(blk, "tensor")
            def _(eng, st):
                eng.matmul(psG[:, :], onesr[:, :], srow[:, 0:512],
                           start=True, stop=True)
                return eng.matmul(psU[:, :], onesr[:, :], srow[:, 512:1024],
                                  start=True, stop=True)

            @sq.step(blk, "vector")
            def _(eng, st):
                eng.tensor_copy(out=mb[:, :], in_=psG[:, :])
                return eng.tensor_copy(out=rb[:, :], in_=psU[:, :])

            for c in range(NCH):
                @sq.step(blk, "vector")
                def _(eng, st, c=c):
                    st.chain(eng.tensor_tensor(out=x2[:, :], in0=x1v[:, c],
                                               in1=mb[:, :], op=ALU.subtract))
                    st.chain(eng.tensor_tensor(out=x2[:, :], in0=x2[:, :],
                                               in1=rb[:, :], op=ALU.mult))
                    st.chain(eng.tensor_copy(out=hsl[:, :], in_=x2[:, :]))
                    st.chain(eng.reduce_max(out=mxc[:, 1:2], in_=x2[:, :],
                                            axis=mybir.AxisListType.X,
                                            apply_absolute_value=True))
                    return eng.tensor_tensor(out=mxc[:, 0:1], in0=mxc[:, 0:1],
                                             in1=mxc[:, 1:2], op=ALU.max)

                @sq.step(blk, "sync")
                def _(eng, st, c=c, tt=tt):
                    st.dma(xn1T[128 * c:128 * (c + 1), 512 * tt:512 * (tt + 1)],
                           hsl[:, :])
                    st.dma(xn1F[128 * c:128 * (c + 1), 512 * tt:512 * (tt + 1)],
                           x2[:, :])

        @sq.step(blk, "tensor")
        def _(eng, st):
            return eng.transpose(psT[0:1, 0:128].bitcast(F32), mxc[:, 0:1],
                                 identf[:, :])

        @sq.step(blk, "vector")
        def _(eng, st):
            st.chain(eng.tensor_copy(out=mrow[0:1, :], in_=psT[0:1, 0:128].bitcast(F32)))
            return eng.reduce_max(out=srow[0:1, 0:1], in_=mrow[0:1, :],
                                  axis=mybir.AxisListType.X)

        @sq.step(blk, "sync")
        def _(eng, st):
            st.dma(xmax[0:1, 0:1], srow[0:1, 0:1])

        @sq.step(blk, "sync")
        def _(eng, st):
            return eng.nop()

    return nc




def build_D():
    nc = bass.Bass("TRN2", target_bir_lowering=False)
    xn1T = nc.dram_tensor("xn1T", [C, TPC], BF16, kind="ExternalInput").ap()
    x1T = nc.dram_tensor("x1T", [C, TPC], F32, kind="ExternalInput").ap()
    wkv = nc.dram_tensor("wkv", [C, 2048], F32, kind="ExternalInput").ap()
    bkv = nc.dram_tensor("bkv", [1, 2048], F32, kind="ExternalInput").ap()
    wq1 = nc.dram_tensor("wq1", [C, 1024], F32, kind="ExternalInput").ap()
    bq1 = nc.dram_tensor("bq1", [1, 1024], F32, kind="ExternalInput").ap()
    wp1 = nc.dram_tensor("wp1", [C, 1024], F32, kind="ExternalInput").ap()
    bp1 = nc.dram_tensor("bp1", [1, 1024], F32, kind="ExternalInput").ap()
    xBT = nc.dram_tensor("xBT", [C, TPC], F32, kind="ExternalOutput").ap()

    from contextlib import ExitStack
    with ExitStack() as _ctx:
        xnsb = _ctx.enter_context(nc.sbuf_tensor("xnsb", [128, NCH * TPC], BF16))
        wkvb = _ctx.enter_context(nc.sbuf_tensor("wkvb", [128, NCH * 2048], BF16))
        wqb = _ctx.enter_context(nc.sbuf_tensor("wqb", [128, NCH * 1024], BF16))
        wpb = _ctx.enter_context(nc.sbuf_tensor("wpb", [128, NCH * 1024], BF16))
        bkvb = _ctx.enter_context(nc.sbuf_tensor("bkvb", [128, 2048], F32))
        bqb = _ctx.enter_context(nc.sbuf_tensor("bqb", [128, 1024], F32))
        bpc = _ctx.enter_context(nc.sbuf_tensor("bpc", [128, NCH], F32))
        brow = _ctx.enter_context(nc.sbuf_tensor("brow", [1, 2048], F32))
        onesr = _ctx.enter_context(nc.sbuf_tensor("onesr", [1, 128], F32))
        identb = _ctx.enter_context(nc.sbuf_tensor("identb", [128, 128], BF16))
        qtok = _ctx.enter_context(nc.sbuf_tensor("qtok", [128, 4 * 1024], BF16))
        ktok = _ctx.enter_context(nc.sbuf_tensor("ktok2", [128, 4 * 1024], BF16))
        vtok = _ctx.enter_context(nc.sbuf_tensor("vtok2", [128, 4 * 1024], BF16))
        atile = _ctx.enter_context(nc.sbuf_tensor("atile", [128, 4 * 16 * 4], F32))
        amx = _ctx.enter_context(nc.sbuf_tensor("amx", [128, 4 * 16], F32))
        tmp1 = _ctx.enter_context(nc.sbuf_tensor("tmp1", [128, 1024], F32))
        avt = _ctx.enter_context(nc.sbuf_tensor("avt", [128, 4 * 1024], BF16))
        avT = _ctx.enter_context(nc.sbuf_tensor("avT", [128, NCH * TPC], BF16))
        xbu = _ctx.enter_context(nc.sbuf_tensor("xbu", [128, 512], F32))
        xao = _ctx.enter_context(nc.sbuf_tensor("xao", [128, 512], F32))
        psA = _ctx.enter_context(nc.psum_tensor("psA", [128, 512], F32))
        psB2 = _ctx.enter_context(nc.psum_tensor("psB2", [128, 512], F32))
        psC2 = _ctx.enter_context(nc.psum_tensor("psC2", [128, 512], F32))
        psD2 = _ctx.enter_context(nc.psum_tensor("psD2", [128, 512], F32))
        ptr = _ctx.enter_context(nc.psum_tensor("ptr", [128, 512], BF16))
        blk = _ctx.enter_context(nc.Block())

        sq = Seq(nc)
        xnv = xnsb.ap().rearrange("p (c t) -> p c t", c=NCH)
        wkvv = wkvb.ap().rearrange("p (c t) -> p c t", c=NCH)
        wqv = wqb.ap().rearrange("p (c t) -> p c t", c=NCH)
        wpv = wpb.ap().rearrange("p (c t) -> p c t", c=NCH)
        qtv = qtok.ap().rearrange("p (b f) -> p b f", b=4)
        ktv = ktok.ap().rearrange("p (b f) -> p b f", b=4)
        vtv = vtok.ap().rearrange("p (b f) -> p b f", b=4)
        avv = avt.ap().rearrange("p (b f) -> p b f", b=4)
        avTv = avT.ap().rearrange("p (c t) -> p c t", c=NCH)
        av_ = atile.ap().rearrange("p (s h t) -> p s h t", s=4, h=16)
        amv = amx.ap().rearrange("p (s h) -> p s h", s=4)

        @sq.step(blk, "gpsimd")
        def _(eng, st):
            for c in range(NCH):
                st.dma(xnv[:, c], xn1T[128 * c:128 * (c + 1), :])
                st.dma(wkvv[:, c], wkv[128 * c:128 * (c + 1), :])
                st.dma(wqv[:, c], wq1[128 * c:128 * (c + 1), :])
                st.dma(wpv[:, c], wp1[128 * c:128 * (c + 1), :])
            for c in range(NCH):
                st.dma(bpc[:, c:c + 1], bp1[0:1, 128 * c:128 * (c + 1)].rearrange("o p -> p o"))

        @sq.step(blk, "gpsimd")
        def _(eng, st):
            return make_ident(nc, eng, identb[:, :], st)

        @sq.step(blk, "sync")
        def _(eng, st):
            st.dma(brow[0:1, :], bkv[0:1, :])

        @sq.step(blk, "vector")
        def _(eng, st):
            return eng.memset(onesr[:, :], 1.0)

        for i in range(4):
            @sq.step(blk, "tensor")
            def _(eng, st, i=i):
                return eng.matmul(psA[:, :], onesr[:, :],
                                  brow[0:1, 512 * i:512 * (i + 1)],
                                  start=True, stop=True)

            @sq.step(blk, "vector")
            def _(eng, st, i=i):
                return eng.tensor_copy(out=bkvb[:, 512 * i:512 * (i + 1)], in_=psA[:, :])

        @sq.step(blk, "sync")
        def _(eng, st):
            st.dma(brow[0:1, 0:1024], bq1[0:1, :])

        for i in range(2):
            @sq.step(blk, "tensor")
            def _(eng, st, i=i):
                return eng.matmul(psA[:, :], onesr[:, :],
                                  brow[0:1, 512 * i:512 * (i + 1)],
                                  start=True, stop=True)

            @sq.step(blk, "vector")
            def _(eng, st, i=i):
                return eng.tensor_copy(out=bqb[:, 512 * i:512 * (i + 1)], in_=psA[:, :])

        for jb in range(4):
            for b in range(4):
                tsl = slice(512 * b + 128 * jb, 512 * b + 128 * (jb + 1))
                for fg in range(4):
                    @sq.step(blk, "tensor")
                    def _(eng, st, fg=fg, tsl=tsl):
                        last = None
                        for k in range(NCH):
                            last = eng.matmul(psA[:, :],
                                              xnv[:, k][:, tsl],
                                              wkvv[:, k][:, 512 * fg:512 * (fg + 1)],
                                              start=(k == 0), stop=(k == NCH - 1))
                        return last

                    @sq.step(blk, "vector")
                    def _(eng, st, fg=fg, b=b):
                        dst = ktv[:, b][:, 512 * fg:512 * (fg + 1)] if fg < 2 else \
                            vtv[:, b][:, 512 * (fg - 2):512 * (fg - 1)]
                        return eng.tensor_tensor(out=dst, in0=psA[:, :],
                                                 in1=bkvb[:, 512 * fg:512 * (fg + 1)],
                                                 op=ALU.add)

                for fg in range(2):
                    @sq.step(blk, "tensor")
                    def _(eng, st, fg=fg, tsl=tsl):
                        last = None
                        for k in range(NCH):
                            last = eng.matmul(psA[:, :],
                                              xnv[:, k][:, tsl],
                                              wqv[:, k][:, 512 * fg:512 * (fg + 1)],
                                              start=(k == 0), stop=(k == NCH - 1))
                        return last

                    @sq.step(blk, "vector")
                    def _(eng, st, fg=fg, b=b):
                        return eng.tensor_tensor(out=qtv[:, b][:, 512 * fg:512 * (fg + 1)],
                                                 in0=psA[:, :],
                                                 in1=bqb[:, 512 * fg:512 * (fg + 1)],
                                                 op=ALU.add)

            for s in range(4):
                for t in range(4):
                    @sq.step(blk, "vector")
                    def _(eng, st, s=s, t=t):
                        st.chain(eng.tensor_tensor(out=tmp1[:, :], in0=qtv[:, s],
                                                   in1=ktv[:, t], op=ALU.mult))
                        return eng.reduce_sum(
                            out=av_[:, s, :, t].unsqueeze(-1),
                            in_=tmp1.ap().rearrange("p (h d) -> p h d", h=16),
                            axis=mybir.AxisListType.X)

            @sq.step(blk, "vector")
            def _(eng, st):
                return eng.reduce_max(out=amx[:, :], in_=atile[:, :].rearrange(
                    "p (s h t) -> p (s h) t", s=4, h=16), axis=mybir.AxisListType.X)

            @sq.step(blk, "vector")
            def _(eng, st):
                return eng.tensor_tensor(
                    out=atile[:, :].rearrange("p (s h t) -> p (s h) t", s=4, h=16),
                    in0=atile[:, :].rearrange("p (s h t) -> p (s h) t", s=4, h=16),
                    in1=amx[:, :].unsqueeze(-1).to_broadcast([128, 64, 4]),
                    op=ALU.subtract)

            @sq.step(blk, "scalar")
            def _(eng, st):
                return eng.activation(out=atile[:, :], in_=atile[:, :], func=AF.Exp,
                                      scale=1.0 / 8.0)

            @sq.step(blk, "vector")
            def _(eng, st):
                return eng.reduce_sum(out=amx[:, :], in_=atile[:, :].rearrange(
                    "p (s h t) -> p (s h) t", s=4, h=16), axis=mybir.AxisListType.X)

            @sq.step(blk, "vector")
            def _(eng, st):
                st.chain(eng.reciprocal(out=amx[:, :], in_=amx[:, :]))
                return eng.tensor_tensor(
                    out=atile[:, :].rearrange("p (s h t) -> p (s h) t", s=4, h=16),
                    in0=atile[:, :].rearrange("p (s h t) -> p (s h) t", s=4, h=16),
                    in1=amx[:, :].unsqueeze(-1).to_broadcast([128, 64, 4]),
                    op=ALU.mult)

            for s in range(4):
                for t in range(4):
                    @sq.step(blk, "vector")
                    def _(eng, st, s=s, t=t):
                        pview = av_[:, s, :, t].unsqueeze(-1).to_broadcast([128, 16, 64])
                        if t == 0:
                            return eng.tensor_tensor(
                                out=avv[:, s].rearrange("p (h d) -> p h d", h=16),
                                in0=vtv[:, t].rearrange("p (h d) -> p h d", h=16),
                                in1=pview, op=ALU.mult)
                        st.chain(eng.tensor_tensor(
                            out=tmp1.ap().rearrange("p (h d) -> p h d", h=16),
                            in0=vtv[:, t].rearrange("p (h d) -> p h d", h=16),
                            in1=pview, op=ALU.mult))
                        return eng.tensor_tensor(out=avv[:, s], in0=avv[:, s],
                                                 in1=tmp1[:, :], op=ALU.add)

            for s in range(4):
                for fg in range(2):
                    @sq.step(blk, "tensor")
                    def _(eng, st, s=s, fg=fg):
                        last = None
                        for u in range(4):
                            fc = 4 * fg + u
                            last = eng.transpose(ptr[:, 128 * u:128 * (u + 1)],
                                                 avv[:, s][:, 128 * fc:128 * (fc + 1)],
                                                 identb[:, :])
                        return last

                    @sq.step(blk, "vector")
                    def _(eng, st, s=s, fg=fg, jb=jb):
                        last = None
                        for u in range(4):
                            fc = 4 * fg + u
                            last = eng.tensor_copy(
                                out=avTv[:, fc][:, 512 * s + 128 * jb:512 * s + 128 * (jb + 1)],
                                in_=ptr[:, 128 * u:128 * (u + 1)])
                        return last

        for oc in range(NCH):
            for tt in range(NTT):
                @sq.step(blk, "tensor")
                def _(eng, st, oc=oc, tt=tt):
                    last = None
                    for k in range(NCH):
                        last = eng.matmul(psD2[:, :],
                                          wpv[:, k][:, 128 * oc:128 * (oc + 1)],
                                          avTv[:, k][:, 512 * tt:512 * (tt + 1)],
                                          start=(k == 0), stop=(k == NCH - 1))
                    return last

                @sq.step(blk, "sync")
                def _(eng, st, oc=oc, tt=tt):
                    st.dma(xbu[:, :], x1T[128 * oc:128 * (oc + 1), 512 * tt:512 * (tt + 1)])

                @sq.step(blk, "vector")
                def _(eng, st, oc=oc):
                    st.chain(eng.tensor_scalar(out=xao[:, :], in0=psD2[:, :],
                                               scalar1=bpc[:, oc:oc + 1], scalar2=None,
                                               op0=ALU.add))
                    return eng.tensor_tensor(out=xao[:, :], in0=xao[:, :],
                                             in1=xbu[:, :], op=ALU.add)

                @sq.step(blk, "sync")
                def _(eng, st, oc=oc, tt=tt):
                    st.dma(xBT[128 * oc:128 * (oc + 1), 512 * tt:512 * (tt + 1)],
                           xao[:, :])

        @sq.step(blk, "sync")
        def _(eng, st):
            return eng.nop()

    return nc




def emit_round_inplace(eng, st, r, ti, tf, td):
    st.chain(eng.tensor_copy(out=ti, in_=r))
    st.chain(eng.tensor_copy(out=tf, in_=ti))
    st.chain(eng.tensor_tensor(out=td, in0=r, in1=tf, op=ALU.subtract))
    st.chain(eng.tensor_scalar(out=r, in0=td, scalar1=0.5, scalar2=None, op0=ALU.is_gt))
    st.chain(eng.tensor_tensor(out=tf, in0=tf, in1=r, op=ALU.add))
    st.chain(eng.tensor_scalar(out=r, in0=td, scalar1=-0.5, scalar2=None, op0=ALU.is_lt))
    return eng.tensor_tensor(out=r, in0=tf, in1=r, op=ALU.subtract)


def emit_qdq(eng, st, out_ap, x_ap, inv_col, s_col, r, ti, tf, td):
    st.chain(eng.tensor_scalar(out=r, in0=x_ap, scalar1=inv_col, scalar2=None,
                               op0=ALU.mult))
    st.chain(emit_round_inplace(eng, st, r, ti, tf, td))
    return eng.tensor_scalar(out=out_ap, in0=r, scalar1=s_col, scalar2=None,
                             op0=ALU.mult)


def emit_scale_from_max(sq, blk, nc, onesr, m_in_row, n_in, qmax, srow, scol, icol, ps):
    @sq.step(blk, "vector")
    def _(eng, st):
        st.chain(eng.reduce_max(out=srow[0:1, 0:1], in_=m_in_row[0:1, 0:n_in],
                                axis=mybir.AxisListType.X))
        st.chain(eng.tensor_scalar(out=srow[0:1, 1:2], in0=srow[0:1, 0:1],
                                   scalar1=1.0 / qmax, scalar2=1e-12,
                                   op0=ALU.mult, op1=ALU.max))
        return eng.reciprocal(out=srow[0:1, 2:3], in_=srow[0:1, 1:2])

    @sq.step(blk, "tensor")
    def _(eng, st):
        eng.matmul(ps[:, 0:1], onesr[:, :], srow[0:1, 1:2], start=True, stop=True)
        return eng.matmul(ps[:, 1:2], onesr[:, :], srow[0:1, 2:3], start=True, stop=True)

    @sq.step(blk, "vector")
    def _(eng, st):
        st.chain(eng.tensor_copy(out=scol, in_=ps[:, 0:1]))
        return eng.tensor_copy(out=icol, in_=ps[:, 1:2])


def emit_colmax_to_scalar(sq, blk, nc, mcol, identf, psT, mrow, out11):
    @sq.step(blk, "tensor")
    def _(eng, st):
        return eng.transpose(psT[0:1, 0:128], mcol, identf[:, :])

    @sq.step(blk, "vector")
    def _(eng, st):
        st.chain(eng.tensor_copy(out=mrow[0:1, :], in_=psT[0:1, 0:128]))
        return eng.reduce_max(out=out11, in_=mrow[0:1, :], axis=mybir.AxisListType.X)


def emit_dequant_chunk(eng, st, w_ap, wq_ap, nblk, sw, swr, r, ti, tf, td):
    wv = w_ap.rearrange("p (b i) -> p b i", i=32)
    st.chain(eng.reduce_max(out=sw, in_=wv, axis=mybir.AxisListType.X,
                            apply_absolute_value=True))
    st.chain(eng.tensor_scalar(out=sw, in0=sw, scalar1=1.0 / 7.0, scalar2=1e-12,
                               op0=ALU.mult, op1=ALU.max))
    st.chain(eng.reciprocal(out=swr, in_=sw))
    rv = r.rearrange("p (b i) -> p b i", i=32)
    st.chain(eng.tensor_tensor(out=rv, in0=wv,
                               in1=swr.unsqueeze(-1).to_broadcast(list(sw.shape) + [32]),
                               op=ALU.mult))
    st.chain(emit_round_inplace(eng, st, r, ti, tf, td))
    return eng.tensor_tensor(out=wq_ap.rearrange("p (b i) -> p b i", i=32), in0=rv,
                             in1=sw.unsqueeze(-1).to_broadcast(list(sw.shape) + [32]),
                             op=ALU.mult)



def build_E1():
    nc = bass.Bass("TRN2", target_bir_lowering=False)
    xn1F = nc.dram_tensor("xn1F", [C, TPC], F32, kind="ExternalInput").ap()
    xmax8 = nc.dram_tensor("xmax8", [1, 8], F32, kind="ExternalInput").ap()
    wgn = nc.dram_tensor("wgn", [I_L1, C], F32, kind="ExternalInput").ap()
    wun = nc.dram_tensor("wun", [I_L1, C], F32, kind="ExternalInput").ap()
    upD = nc.dram_tensor("upD", [I_L1, TPC], BF16, kind="ExternalOutput").ap()
    gateD = nc.dram_tensor("gateD", [I_L1, TPC], BF16, kind="ExternalOutput").ap()
    mups = nc.dram_tensor("mups", [1, 1], F32, kind="ExternalOutput").ap()
    mgts = nc.dram_tensor("mgts", [1, 1], F32, kind="ExternalOutput").ap()

    NIC = I_L1 // 128

    from contextlib import ExitStack
    with ExitStack() as _ctx:
        xqsb = _ctx.enter_context(nc.sbuf_tensor("xqsb", [128, NCH * TPC], BF16))
        wgT = _ctx.enter_context(nc.sbuf_tensor("wgT", [128, NCH * I_L1], BF16))
        wuT = _ctx.enter_context(nc.sbuf_tensor("wuT", [128, NCH * I_L1], BF16))
        wch = _ctx.enter_context(nc.sbuf_tensor("wch", [128, 1024], F32))
        wqc = _ctx.enter_context(nc.sbuf_tensor("wqc", [128, 1024], BF16))
        sw = _ctx.enter_context(nc.sbuf_tensor("sw", [128, 32], F32))
        swr = _ctx.enter_context(nc.sbuf_tensor("swr", [128, 32], F32))
        rr = _ctx.enter_context(nc.sbuf_tensor("rr", [128, 2048], F32))
        ti = _ctx.enter_context(nc.sbuf_tensor("ti", [128, 2048], I32))
        tf = _ctx.enter_context(nc.sbuf_tensor("tf", [128, 2048], F32))
        td = _ctx.enter_context(nc.sbuf_tensor("td", [128, 2048], F32))
        xch = _ctx.enter_context(nc.sbuf_tensor("xch", [128, 2048], F32))
        urow = _ctx.enter_context(nc.sbuf_tensor("urow", [128, 512], BF16))
        onesr = _ctx.enter_context(nc.sbuf_tensor("onesr", [1, 128], F32))
        identf = _ctx.enter_context(nc.sbuf_tensor("identf", [128, 128], F32))
        identb = _ctx.enter_context(nc.sbuf_tensor("identb", [128, 128], BF16))
        srow = _ctx.enter_context(nc.sbuf_tensor("srow", [1, 8], F32))
        mrow = _ctx.enter_context(nc.sbuf_tensor("mrow", [1, 128], F32))
        scol = _ctx.enter_context(nc.sbuf_tensor("scol", [128, 1], F32))
        icol = _ctx.enter_context(nc.sbuf_tensor("icol", [128, 1], F32))
        mxu = _ctx.enter_context(nc.sbuf_tensor("mxu", [128, 3], F32))
        mxg = _ctx.enter_context(nc.sbuf_tensor("mxg", [128, 3], F32))
        psA = _ctx.enter_context(nc.psum_tensor("psA", [128, 512], F32))
        psW = _ctx.enter_context(nc.psum_tensor("psW", [128, 512], BF16))
        psT = _ctx.enter_context(nc.psum_tensor("psT", [128, 128], F32))
        blk = _ctx.enter_context(nc.Block())

        sq = Seq(nc)
        xqv = xqsb.ap().rearrange("p (c t) -> p c t", c=NCH)
        wgv = wgT.ap().rearrange("p (c t) -> p c t", c=NCH)
        wuv = wuT.ap().rearrange("p (c t) -> p c t", c=NCH)

        @sq.step(blk, "gpsimd")
        def _(eng, st):
            st.dma(srow[0:1, :], xmax8[0:1, :])

        @sq.step(blk, "gpsimd")
        def _(eng, st):
            st.chain(make_ident(nc, eng, identf[:, :], st))
            return make_ident(nc, eng, identb[:, :], st)

        @sq.step(blk, "vector")
        def _(eng, st):
            eng.memset(onesr[:, :], 1.0)
            eng.memset(mxu[:, 0:1], 0.0)
            return eng.memset(mxg[:, 0:1], 0.0)

        emit_scale_from_max(sq, blk, nc, onesr, srow, 8, 32767.0, srow, scol[:, 0:1], icol[:, 0:1], psT)

        for c in range(NCH):
            @sq.step(blk, "sync")
            def _(eng, st, c=c):
                st.dma(xch[:, :], xn1F[128 * c:128 * (c + 1), :])

            @sq.step(blk, "vector")
            def _(eng, st, c=c):
                return emit_qdq(eng, st, xqv[:, c], xch[:, :], icol[:, 0:1],
                                scol[:, 0:1], rr[:, :], ti[:, :], tf[:, :], td[:, :])

        for which, wn, wTv in (("g", wgn, wgv), ("u", wun, wuv)):
            for oc in range(NIC):
                @sq.step(blk, "sync")
                def _(eng, st, wn=wn, oc=oc):
                    st.dma(wch[:, :], wn[128 * oc:128 * (oc + 1), :])

                @sq.step(blk, "vector")
                def _(eng, st):
                    return emit_dequant_chunk(eng, st, wch[:, :], wqc[:, :], 32,
                                              sw[:, :], swr[:, :], rr[:, 0:1024],
                                              ti[:, 0:1024], tf[:, 0:1024], td[:, 0:1024])

                for hc in range(NCH):
                    @sq.step(blk, "tensor")
                    def _(eng, st, hc=hc):
                        return eng.transpose(psW[:, 0:128],
                                             wqc[:, 128 * hc:128 * (hc + 1)],
                                             identb[:, :])

                    @sq.step(blk, "vector")
                    def _(eng, st, hc=hc, oc=oc, wTv=wTv):
                        return eng.tensor_copy(out=wTv[:, hc][:, 128 * oc:128 * (oc + 1)],
                                               in_=psW[:, 0:128])

        for which, wTv, outD, mx in (("g", wgv, gateD, mxg), ("u", wuv, upD, mxu)):
            for ic in range(NIC):
                for tt in range(NTT):
                    @sq.step(blk, "tensor")
                    def _(eng, st, ic=ic, tt=tt, wTv=wTv):
                        last = None
                        for k in range(NCH):
                            last = eng.matmul(psA[:, :],
                                              wTv[:, k][:, 128 * ic:128 * (ic + 1)],
                                              xqv[:, k][:, 512 * tt:512 * (tt + 1)],
                                              start=(k == 0), stop=(k == NCH - 1))
                        return last

                    @sq.step(blk, "vector")
                    def _(eng, st, mx=mx):
                        st.chain(eng.tensor_copy(out=urow[:, :], in_=psA[:, :]))
                        st.chain(eng.reduce_max(out=mx[:, 1:2], in_=psA[:, :],
                                                axis=mybir.AxisListType.X,
                                                apply_absolute_value=True))
                        return eng.tensor_tensor(out=mx[:, 0:1], in0=mx[:, 0:1],
                                                 in1=mx[:, 1:2], op=ALU.max)

                    @sq.step(blk, "sync")
                    def _(eng, st, ic=ic, tt=tt, outD=outD):
                        st.dma(outD[128 * ic:128 * (ic + 1), 512 * tt:512 * (tt + 1)],
                               urow[:, :])

        emit_colmax_to_scalar(sq, blk, nc, mxg[:, 0:1], identf, psT, mrow, srow[0:1, 3:4])
        emit_colmax_to_scalar(sq, blk, nc, mxu[:, 0:1], identf, psT, mrow, srow[0:1, 4:5])

        @sq.step(blk, "sync")
        def _(eng, st):
            st.dma(mgts[0:1, 0:1], srow[0:1, 3:4])
            st.dma(mups[0:1, 0:1], srow[0:1, 4:5])

        @sq.step(blk, "sync")
        def _(eng, st):
            return eng.nop()

    return nc



def build_E2():
    nc = bass.Bass("TRN2", target_bir_lowering=False)
    upD = nc.dram_tensor("upD", [I_L1, TPC], BF16, kind="ExternalInput").ap()
    gateD = nc.dram_tensor("gateD", [I_L1, TPC], BF16, kind="ExternalInput").ap()
    mup8 = nc.dram_tensor("mup8", [1, 8], F32, kind="ExternalInput").ap()
    mgt8 = nc.dram_tensor("mgt8", [1, 8], F32, kind="ExternalInput").ap()
    upqD = nc.dram_tensor("upqD", [I_L1, TPC], BF16, kind="ExternalOutput").ap()
    g2D = nc.dram_tensor("g2D", [I_L1, TPC], BF16, kind="ExternalOutput").ap()
    mg2s = nc.dram_tensor("mg2s", [1, 1], F32, kind="ExternalOutput").ap()

    NIC = I_L1 // 128

    from contextlib import ExitStack
    with ExitStack() as _ctx:
        ch = _ctx.enter_context(nc.sbuf_tensor("ch", [128, TPC], F32))
        ch2 = _ctx.enter_context(nc.sbuf_tensor("ch2", [128, TPC], F32))
        chb = _ctx.enter_context(nc.sbuf_tensor("chb", [128, TPC], BF16))
        rr = _ctx.enter_context(nc.sbuf_tensor("rr", [128, TPC], F32))
        ti = _ctx.enter_context(nc.sbuf_tensor("ti", [128, TPC], I32))
        tf = _ctx.enter_context(nc.sbuf_tensor("tf", [128, TPC], F32))
        td = _ctx.enter_context(nc.sbuf_tensor("td", [128, TPC], F32))
        onesr = _ctx.enter_context(nc.sbuf_tensor("onesr", [1, 128], F32))
        identf = _ctx.enter_context(nc.sbuf_tensor("identf", [128, 128], F32))
        srow = _ctx.enter_context(nc.sbuf_tensor("srow", [1, 8], F32))
        mrow = _ctx.enter_context(nc.sbuf_tensor("mrow", [1, 128], F32))
        sgc = _ctx.enter_context(nc.sbuf_tensor("sgc", [128, 2], F32))
        suc = _ctx.enter_context(nc.sbuf_tensor("suc", [128, 2], F32))
        mxc = _ctx.enter_context(nc.sbuf_tensor("mxc", [128, 2], F32))
        psT = _ctx.enter_context(nc.psum_tensor("psT", [128, 128], F32))
        blk = _ctx.enter_context(nc.Block())

        sq = Seq(nc)

        @sq.step(blk, "gpsimd")
        def _(eng, st):
            st.dma(srow[0:1, 0:8], mgt8[0:1, :])

        @sq.step(blk, "gpsimd")
        def _(eng, st):
            return make_ident(nc, eng, identf[:, :], st)

        @sq.step(blk, "vector")
        def _(eng, st):
            eng.memset(onesr[:, :], 1.0)
            return eng.memset(mxc[:, 0:1], 0.0)

        emit_scale_from_max(sq, blk, nc, onesr, srow, 8, 32767.0, srow,
                            sgc[:, 0:1], sgc[:, 1:2], psT)

        @sq.step(blk, "gpsimd")
        def _(eng, st):
            st.dma(srow[0:1, 0:8], mup8[0:1, :])

        emit_scale_from_max(sq, blk, nc, onesr, srow, 8, 32767.0, srow,
                            suc[:, 0:1], suc[:, 1:2], psT)

        for ic in range(NIC):
            @sq.step(blk, "sync")
            def _(eng, st, ic=ic):
                st.dma(chb[:, :], gateD[128 * ic:128 * (ic + 1), :])

            @sq.step(blk, "vector")
            def _(eng, st):
                st.chain(eng.tensor_copy(out=ch[:, :], in_=chb[:, :]))
                return emit_qdq(eng, st, ch[:, :], ch[:, :], sgc[:, 1:2],
                                sgc[:, 0:1], rr[:, :], ti[:, :], tf[:, :], td[:, :])

            @sq.step(blk, "scalar")
            def _(eng, st):
                return eng.activation(out=ch2[:, :], in_=ch[:, :], func=AF.Sigmoid)

            @sq.step(blk, "vector")
            def _(eng, st):
                st.chain(eng.tensor_scalar(out=rr[:, :], in0=ch2[:, :],
                                           scalar1=65536.0, scalar2=None, op0=ALU.mult))
                st.chain(emit_round_inplace(eng, st, rr[:, :], ti[:, :], tf[:, :], td[:, :]))
                st.chain(eng.tensor_scalar(out=rr[:, :], in0=rr[:, :], scalar1=65535.0,
                                           scalar2=1.0 / 65536.0, op0=ALU.min, op1=ALU.mult))
                st.chain(eng.tensor_tensor(out=ch[:, :], in0=ch[:, :], in1=rr[:, :],
                                           op=ALU.mult))
                st.chain(eng.tensor_copy(out=chb[:, :], in_=ch[:, :]))
                st.chain(eng.reduce_max(out=mxc[:, 1:2], in_=ch[:, :],
                                        axis=mybir.AxisListType.X,
                                        apply_absolute_value=True))
                return eng.tensor_tensor(out=mxc[:, 0:1], in0=mxc[:, 0:1],
                                         in1=mxc[:, 1:2], op=ALU.max)

            @sq.step(blk, "sync")
            def _(eng, st, ic=ic):
                st.dma(g2D[128 * ic:128 * (ic + 1), :], chb[:, :])

            @sq.step(blk, "sync")
            def _(eng, st, ic=ic):
                st.dma(chb[:, :], upD[128 * ic:128 * (ic + 1), :])

            @sq.step(blk, "vector")
            def _(eng, st):
                st.chain(eng.tensor_copy(out=ch[:, :], in_=chb[:, :]))
                st.chain(emit_qdq(eng, st, ch[:, :], ch[:, :], suc[:, 1:2],
                                  suc[:, 0:1], rr[:, :], ti[:, :], tf[:, :], td[:, :]))
                return eng.tensor_copy(out=chb[:, :], in_=ch[:, :])

            @sq.step(blk, "sync")
            def _(eng, st, ic=ic):
                st.dma(upqD[128 * ic:128 * (ic + 1), :], chb[:, :])

        emit_colmax_to_scalar(sq, blk, nc, mxc[:, 0:1], identf, psT, mrow, srow[0:1, 3:4])

        @sq.step(blk, "sync")
        def _(eng, st):
            st.dma(mg2s[0:1, 0:1], srow[0:1, 3:4])

        @sq.step(blk, "sync")
        def _(eng, st):
            return eng.nop()

    return nc



def build_E3():
    nc = bass.Bass("TRN2", target_bir_lowering=False)
    upqD = nc.dram_tensor("upqD", [I_L1, TPC], BF16, kind="ExternalInput").ap()
    g2D = nc.dram_tensor("g2D", [I_L1, TPC], BF16, kind="ExternalInput").ap()
    mg28 = nc.dram_tensor("mg28", [1, 8], F32, kind="ExternalInput").ap()
    oD = nc.dram_tensor("oD", [I_L1, TPC], BF16, kind="ExternalOutput").ap()
    mos = nc.dram_tensor("mos", [1, 1], F32, kind="ExternalOutput").ap()

    NIC = I_L1 // 128

    from contextlib import ExitStack
    with ExitStack() as _ctx:
        ch = _ctx.enter_context(nc.sbuf_tensor("ch", [128, TPC], F32))
        chb = _ctx.enter_context(nc.sbuf_tensor("chb", [128, TPC], BF16))
        chb2 = _ctx.enter_context(nc.sbuf_tensor("chb2", [128, TPC], BF16))
        rr = _ctx.enter_context(nc.sbuf_tensor("rr", [128, TPC], F32))
        ti = _ctx.enter_context(nc.sbuf_tensor("ti", [128, TPC], I32))
        tf = _ctx.enter_context(nc.sbuf_tensor("tf", [128, TPC], F32))
        td = _ctx.enter_context(nc.sbuf_tensor("td", [128, TPC], F32))
        onesr = _ctx.enter_context(nc.sbuf_tensor("onesr", [1, 128], F32))
        identf = _ctx.enter_context(nc.sbuf_tensor("identf", [128, 128], F32))
        srow = _ctx.enter_context(nc.sbuf_tensor("srow", [1, 8], F32))
        mrow = _ctx.enter_context(nc.sbuf_tensor("mrow", [1, 128], F32))
        sgc = _ctx.enter_context(nc.sbuf_tensor("sgc", [128, 2], F32))
        mxc = _ctx.enter_context(nc.sbuf_tensor("mxc", [128, 2], F32))
        psT = _ctx.enter_context(nc.psum_tensor("psT", [128, 128], F32))
        blk = _ctx.enter_context(nc.Block())

        sq = Seq(nc)

        @sq.step(blk, "gpsimd")
        def _(eng, st):
            st.dma(srow[0:1, 0:8], mg28[0:1, :])

        @sq.step(blk, "gpsimd")
        def _(eng, st):
            return make_ident(nc, eng, identf[:, :], st)

        @sq.step(blk, "vector")
        def _(eng, st):
            eng.memset(onesr[:, :], 1.0)
            return eng.memset(mxc[:, 0:1], 0.0)

        emit_scale_from_max(sq, blk, nc, onesr, srow, 8, 32767.0, srow,
                            sgc[:, 0:1], sgc[:, 1:2], psT)

        for ic in range(NIC):
            @sq.step(blk, "sync")
            def _(eng, st, ic=ic):
                st.dma(chb[:, :], g2D[128 * ic:128 * (ic + 1), :])
                st.dma(chb2[:, :], upqD[128 * ic:128 * (ic + 1), :])

            @sq.step(blk, "vector")
            def _(eng, st):
                st.chain(eng.tensor_copy(out=ch[:, :], in_=chb[:, :]))
                st.chain(emit_qdq(eng, st, ch[:, :], ch[:, :], sgc[:, 1:2],
                                  sgc[:, 0:1], rr[:, :], ti[:, :], tf[:, :], td[:, :]))
                st.chain(eng.tensor_tensor(out=ch[:, :], in0=ch[:, :], in1=chb2[:, :],
                                           op=ALU.mult))
                st.chain(eng.tensor_copy(out=chb[:, :], in_=ch[:, :]))
                st.chain(eng.reduce_max(out=mxc[:, 1:2], in_=ch[:, :],
                                        axis=mybir.AxisListType.X,
                                        apply_absolute_value=True))
                return eng.tensor_tensor(out=mxc[:, 0:1], in0=mxc[:, 0:1],
                                         in1=mxc[:, 1:2], op=ALU.max)

            @sq.step(blk, "sync")
            def _(eng, st, ic=ic):
                st.dma(oD[128 * ic:128 * (ic + 1), :], chb[:, :])

        emit_colmax_to_scalar(sq, blk, nc, mxc[:, 0:1], identf, psT, mrow, srow[0:1, 3:4])

        @sq.step(blk, "sync")
        def _(eng, st):
            st.dma(mos[0:1, 0:1], srow[0:1, 3:4])

        @sq.step(blk, "sync")
        def _(eng, st):
            return eng.nop()

    return nc



def build_E4():
    nc = bass.Bass("TRN2", target_bir_lowering=False)
    oD = nc.dram_tensor("oD", [I_L1, TPC], BF16, kind="ExternalInput").ap()
    mo8 = nc.dram_tensor("mo8", [1, 8], F32, kind="ExternalInput").ap()
    wdn = nc.dram_tensor("wdn", [C, I_L1], F32, kind="ExternalInput").ap()
    xBT = nc.dram_tensor("xBT", [C, TPC], F32, kind="ExternalInput").ap()
    xoutT = nc.dram_tensor("xoutT", [C, TPC], F32, kind="ExternalOutput").ap()

    NIC = I_L1 // 128

    from contextlib import ExitStack
    with ExitStack() as _ctx:
        oq = _ctx.enter_context(nc.sbuf_tensor("oq", [128, NIC * TPC // 4], BF16))
        wdT = _ctx.enter_context(nc.sbuf_tensor("wdT", [128, NIC * 1024], BF16))
        wch = _ctx.enter_context(nc.sbuf_tensor("wch", [128, I_L1], F32))
        wqc = _ctx.enter_context(nc.sbuf_tensor("wqc", [128, I_L1], BF16))
        sw = _ctx.enter_context(nc.sbuf_tensor("sw", [128, 96], F32))
        swr = _ctx.enter_context(nc.sbuf_tensor("swr", [128, 96], F32))
        rr = _ctx.enter_context(nc.sbuf_tensor("rr", [128, I_L1], F32))
        ti = _ctx.enter_context(nc.sbuf_tensor("ti", [128, I_L1], I32))
        tf = _ctx.enter_context(nc.sbuf_tensor("tf", [128, I_L1], F32))
        td = _ctx.enter_context(nc.sbuf_tensor("td", [128, I_L1], F32))
        chb = _ctx.enter_context(nc.sbuf_tensor("chb", [128, TPC], BF16))
        ch = _ctx.enter_context(nc.sbuf_tensor("ch", [128, TPC], F32))
        onesr = _ctx.enter_context(nc.sbuf_tensor("onesr", [1, 128], F32))
        identb = _ctx.enter_context(nc.sbuf_tensor("identb", [128, 128], BF16))
        srow = _ctx.enter_context(nc.sbuf_tensor("srow", [1, 8], F32))
        scol = _ctx.enter_context(nc.sbuf_tensor("scol", [128, 1], F32))
        icol = _ctx.enter_context(nc.sbuf_tensor("icol", [128, 1], F32))
        xbu = _ctx.enter_context(nc.sbuf_tensor("xbu", [128, 512], F32))
        xao = _ctx.enter_context(nc.sbuf_tensor("xao", [128, 512], F32))
        psA = _ctx.enter_context(nc.psum_tensor("psA", [128, 512], F32))
        psW = _ctx.enter_context(nc.psum_tensor("psW", [128, 128], BF16))
        psT = _ctx.enter_context(nc.psum_tensor("psT", [128, 128], F32))
        blk = _ctx.enter_context(nc.Block())

        sq = Seq(nc)
        oqv = oq.ap().rearrange("p (c t) -> p c t", c=NIC)
        wdv = wdT.ap().rearrange("p (c t) -> p c t", c=NIC)

        @sq.step(blk, "gpsimd")
        def _(eng, st):
            st.dma(srow[0:1, 0:8], mo8[0:1, :])

        @sq.step(blk, "gpsimd")
        def _(eng, st):
            return make_ident(nc, eng, identb[:, :], st)

        @sq.step(blk, "vector")
        def _(eng, st):
            return eng.memset(onesr[:, :], 1.0)

        emit_scale_from_max(sq, blk, nc, onesr, srow, 8, 32767.0, srow, scol[:, 0:1], icol[:, 0:1], psT)

        for hc in range(NCH):
            @sq.step(blk, "sync")
            def _(eng, st, hc=hc):
                st.dma(wch[:, :], wdn[128 * hc:128 * (hc + 1), :])

            @sq.step(blk, "vector")
            def _(eng, st):
                return emit_dequant_chunk(eng, st, wch[:, :], wqc[:, :], 96,
                                          sw[:, :], swr[:, :], rr[:, :],
                                          ti[:, :], tf[:, :], td[:, :])

            for ic in range(NIC):
                @sq.step(blk, "tensor")
                def _(eng, st, ic=ic):
                    return eng.transpose(psW[:, 0:128],
                                         wqc[:, 128 * ic:128 * (ic + 1)],
                                         identb[:, :])

                @sq.step(blk, "vector")
                def _(eng, st, ic=ic, hc=hc):
                    return eng.tensor_copy(out=wdv[:, ic][:, 128 * hc:128 * (hc + 1)],
                                           in_=psW[:, 0:128])

        for tt in range(NTT):
            for ic in range(NIC):
                @sq.step(blk, "sync")
                def _(eng, st, ic=ic, tt=tt):
                    st.dma(chb[:, 0:512], oD[128 * ic:128 * (ic + 1),
                                             512 * tt:512 * (tt + 1)])

                @sq.step(blk, "vector")
                def _(eng, st, ic=ic):
                    st.chain(eng.tensor_copy(out=ch[:, 0:512], in_=chb[:, 0:512]))
                    return emit_qdq(eng, st, oqv[:, ic], ch[:, 0:512], icol[:, 0:1],
                                    scol[:, 0:1], rr[:, 0:512], ti[:, 0:512],
                                    tf[:, 0:512], td[:, 0:512])

            for oc in range(NCH):
                @sq.step(blk, "tensor")
                def _(eng, st, oc=oc):
                    last = None
                    for k in range(NIC):
                        last = eng.matmul(psA[:, :],
                                          wdv[:, k][:, 128 * oc:128 * (oc + 1)],
                                          oqv[:, k], start=(k == 0), stop=(k == NIC - 1))
                    return last

                @sq.step(blk, "sync")
                def _(eng, st, oc=oc, tt=tt):
                    st.dma(xbu[:, :], xBT[128 * oc:128 * (oc + 1), 512 * tt:512 * (tt + 1)])

                @sq.step(blk, "vector")
                def _(eng, st):
                    return eng.tensor_tensor(out=xao[:, :], in0=psA[:, :],
                                             in1=xbu[:, :], op=ALU.add)

                @sq.step(blk, "sync")
                def _(eng, st, oc=oc, tt=tt):
                    st.dma(xoutT[128 * oc:128 * (oc + 1), 512 * tt:512 * (tt + 1)],
                           xao[:, :])

        @sq.step(blk, "sync")
        def _(eng, st):
            return eng.nop()

    return nc



def _shard_x_T(x):
    out = []
    for c in range(NCORE):
        sl = x[:, 512 * c:512 * (c + 1), :]
        sl = sl.reshape(TPC, C).T
        out.append(np.ascontiguousarray(sl, dtype=np.float32))
    return out


def run_A(inputs):
    key = "A"
    if key not in _prog_cache:
        _prog_cache[key] = build_A()
    nc = _prog_cache[key]
    xs = _shard_x_T(inputs["L0_attn_hidden_states"])
    base = {
        "wq": np.ascontiguousarray(inputs["L0_attn_Wq"], np.float32),
        "wk": np.ascontiguousarray(inputs["L0_attn_Wk"], np.float32),
        "wvv": np.ascontiguousarray(inputs["L0_attn_Wv"], np.float32),
        "wvg": np.ascontiguousarray(inputs["L0_attn_Wvg"], np.float32),
        "wqi": np.ascontiguousarray(inputs["L0_attn_Wqi"], np.float32),
        "wki": np.ascontiguousarray(inputs["L0_attn_Wki"], np.float32),
        "wwi": np.ascontiguousarray(inputs["L0_attn_Wwi"], np.float32),
        "wog": np.ascontiguousarray(inputs["L0_attn_Wog"], np.float32),
        "bvg": np.ascontiguousarray(inputs["L0_attn_bvg"].reshape(1, 256), np.float32),
        "bog": np.ascontiguousarray(inputs["L0_attn_bog"].reshape(1, 1024), np.float32),
    }
    in_maps = [dict(base, xT=xs[c],
                    posoff=np.full((128, 1), 512.0 * c, np.float32))
               for c in range(NCORE)]
    res = run_bass_kernel_spmd(nc, in_maps, core_ids=list(range(NCORE)))
    return res.results


def assemble_kv_all(kv_locs):
    kv = np.empty((B * L, 576), kv_locs[0].dtype)
    for c in range(NCORE):
        for b in range(B):
            kv[b * L + 512 * c: b * L + 512 * (c + 1)] = \
                kv_locs[c][b * 512:(b + 1) * 512]
    return kv


def run_B(inputs, a_results, n_batches=4):
    key = ("B", n_batches)
    if key not in _prog_cache:
        _prog_cache[key] = build_B(n_batches=n_batches)
    nc = _prog_cache[key]
    xs = _shard_x_T(inputs["L0_attn_hidden_states"])
    kv = assemble_kv_all([r["kv_loc"] for r in a_results])
    base = {
        "kv_all": kv,
        "bi_in": np.ascontiguousarray(inputs["L0_attn_bi"].reshape(1, 4), np.float32),
        "wo": np.ascontiguousarray(inputs["L0_attn_Wo"], np.float32),
    }
    in_maps = [dict(base, xT=xs[c], qrT=a_results[c]["qrT"],
                    qiT=a_results[c]["qiT"], g1T=a_results[c]["g1T"],
                    wI_tok=a_results[c]["wI_tok"]) for c in range(NCORE)]
    res = run_bass_kernel_spmd(nc, in_maps, core_ids=list(range(NCORE)))
    return res.results


def _c_base_inputs(inputs):
    tr = lambda a, ax: np.ascontiguousarray(np.transpose(np.asarray(a, np.float32), ax))
    return {
        "gwT": tr(inputs["L0_ffn_gate_w"], (1, 0)),
        "ebias": np.ascontiguousarray(np.asarray(inputs["L0_ffn_expert_bias"], np.float32).reshape(1, 16)),
        "wgT": tr(inputs["L0_ffn_w_gate"], (0, 2, 1)),
        "wuT": tr(inputs["L0_ffn_w_up"], (0, 2, 1)),
        "wdT": tr(inputs["L0_ffn_w_down"], (0, 2, 1)),
        "sgT": tr(inputs["L0_ffn_shared_gate_w"], (1, 0)),
        "suT": tr(inputs["L0_ffn_shared_up_w"], (1, 0)),
        "sdT": tr(inputs["L0_ffn_shared_down_w"], (1, 0)),
    }


def run_C(inputs, a_results, b_results):
    key = "C"
    if key not in _prog_cache:
        _prog_cache[key] = build_C()
    nc = _prog_cache[key]
    base = _c_base_inputs(inputs)
    in_maps = [dict(base, xnT=a_results[c]["xnT"], xAT=b_results[c]["xAT"])
               for c in range(NCORE)]
    res = run_bass_kernel_spmd(nc, in_maps, core_ids=list(range(NCORE)))
    return res.results


def _d_base_inputs(inputs):
    return {
        "wkv": np.ascontiguousarray(inputs["L1_attn_Wkv"], np.float32),
        "bkv": np.ascontiguousarray(np.asarray(inputs["L1_attn_bkv"], np.float32).reshape(1, 2048)),
        "wq1": np.ascontiguousarray(inputs["L1_attn_Wq"], np.float32),
        "bq1": np.ascontiguousarray(np.asarray(inputs["L1_attn_bq"], np.float32).reshape(1, 1024)),
        "wp1": np.ascontiguousarray(inputs["L1_attn_Wp"], np.float32),
        "bp1": np.ascontiguousarray(np.asarray(inputs["L1_attn_bp"], np.float32).reshape(1, 1024)),
    }


def run_D(inputs, c_results):
    key = "D"
    if key not in _prog_cache:
        _prog_cache[key] = build_D()
    nc = _prog_cache[key]
    base = _d_base_inputs(inputs)
    in_maps = [dict(base, xn1T=c_results[c]["xn1T"], x1T=c_results[c]["x1T"])
               for c in range(NCORE)]
    res = run_bass_kernel_spmd(nc, in_maps, core_ids=list(range(NCORE)))
    return res.results


def run_E1(inputs, c_results):
    if "E1" not in _prog_cache:
        _prog_cache["E1"] = build_E1()
    nc = _prog_cache["E1"]
    xmax8 = np.ascontiguousarray(
        np.stack([c_results[c]["xmax"][0, 0] for c in range(NCORE)]).reshape(1, 8),
        np.float32)
    base = {
        "xmax8": xmax8,
        "wgn": np.ascontiguousarray(inputs["L1_ffn_w_gate"], np.float32),
        "wun": np.ascontiguousarray(inputs["L1_ffn_w_up"], np.float32),
    }
    in_maps = [dict(base, xn1F=c_results[c]["xn1F"]) for c in range(NCORE)]
    return run_bass_kernel_spmd(nc, in_maps, core_ids=list(range(NCORE))).results


def run_E2(e1_results):
    if "E2" not in _prog_cache:
        _prog_cache["E2"] = build_E2()
    nc = _prog_cache["E2"]
    mup8 = np.ascontiguousarray(
        np.stack([e1_results[c]["mups"][0, 0] for c in range(NCORE)]).reshape(1, 8), np.float32)
    mgt8 = np.ascontiguousarray(
        np.stack([e1_results[c]["mgts"][0, 0] for c in range(NCORE)]).reshape(1, 8), np.float32)
    in_maps = [dict(mup8=mup8, mgt8=mgt8, upD=e1_results[c]["upD"],
                    gateD=e1_results[c]["gateD"]) for c in range(NCORE)]
    return run_bass_kernel_spmd(nc, in_maps, core_ids=list(range(NCORE))).results


def run_E3(e2_results):
    if "E3" not in _prog_cache:
        _prog_cache["E3"] = build_E3()
    nc = _prog_cache["E3"]
    mg28 = np.ascontiguousarray(
        np.stack([e2_results[c]["mg2s"][0, 0] for c in range(NCORE)]).reshape(1, 8), np.float32)
    in_maps = [dict(mg28=mg28, upqD=e2_results[c]["upqD"],
                    g2D=e2_results[c]["g2D"]) for c in range(NCORE)]
    return run_bass_kernel_spmd(nc, in_maps, core_ids=list(range(NCORE))).results


def run_E4(inputs, e3_results, d_results):
    if "E4" not in _prog_cache:
        _prog_cache["E4"] = build_E4()
    nc = _prog_cache["E4"]
    mo8 = np.ascontiguousarray(
        np.stack([e3_results[c]["mos"][0, 0] for c in range(NCORE)]).reshape(1, 8), np.float32)
    base = {"mo8": mo8,
            "wdn": np.ascontiguousarray(inputs["L1_ffn_w_down"], np.float32)}
    in_maps = [dict(base, oD=e3_results[c]["oD"], xBT=d_results[c]["xBT"])
               for c in range(NCORE)]
    return run_bass_kernel_spmd(nc, in_maps, core_ids=list(range(NCORE))).results


def kernel(**inputs):
    inputs = {k: np.asarray(v) for k, v in inputs.items()}
    ra = run_A(inputs)
    rb = run_B(inputs, ra)
    rc = run_C(inputs, ra, rb)
    rd = run_D(inputs, rc)
    re1 = run_E1(inputs, rc)
    re2 = run_E2(re1)
    re3 = run_E3(re2)
    re4 = run_E4(inputs, re3, rd)
    out = np.empty((B, L, C), np.float32)
    for c in range(NCORE):
        xo = re4[c]["xoutT"]
        out[:, 512 * c:512 * (c + 1), :] = xo.T.reshape(B, 512, C)
    return out
